# version 12
# baseline (speedup 1.0000x reference)
import numpy as np
from itertools import chain as _ichain
import ml_dtypes
from contextlib import ExitStack

import concourse.bass as bass
import concourse.mybir as mybir
from concourse.bass_utils import run_bass_kernel_spmd

F32 = mybir.dt.float32
BF16 = mybir.dt.bfloat16
AT = mybir.ActivationFunctionType
OP = mybir.AluOpType
AX = mybir.AxisListType

B_TOT = 32
N_CORES = 8
BPC = 4
D = 1024
I_FF = 2816
W = 64
LN_EPS = 1e-5
DC = D // 128
IC = I_FF // 128
NBI = I_FF // 16

ENGS = ("v", "s", "t", "g", "y")


def _bf(x):
    return np.asarray(x, dtype=ml_dtypes.bfloat16)


def make_constants():
    sel = np.zeros((32, 128, 128), np.float32)
    for vv in range(32):
        for m in range(128):
            sel[vv, (m // 32) * 32 + vv, m] = 1.0
    b3i = np.zeros((128, 2 * W), np.float32)
    for p in range(128):
        for pl in range(2):
            b3i[p, pl * W + pl * 32 + (p % 32)] = 1.0
    return {"sel": _bf(sel.reshape(32 * 128, 128)),
            "b3init": _bf(b3i)}


class Sched:
    def __init__(self, nc):
        self.nc = nc
        self.ops = {e: [] for e in ENGS}
        self.sems = {}
        self.counts = {}

    def add_sem(self, es, name):
        h = es.enter_context(self.nc.semaphore(name))
        self.sems[name] = h
        self.counts[name] = 0

    def emit(self, eng, fn):
        self.ops[eng].append(fn)

    def tick(self, sem, n=1):
        self.counts[sem] += n
        return self.counts[sem]

    def edge(self, eng, sem, n=1):
        h = self.sems[sem]
        self.counts[sem] += n
        self.ops[eng].append(lambda e, h=h, n=n: e.drain().then_inc(h, n))
        return self.counts[sem]

    def dma(self, eng, sem, fn):
        h = self.sems[sem]
        self.counts[sem] += 16
        self.ops[eng].append(lambda e, h=h: fn(e, h))
        return self.counts[sem]

    def wait(self, eng, sem, count=None):
        h = self.sems[sem]
        c = self.counts[sem] if count is None else count
        if c > 0:
            self.ops[eng].append(lambda e, h=h, c=c: e.wait_ge(h, c))

    def barrier(self):
        names = {"v": "V", "s": "S", "t": "P", "g": "Q", "y": "Y"}
        for eng, sem in names.items():
            self.edge(eng, sem)
        snap = {sem: self.counts[sem] for sem in names.values()}
        for eng in names:
            for sem, c in snap.items():
                self.wait(eng, sem, c)

    def play(self, block):
        ops = self.ops

        @block.vector
        def _(v):
            for fn in ops["v"]:
                fn(v)

        @block.scalar
        def _(s):
            for fn in ops["s"]:
                fn(s)

        @block.tensor
        def _(t):
            for fn in ops["t"]:
                fn(t)

        @block.gpsimd
        def _(g):
            for fn in ops["g"]:
                fn(g)

        @block.sync
        def _(y):
            for fn in ops["y"]:
                fn(y)


def build_program(T):
    nc = bass.Bass("TRN2", target_bir_lowering=False)
    NCH = T // W
    assert T % W == 0 and NCH >= 2

    x_h = nc.dram_tensor("x", [BPC, T, D], F32, kind="ExternalInput").ap()
    wq_h = nc.dram_tensor("wq", [D, D], F32, kind="ExternalInput").ap()
    wqb_h = nc.dram_tensor("wqb", [D], F32, kind="ExternalInput").ap()
    w2_h = nc.dram_tensor("w2", [D], F32, kind="ExternalInput").ap()
    lng_h = nc.dram_tensor("lng", [D], F32, kind="ExternalInput").ap()
    lnb_h = nc.dram_tensor("lnb", [D], F32, kind="ExternalInput").ap()
    wg_h = nc.dram_tensor("wg", [I_FF, D], F32, kind="ExternalInput").ap()
    wu_h = nc.dram_tensor("wu", [I_FF, D], F32, kind="ExternalInput").ap()
    wd_h = nc.dram_tensor("wd", [D, I_FF], F32, kind="ExternalInput").ap()
    cvec_h = nc.dram_tensor("cvec", [BPC, D], F32, kind="ExternalInput").ap()
    wmod_h = nc.dram_tensor("wmod", [D, 6 * D], F32, kind="ExternalInput").ap()
    sel_h = nc.dram_tensor("sel", [32 * 128, 128], BF16, kind="ExternalInput").ap()
    b3i_h = nc.dram_tensor("b3init", [128, 2 * W], BF16, kind="ExternalInput").ap()
    out_h = nc.dram_tensor("out", [BPC, T, D], F32, kind="ExternalOutput").ap()
    wdq_h = nc.dram_tensor("wdq", [IC, 128, D], BF16, kind="Internal").ap()

    es = ExitStack()
    SB = lambda n, s, d=F32: es.enter_context(nc.sbuf_tensor(n, s, d))
    PS = lambda n, s, d=F32: es.enter_context(nc.psum_tensor(n, s, d))

    with es:
        QW = 1408
        wq_sb = SB("wq_sb", [128, DC * D], BF16)
        wgT = SB("wgT", [128, DC * I_FF], BF16)
        wuT = SB("wuT", [128, DC * I_FF], BF16)
        wdTs = SB("wdTs", [128, 4 * D], BF16)
        dn2 = SB("dn2", [128, 12 * 128], BF16)
        tstg = SB("tstg", [128, 2 * 128], BF16)
        wqb_f = SB("wqb_f", [128, DC], F32)
        w2_f = SB("w2_f", [128, DC], F32)
        lng_f = SB("lng_f", [128, DC], F32)
        lnb_f = SB("lnb_f", [128, DC], F32)
        modf = SB("modf", [128, 6 * BPC * DC], F32)
        lngga = SB("lngga", [128, BPC * DC], F32)
        lnbga = SB("lnbga", [128, BPC * DC], F32)
        silc = SB("silc", [128, BPC * DC], BF16)
        sel_sb = SB("sel_sb", [128, 32 * 128], BF16)
        b3i_sb = SB("b3i_sb", [128, 2 * W], BF16)
        xtd = SB("xtd", [128, 2 * 2 * D], F32)
        fT = SB("fT", [128, DC * 256], BF16)
        stat = SB("stat", [128, 16], F32)
        wb3 = SB("wb3", [128, 4 * 128], BF16)
        eh = SB("eh", [128, W + 1], BF16)
        ebc = SB("ebc", [128, W + 1], F32)
        wnrm = SB("wnrm", [128, W], BF16)
        zac = SB("zac", [128, 4], F32)
        col32 = SB("col32", [128, 4], F32)
        vtok = SB("vtok", [128, 2 * 2 * D], BF16)
        vT = SB("vT", [128, DC * 256], BF16)
        yt = SB("yt", [128, 2 * D], BF16)
        h2d = SB("h2d", [128, 2 * 2 * D], BF16)
        h3 = SB("h3", [128, I_FF // 2], BF16)
        hqt = SB("hqt", [128, I_FF], BF16)

        qsc = SB("qsc", [128, QW // 16], F32)
        qrs = SB("qrs", [128, QW // 16], F32)
        qa = SB("qa", [128, QW], BF16)
        gbt = SB("gbt", [128, D], BF16)
        cat = SB("cat", [128, D], BF16)
        sat = SB("sat", [128, D], BF16)
        cbt = SB("cbt", [128, D], BF16)
        sbt = SB("sbt", [128, D], BF16)
        lgt = SB("lgt", [128, D], BF16)
        lbt = SB("lbt", [128, D], BF16)
        junk = SB("junk", [128, 4 * W], BF16)
        sgn = SB("sgn", [128, QW], BF16)
        ones1 = SB("ones1", [128, 4], F32)

        qT = yt[:, 0:2 * D]
        wmst = hqt[:, 0:1536]
        xq = yt[:]
        wnat = vtok[:, 0:I_FF]
        cstg = xtd[:, 0:BPC * DC]
        ms1 = xtd[:, 64:64 + BPC * DC]
        ms2 = xtd[:, 128:128 + BPC * DC]

        pA = PS("pA", [128, 512], F32)
        pB = PS("pB", [128, 512], F32)
        pC = PS("pC", [128, 512], F32)
        pD = PS("pD", [128, 512], F32)
        pbc = [PS("pbc0", [128, W + 1], F32), PS("pbc1", [128, W + 1], F32)]

        sch = Sched(nc)
        for sname in ("L", "G", "G2", "G3", "G4", "V", "S", "P", "Q", "X", "X2", "O", "Y"):
            sch.add_sem(es, sname)

        wb3v = wb3[:].rearrange("p (pl c) -> p pl c", pl=4)
        modv = modf[:].rearrange("p (j b dc) -> p j b dc", j=6, b=BPC)
        fT3 = fT[:].rearrange("p (dc t) -> p dc t", dc=DC)
        qT3 = qT.rearrange("p (dc t) -> p dc t", dc=DC)
        vT3 = vT[:].rearrange("p (dc t) -> p dc t", dc=DC)
        silc3 = silc[:].rearrange("p (b dc) -> p b dc", b=BPC)
        ms1v = ms1.rearrange("p (b dc) -> p b dc", b=BPC)
        ms2v = ms2.rearrange("p (b dc) -> p b dc", b=BPC)

        def ldvec(dst, src):
            sch.dma("y", "L", lambda e, h, dst=dst, src=src: e.dma_start(
                out=dst[:], in_=src.rearrange("(dc p) -> p dc", p=128),
                allow_slow_non_contiguous=True).then_inc(h, 16))

        ldvec(wqb_f, wqb_h)
        ldvec(w2_f, w2_h)
        ldvec(lng_f, lng_h)
        ldvec(lnb_f, lnb_h)
        sch.dma("y", "L", lambda e, h: e.dma_start(
            out=sel_sb[:].rearrange("p (v m) -> p v m", v=32),
            in_=sel_h.rearrange("(v p) m -> p v m", p=128)).then_inc(h, 16))
        sch.dma("y", "L", lambda e, h: e.dma_start(out=b3i_sb[:], in_=b3i_h).then_inc(h, 16))
        sch.dma("y", "L", lambda e, h: e.dma_start(
            out=cstg.rearrange("p (b dc) -> p b dc", b=BPC),
            in_=cvec_h.rearrange("b (dc p) -> p b dc", p=128),
            allow_slow_non_contiguous=True).then_inc(h, 16))
        vec_ld = sch.counts["L"]
        sch.dma("g", "G", lambda e, h: e.dma_start(
            out=wq_sb[:].rearrange("p (dc o) -> p dc o", dc=DC),
            in_=wq_h.rearrange("(dc p) o -> p dc o", p=128)).then_inc(h, 16))
        wq_ld = sch.counts["G"]

        sch.wait("s", "L", vec_ld)
        sch.emit("s", lambda s: s.activation(silc[:], cstg, AT.Silu))
        silc_done = sch.edge("s", "S")

        first_mm = [True]
        for dc in range(DC):
            for qtr in range(4):
                sch.dma("g", "G", lambda e, h, dc=dc, qtr=qtr: e.dma_start(
                    out=wmst[:], in_=wmod_h[dc * 128:(dc + 1) * 128,
                                            qtr * 1536:(qtr + 1) * 1536]).then_inc(h, 16))
                sch.wait("t", "G")
                if dc == 0 and qtr == 0:
                    sch.wait("t", "S", silc_done)
                for mcl in range(12):
                    mc = qtr * 12 + mcl
                    sch.emit("t", lambda t, dc=dc, mc=mc, mcl=mcl, f=first_mm[0]:
                             t.matmul(
                        pA[:, mc * 4:(mc + 1) * 4],
                        wmst[:, mcl * 128:(mcl + 1) * 128],
                        silc3[:, :, dc],
                        start=f, stop=(dc == DC - 1),
                        skip_group_check=True))
                    first_mm[0] = False
                pe_c = sch.edge("t", "P")
                sch.wait("g", "P", pe_c)
        mod_mm = sch.edge("t", "P")

        sch.wait("v", "P", mod_mm)
        pAv = pA[:, 0:192].rearrange("p (j dc b) -> p j b dc", j=6, dc=DC)
        sch.emit("v", lambda v: v.tensor_copy(modv[:, :, :, :], pAv))
        sch.emit("v", lambda v: v.drain())

        def gprime(j):
            def f(v, j=j):
                m = modv[:, j]
                v.tensor_scalar(ms1v, m, 0.0, 2.0, OP.is_ge, OP.mult)
                v.drain()
                v.tensor_scalar(ms2v, ms1v, 1.0, None, OP.subtract)
                v.drain()
                v.tensor_scalar(ms1v, m, -1.0, None, OP.mult)
                v.drain()
                v.tensor_tensor(ms1v, ms1v, m, OP.max)
                v.drain()
                v.tensor_scalar(ms1v, ms1v, 0.25, None, OP.max)
                v.drain()
                v.tensor_tensor(m, ms1v, ms2v, OP.mult)
                v.drain()
            sch.emit("v", f)

        gprime(2)
        gprime(5)
        sch.emit("v", lambda v: v.tensor_scalar(modv[:, 1], modv[:, 1], 1.0, None, OP.add))
        sch.emit("v", lambda v: v.tensor_scalar(modv[:, 4], modv[:, 4], 1.0, None, OP.add))
        sch.emit("v", lambda v: v.drain())

        def f_lngga(v):
            for b in range(BPC):
                v.tensor_tensor(lngga[:, b * DC:(b + 1) * DC], lng_f[:], modv[:, 2, b], OP.mult)
                v.tensor_tensor(lnbga[:, b * DC:(b + 1) * DC], lnb_f[:], modv[:, 2, b], OP.mult)
            v.memset(ones1[:], 1.0)
            v.drain()
        sch.emit("v", f_lngga)
        sch.emit("v", lambda v: v.memset(eh[:, W:W + 1], 1.0))
        sch.emit("v", lambda v: v.drain())
        mod_done = sch.edge("v", "V")

        def emit_quant(src_ap, dst_ap, NC, scrB=None, scrB2=None):
            nb = NC // 16
            src3 = src_ap.rearrange("p (nb k) -> p nb k", k=16)
            sc3 = qsc[:, 0:nb].rearrange("p (nb o) -> p nb o", o=1)

            def v1(v):
                v.tensor_reduce(sc3, src3, AX.X, OP.max, apply_absolute_value=True)
                v.drain()
                v.tensor_scalar(qsc[:, 0:nb], qsc[:, 0:nb], 1.0 / 6.0, 1e-12,
                                OP.mult, OP.max)
                v.drain()
                v.reciprocal(qrs[:, 0:nb], qsc[:, 0:nb])
                v.drain()
            sch.emit("v", v1)
            c_v = sch.edge("v", "V")
            sch.wait("s", "V", c_v)
            sch.emit("s", lambda s: s.activation(sgn[:, 0:NC], src_ap, AT.Sign))
            c_s = sch.edge("s", "S")
            sch.wait("v", "S", c_s)
            yield

            def v2a(v):
                rsb = qrs[:, 0:nb].rearrange("p (nb o) -> p nb o", o=1).broadcast_to([128, nb, 16])
                a13 = qa[:, 0:NC].rearrange("p (nb k) -> p nb k", k=16)
                v.tensor_tensor(a13, src3, rsb, OP.mult)
                v.drain()
                v.tensor_tensor(qa[:, 0:NC], qa[:, 0:NC], sgn[:, 0:NC], OP.mult)
                v.drain()
            sch.emit("v", v2a)
            yield
            a = dst_ap
            bb = [scrB[:, 0:NC], scrB2[:, 0:NC]]
            vv = qa[:, 0:NC]
            steps = [(1.25, 0.5), (1.75, 0.5), (2.5, 1.0), (3.5, 1.0), (5.0, 2.0)]

            def v2b(v):
                v.tensor_scalar(a, vv, 0.25, 0.5, OP.is_gt, OP.mult)
                v.tensor_scalar(bb[0], vv, 0.75, 0.5, OP.is_gt, OP.mult)
                v.drain()
                for i, (thr, wgt) in list(enumerate(steps))[:2]:
                    v.tensor_tensor(a, a, bb[i % 2], OP.add)
                    v.tensor_scalar(bb[(i + 1) % 2], vv, thr, wgt, OP.is_gt, OP.mult)
                    v.drain()
            sch.emit("v", v2b)
            yield

            def v2c(v):
                for i, (thr, wgt) in list(enumerate(steps))[2:]:
                    v.tensor_tensor(a, a, bb[i % 2], OP.add)
                    v.tensor_scalar(bb[(i + 1) % 2], vv, thr, wgt, OP.is_gt, OP.mult)
                    v.drain()
                v.tensor_tensor(a, a, bb[len(steps) % 2], OP.add)
                v.drain()
            sch.emit("v", v2c)
            yield

            def v2d(v):
                v.tensor_tensor(a, a, sgn[:, 0:NC], OP.mult)
                v.drain()
                scb = qsc[:, 0:nb].rearrange("p (nb o) -> p nb o", o=1).broadcast_to([128, nb, 16])
                dst3 = dst_ap.rearrange("p (nb k) -> p nb k", k=16)
                v.tensor_tensor(dst3, dst3, scb, OP.mult)
                v.drain()
            sch.emit("v", v2d)

        sch.wait("g", "V", mod_done)

        def weight_pipe(src_h, nslab, ncols, dst, dst_col, to_dram=False):
            for sl in range(nslab):
                sch.dma("g", "G", lambda e, h, sl=sl: e.dma_start(
                    out=wnat[:, 0:ncols], in_=src_h[sl * 128:(sl + 1) * 128, :]).then_inc(h, 16))
                sch.wait("v", "G")
                sch.wait("s", "G")
                for q0 in range(0, ncols, QW):
                    qw = min(QW, ncols - q0)
                    for _ in emit_quant(wnat[:, q0:q0 + qw],
                                        hqt[:, q0:q0 + qw], qw, h3, yt):
                        pass
                cq = sch.edge("v", "V")
                sch.wait("y", "V", cq)
                sch.wait("s", "V", cq)
                if to_dram:
                    for cb in range(ncols // 128):
                        if cb >= 2:
                            sch.wait("y", "O", sch.counts["O"])
                        sch.dma("y", "X", lambda e, h, sl=sl, cb=cb:
                                e.dma_start_transpose(
                            out=tstg[:, (cb % 2) * 128:(cb % 2 + 1) * 128],
                            in_=hqt[:, cb * 128:(cb + 1) * 128]).then_inc(h, 16))
                        sch.wait("y", "X", sch.counts["X"])
                        sch.dma("y", "O", lambda e, h, sl=sl, cb=cb: e.dma_start(
                            out=wdq_h[cb, :, sl * 128:(sl + 1) * 128],
                            in_=tstg[:, (cb % 2) * 128:(cb % 2 + 1) * 128]
                        ).then_inc(h, 16))
                    cx, co = sch.counts["X"], sch.counts["O"]
                    for e in ("v", "s", "g", "y"):
                        sch.wait(e, "X", cx)
                    sch.wait("y", "O", co)
                else:
                    for cb in range(ncols // 128):
                        eng, sem = ("y", "X") if cb % 2 == 0 else ("s", "X2")
                        sch.dma(eng, sem, lambda e, h, sl=sl, cb=cb:
                                e.dma_start_transpose(
                            out=dst[:, dst_col(cb, sl):dst_col(cb, sl) + 128],
                            in_=hqt[:, cb * 128:(cb + 1) * 128]).then_inc(h, 16))
                    cx, cx2 = sch.counts["X"], sch.counts["X2"]
                    for e in ("v", "s", "g", "y"):
                        sch.wait(e, "X", cx)
                        sch.wait(e, "X2", cx2)

        weight_pipe(wg_h, IC, D, wgT, lambda cb, sl: cb * I_FF + sl * 128)
        weight_pipe(wu_h, IC, D, wuT, lambda cb, sl: cb * I_FF + sl * 128)
        weight_pipe(wd_h, DC, I_FF, None, None, to_dram=True)
        sch.wait("t", "G")
        sch.barrier()

        def emit_rstd(nsrc, ndst):
            vs = stat[:, nsrc:nsrc + 2]
            t_ = stat[:, 4:6]
            r_ = stat[:, 6:8]
            u_ = stat[:, 8:10]
            w_ = stat[:, 10:12]
            y_ = stat[:, ndst:ndst + 2]

            def f1(v):
                v.tensor_scalar(t_, vs, 1.0 / D, LN_EPS, OP.mult, OP.add)
                v.drain()
                v.reciprocal(r_, t_)
                v.drain()
            sch.emit("v", f1)
            c_v = sch.edge("v", "V")
            sch.wait("g", "V", c_v)

            def f2(g):
                g.tensor_scalar(u_, t_, 2.0, None, OP.is_gt)
                g.drain()
                g.tensor_scalar(w_, u_, 0.58, 0.42, OP.mult, OP.add)
                g.tensor_scalar(u_, u_, -0.395, 0.62, OP.mult, OP.add)
                g.drain()
                g.tensor_tensor(u_, u_, t_, OP.mult)
                g.drain()
                g.tensor_tensor(u_, u_, w_, OP.add)
                g.drain()
                g.tensor_tensor(y_, r_, u_, OP.mult)
                g.drain()
                for _ in range(3):
                    g.tensor_tensor(u_, y_, y_, OP.mult)
                    g.drain()
                    g.tensor_tensor(u_, u_, t_, OP.mult)
                    g.drain()
                    g.tensor_scalar(u_, u_, -0.5, 1.5, OP.mult, OP.add)
                    g.drain()
                    g.tensor_tensor(y_, y_, u_, OP.mult)
                    g.drain()
            sch.emit("g", f2)
            c_g = sch.edge("g", "Q")
            sch.wait("v", "Q", c_g)

        def xpre():
            sch.wait("y", "X")
            sch.wait("y", "X2")

        def xw(eng, cp=None):
            cx, cx2 = cp if cp else (sch.counts["X"], sch.counts["X2"])
            sch.wait(eng, "X", cx)
            sch.wait(eng, "X2", cx2)

        def xbar_tiles(src_tok, dstT3):
            for pl in range(2):
                for dc in range(DC):
                    sch.dma("y", "X", lambda e, h, pl=pl, dc=dc: e.dma_start_transpose(
                        out=dstT3[:, dc, pl * 128:(pl + 1) * 128],
                        in_=src_tok[:, pl * D + dc * 128: pl * D + (dc + 1) * 128]
                    ).then_inc(h, 16))
            return (sch.counts["X"], sch.counts["X2"])

        def xbar_back(srcT3, dst_tok):
            for pl in range(2):
                for dc in range(DC):
                    sch.dma("y", "X", lambda e, h, pl=pl, dc=dc: e.dma_start_transpose(
                        out=dst_tok[:, pl * D + dc * 128: pl * D + (dc + 1) * 128],
                        in_=srcT3[:, dc, pl * 128:(pl + 1) * 128]).then_inc(h, 16))
            return (sch.counts["X"], sch.counts["X2"])

        def _mod_bcast(j):
            a = modv[:, j].transpose([0, 2, 1]).unsqueeze(3)
            return a.broadcast_to([128, DC, BPC, 32])

        def feat_affine(T3, j_scale, j_shift):
            def f(v):
                for pl in range(2):
                    T5 = T3.rearrange("p dc (pl b q) -> p dc pl b q",
                                      pl=2, b=BPC)[:, :, pl]
                    v.tensor_tensor(T5, T5, _mod_bcast(j_scale), OP.mult)
                v.drain()
                if j_shift is not None:
                    for pl in range(2):
                        T5 = T3.rearrange("p dc (pl b q) -> p dc pl b q",
                                          pl=2, b=BPC)[:, :, pl]
                        v.tensor_tensor(T5, T5, _mod_bcast(j_shift), OP.add)
                    v.drain()
            sch.emit("v", f)

        def emit_ln(src_ap, dst_bf_ap, musl, varsl, rstdsl, sq_scratch):
            s2 = src_ap.rearrange("p (pl d) -> p pl d", pl=2)

            def f0(v):
                v.tensor_reduce(stat[:, musl:musl + 2].unsqueeze(2), s2, AX.X, OP.add)
                v.drain()
                v.tensor_scalar(stat[:, musl:musl + 2], stat[:, musl:musl + 2],
                                1.0 / D, None, OP.mult)
                v.drain()
            sch.emit("v", f0)
            yield
            for pl in range(2):
                s_ = src_ap[:, pl * D:(pl + 1) * D]
                d_ = dst_bf_ap[:, pl * D:(pl + 1) * D]

                def f(v, s_=s_, d_=d_, pl=pl):
                    v.tensor_scalar(d_, s_, stat[:, musl + pl:musl + pl + 1], None,
                                    OP.subtract)
                    v.drain()
                    v.scalar_tensor_tensor(sq_scratch[:, pl * D:(pl + 1) * D], d_, 1.0,
                                           d_, OP.mult, OP.mult,
                                           accum_out=stat[:, varsl + pl:varsl + pl + 1])
                    v.drain()
                sch.emit("v", f)
                yield
            emit_rstd(varsl, rstdsl)
            yield

            def fn(v):
                for pl in range(2):
                    v.tensor_scalar(dst_bf_ap[:, pl * D:(pl + 1) * D],
                                    dst_bf_ap[:, pl * D:(pl + 1) * D],
                                    stat[:, rstdsl + pl:rstdsl + pl + 1], None, OP.mult)
                v.drain()
            sch.emit("v", fn)

        def feat_affine_ln(T3, vec_a, vec_b):
            def bc(vec):
                a = vec[:].rearrange("p (b dc) -> p b dc", b=BPC).transpose(
                    [0, 2, 1]).unsqueeze(3)
                return a.broadcast_to([128, DC, BPC, 32])

            def f(v):
                for op, vec in ((OP.mult, vec_a), (OP.add, vec_b)):
                    for pl in range(2):
                        T5 = T3.rearrange("p dc (pl b q) -> p dc pl b q",
                                          pl=2, b=BPC)[:, :, pl]
                        v.tensor_tensor(T5, T5, bc(vec), op)
                    v.drain()
            sch.emit("v", f)

        def _vec_bcast(vec):
            a = vec[:].rearrange("p (b dc) -> p b dc", b=BPC).transpose(
                [0, 2, 1]).unsqueeze(3)
            return a.broadcast_to([128, DC, BPC, 32])

        for bsrc, dst in ((_mod_bcast(5), gbt), (_mod_bcast(1), cat),
                          (_mod_bcast(0), sat), (_mod_bcast(4), cbt),
                          (_mod_bcast(3), sbt), (_vec_bcast(lngga), lgt),
                          (_vec_bcast(lnbga), lbt)):
            def f_gfm(v, bsrc=bsrc):
                v.memset(fT[:], 1.0)
                v.drain()
                T5 = fT3.rearrange("p dc (pl b q) -> p dc pl b q",
                                   pl=2, b=BPC)[:, :, 0]
                v.tensor_tensor(T5, T5, bsrc, OP.mult)
                v.drain()
            sch.emit("v", f_gfm)
            c_v = sch.edge("v", "V")
            sch.wait("y", "V", c_v)
            xpre()
            for dc in range(DC):
                sch.dma("y", "X", lambda e, h, dc=dc, dst=dst:
                        e.dma_start_transpose(
                    out=dst[:, dc * 128:(dc + 1) * 128],
                    in_=fT3[:, dc, 0:128]).then_inc(h, 16))
            xw("v")

        def tok_affine(dst, srcv, mtile, atile):
            d3 = dst.rearrange("p (pl f) -> p pl f", pl=2)
            s3 = srcv.rearrange("p (pl f) -> p pl f", pl=2)
            mb = mtile[:].unsqueeze(1).broadcast_to([128, 2, D])
            ab = atile[:].unsqueeze(1).broadcast_to([128, 2, D])

            def f1(v):
                v.tensor_tensor(d3, s3, mb, OP.mult)
                v.drain()
            sch.emit("v", f1)
            yield

            def f2(v):
                v.tensor_tensor(d3, d3, ab, OP.add)
                v.drain()
            sch.emit("v", f2)

        IHALF = [[(0, 512), (512, 1024), (1024, 1408)],
                 [(1408, 1920), (1920, 2432), (2432, 2816)]]
        P0D = {}
        SCI = {}
        VRD = {}
        FFP = {}
        QMP = {}
        DWN = {"last": None}

        def slot(k):
            par = k % 2
            return (xtd[:, par * 2 * D:(par + 1) * 2 * D],
                    h2d[:, par * 2 * D:(par + 1) * 2 * D])

        def pre_gen(k, lnscr=None):
            if lnscr is None:
                lnscr = vT[:]
            xt, h2t = slot(k)
            sch.wait("s", "V")
            sch.wait("s", "O")
            for pl in range(2):
                sch.dma("s", "L", lambda e, h, pl=pl, k=k: e.dma_start(
                    out=xt[:, pl * D:(pl + 1) * D],
                    in_=x_h[:, k * W + pl * 32: k * W + pl * 32 + 32, :]
                ).then_inc(h, 16))
            sch.wait("v", "L")
            yield
            yield from emit_ln(xt, yt[:], 0, 2, 12, lnscr)
            yield
            yield from tok_affine(h2t, yt[:], cat, sat)
            c_hT = sch.edge("v", "V")
            yield
            if k >= 1:
                sch.wait("y", "V", c_hT)
                if (k - 2) in FFP:
                    sch.wait("y", "P", FFP[k - 2])
                xpre()
                cp = xbar_tiles(h2t, fT3)
                xw("t", cp)
                yield
            if k >= 1:
                for oc in range(DC):
                    pq = pA if oc % 2 == 0 else pB
                    for dc in range(DC):
                        sch.emit("t", lambda t, oc=oc, dc=dc, pq=pq: t.matmul(
                            pq[:, 0:256],
                            wq_sb[:, dc * D + oc * 128: dc * D + (oc + 1) * 128],
                            fT3[:, dc, :],
                            start=(dc == 0), stop=(dc == DC - 1),
                            skip_group_check=True))
                    c_p = sch.edge("t", "P")
                    sch.wait("v", "P", c_p)

                    def fq(v, oc=oc, pq=pq):
                        v.tensor_scalar(qT3[:, oc, :], pq[:, 0:256],
                                        wqb_f[:, oc:oc + 1], w2_f[:, oc:oc + 1],
                                        OP.add, OP.mult)
                        v.drain()
                    sch.emit("v", fq)
                    c_vq = sch.edge("v", "V")
                    sch.wait("t", "V", c_vq)
                    if oc % 2 == 1:
                        yield
                QMP[k] = sch.counts["P"]

        def emit_qp0(k):
            xw("t")
            sch.wait("t", "V")
            vT4 = vT3[:, :, :].rearrange("p dc (pl q) -> p dc pl q", pl=2)
            first = True
            for b in range(BPC):
                for plr in range(2):
                    for dc in range(DC):
                        sch.emit("t", lambda t, b=b, plr=plr, dc=dc, first=first:
                                 t.matmul(
                            pC[b * 32:(b + 1) * 32, plr * 64: plr * 64 + 64],
                            qT3[:, dc, plr * 128 + b * 32: plr * 128 + b * 32 + 32],
                            vT4[:, dc, :, b * 32:(b + 1) * 32],
                            start=(dc == 0 and first), stop=(dc == DC - 1),
                            skip_group_check=True, tile_position=(0, b * 32)))
                    first = False
            P0D[k] = p0_done = sch.edge("t", "P")
            sch.wait("v", "P", p0_done)

            def f_p0(v):
                v.tensor_copy(wb3v[:, 0:2, 0:W], pC[:, 0:128].rearrange(
                    "p (pl c) -> p pl c", pl=2))
                v.tensor_copy(wb3v[:, 2:4, 0:W], b3i_sb[:].rearrange(
                    "p (pl c) -> p pl c", pl=2))
                v.drain()
            sch.emit("v", f_p0)
            SCI[k] = sch.edge("v", "V")

        def emit_scan_step(k, c, st):
            s_evac = st["s_evac"]
            v_col = st["v_col"]
            plc = c // 32
            pl0 = 0 if c < 32 else 1
            npl = 4 - pl0
            sch.wait("s", "V", v_col.get(c - 1, SCI[k]))
            cs = sch.tick("S")
            sch.emit("s", lambda s, plc=plc, c=c, cs=cs: s.activation(
                eh[:, 0:W], wb3v[:, plc, c:c + W], AT.Exp).then_inc(
                sch.sems["S"], 1))
            sch.wait("t", "S", cs)
            if c >= 2:
                sch.wait("t", "V", s_evac[c - 2])
            cp = sch.tick("P")
            sch.emit("t", lambda t, c=c: t.matmul(
                pbc[c % 2][:], sel_sb[:, (c % 32) * 128:(c % 32 + 1) * 128],
                eh[:], start=True, stop=True).then_inc(sch.sems["P"], 1))
            sch.wait("v", "P", cp)
            ce1 = sch.tick("V")
            sch.emit("v", lambda v, c=c: v.tensor_scalar(
                ebc[:], pbc[c % 2][:], 1.0, 0.0, OP.mult, OP.add,
                accum_out=zac[:, 0:1]).then_inc(sch.sems["V"], 1))
            s_evac[c] = ce1
            sch.wait("v", "V", ce1)
            ce2 = sch.tick("V")
            sch.emit("v", lambda v: v.reciprocal(
                zac[:, 2:3], zac[:, 0:1]).then_inc(sch.sems["V"], 1))
            if c >= 1:
                sch.wait("v", "V", v_col[c - 1])
            ce3 = sch.tick("V")

            def f_tt(v, c=c, pl0=pl0, npl=npl):
                wn = ebc[:, 0:W].rearrange("p (o k) -> p o k", o=1
                                           ).broadcast_to([128, npl, W])
                jv = junk[:].rearrange("p (pl k) -> p pl k", pl=4)[:, pl0:4]
                v.tensor_tensor(jv, wb3v[:, pl0:4, c:c + W], wn,
                                OP.mult).then_inc(sch.sems["V"], 1)
            sch.emit("v", f_tt)
            sch.wait("v", "V", ce3)
            ce4 = sch.tick("V")

            def f_rd(v, pl0=pl0, npl=npl):
                jv = junk[:].rearrange("p (pl k) -> p pl k", pl=4)[:, pl0:4]
                v.tensor_reduce(col32[:, pl0:4].unsqueeze(2), jv,
                                AX.X, OP.add).then_inc(sch.sems["V"], 1)
            sch.emit("v", f_rd)
            sch.wait("v", "V", ce2)
            sch.wait("v", "V", ce4)
            ce5 = sch.tick("V")

            def f_sc(v, c=c, pl0=pl0):
                v.tensor_scalar(wb3v[:, pl0:4, W + c:W + c + 1],
                                col32[:, pl0:4].unsqueeze(2),
                                zac[:, 2:3], None, OP.mult).then_inc(
                    sch.sems["V"], 1)
            sch.emit("v", f_sc)
            v_col[c] = ce5

        def emit_vtail(k):
            par = k % 2
            vcur = vtok[:, par * 2 * D:(par + 1) * 2 * D]
            vprev = vtok[:, (1 - par) * 2 * D:(2 - par) * 2 * D]
            vprev3 = vprev.rearrange("p (pl d) -> p pl d", pl=2)
            sch.wait("t", "V")
            pmap = {(0, 0): pA, (0, 1): pB, (1, 0): pC, (1, 1): pD}
            for b in range(BPC):
                for dh in range(2):
                    for plo in range(2):
                        pq = pmap[(dh, plo)]
                        for pli in range(2):
                            sch.emit("t", lambda t, b=b, dh=dh, plo=plo, pli=pli,
                                     pq=pq: t.matmul(
                                pq[b * 32:(b + 1) * 32, 0:512],
                                wb3v[b * 32:(b + 1) * 32, 2 + pli,
                                     W + plo * 32: W + plo * 32 + 32],
                                vprev3[b * 32:(b + 1) * 32, pli,
                                       dh * 512:(dh + 1) * 512],
                                start=(pli == 0), stop=(pli == 1),
                                skip_group_check=True,
                                tile_position=(b * 32, b * 32)))
            vmm_done = sch.edge("t", "P")
            sch.wait("v", "P", vmm_done)

            def f_vev(v, vcur=vcur):
                for dh in range(2):
                    for plo in range(2):
                        v.tensor_copy(
                            vcur[:, plo * D + dh * 512: plo * D + (dh + 1) * 512],
                            pmap[(dh, plo)][:])
                v.drain()
            sch.emit("v", f_vev)
            VRD[k] = v_ready = sch.edge("v", "V")
            if k + 1 < NCH:
                sch.wait("y", "V", v_ready)
                sch.wait("s", "V", v_ready)
                xpre()
                xbar_tiles(vcur, vT3)

        def post_gen(k):
            par = k % 2
            xt, h2t = slot(k)
            vcur = vtok[:, par * 2 * D:(par + 1) * 2 * D]
            vprev = vtok[:, (1 - par) * 2 * D:(2 - par) * 2 * D]
            sch.wait("s", "V", VRD[k])
            if (k + 1) in P0D:
                sch.wait("s", "P", P0D[k + 1])
            sch.emit("s", lambda s, vcur=vcur: s.activation(yt[:], vcur, AT.Tanh))
            tanh_done = sch.edge("s", "S")
            sch.wait("v", "S", tanh_done)

            def f_y1(v):
                v.tensor_scalar(yt[:], yt[:], 1.0, None, OP.add)
                v.drain()
            sch.emit("v", f_y1)
            yield

            def f_y2(v):
                v.tensor_tensor(yt[:], yt[:], h2t, OP.mult)
                v.drain()
            sch.emit("v", f_y2)
            yield
            yield from emit_ln(yt[:], h2t, 0, 2, 12, vT[:])
            yield
            yield from tok_affine(yt[:], h2t, lgt, lbt)
            yield

            def f_res(v):
                v.tensor_tensor(xt, xt, yt[:], OP.add)
                v.drain()
            sch.emit("v", f_res)
            yield from emit_ln(xt, h2t, 0, 2, 12, vT[:])
            yield
            yield from tok_affine(h2t, h2t, cbt, sbt)
            yield
            for q0 in range(0, 2 * D, QW):
                qw = min(QW, 2 * D - q0)
                yield from emit_quant(h2t[:, q0:q0 + qw], xq[:, q0:q0 + qw],
                                      qw, h3, hqt)
                yield
            c_v = sch.edge("v", "V")
            sch.wait("y", "V", c_v)
            if (k + 1) in QMP:
                sch.wait("y", "P", QMP[k + 1])
            xpre()
            cp3 = xbar_tiles(xq, fT3)
            xw("t", cp3)
            yield
            for tl in range(2):
                for half in range(2):
                    sch.wait("s", "V")
                    ICH3 = IHALF[half]

                    def gu_mms(j, dcs, tl=tl, ICH3=ICH3):
                        i0, i1 = ICH3[j]
                        iw = i1 - i0
                        pg = pA if j % 2 == 0 else pB
                        pu = pC if j % 2 == 0 else pD
                        for dc in dcs:
                            sch.emit("t", lambda t, tl=tl, i0=i0, iw=iw,
                                     dc=dc, pg=pg: t.matmul(
                                pg[:, 0:iw],
                                fT3[:, dc, tl * 128:(tl + 1) * 128],
                                wgT[:, dc * I_FF + i0: dc * I_FF + i0 + iw],
                                start=(dc == 0), stop=(dc == DC - 1),
                                skip_group_check=True))
                            sch.emit("t", lambda t, tl=tl, i0=i0, iw=iw,
                                     dc=dc, pu=pu: t.matmul(
                                pu[:, 0:iw],
                                fT3[:, dc, tl * 128:(tl + 1) * 128],
                                wuT[:, dc * I_FF + i0: dc * I_FF + i0 + iw],
                                start=(dc == 0), stop=(dc == DC - 1),
                                skip_group_check=True))
                        return sch.edge("t", "P")

                    c_p = {}
                    c_vs = {}
                    gu_mms(0, range(0, 4))
                    yield
                    c_p[0] = gu_mms(0, range(4, DC))
                    yield
                    for j in range(len(ICH3)):
                        if j + 1 < len(ICH3):
                            if j >= 1:
                                sch.wait("t", "V", c_vs[j - 1])
                            gu_mms(j + 1, range(0, 4))
                            yield
                            c_p[j + 1] = gu_mms(j + 1, range(4, DC))
                        i0, i1 = ICH3[j]
                        iw = i1 - i0
                        hoff = half * 1408
                        pg = pA if j % 2 == 0 else pB
                        pu = pC if j % 2 == 0 else pD
                        so = (j % 2) * 512
                        sch.wait("s", "P", c_p[j])
                        sch.emit("s", lambda s, iw=iw, pg=pg, so=so:
                                 s.activation(h2t[:, so:so + iw],
                                              pg[:, 0:iw], AT.Tanh,
                                              scale=0.5))
                        sch.emit("s", lambda s: s.drain())
                        sch.emit("s", lambda s, iw=iw, so=so: s.activation(
                            h2t[:, so:so + iw], h2t[:, so:so + iw], AT.Copy,
                            bias=0.5, scale=0.5))
                        c_s = sch.edge("s", "S")
                        sch.wait("v", "S", c_s)

                        def f_su(v, i0=i0, iw=iw, hoff=hoff, pg=pg, pu=pu,
                                 so=so, h2t=h2t):
                            v.tensor_tensor(h2t[:, D + so:D + so + iw],
                                            pg[:, 0:iw],
                                            h2t[:, so:so + iw], OP.mult)
                            v.drain()
                            v.tensor_tensor(h3[:, i0 - hoff:i0 - hoff + iw],
                                            h2t[:, D + so:D + so + iw],
                                            pu[:, 0:iw], OP.mult)
                            v.drain()
                        sch.emit("v", f_su)
                        c_vs[j] = sch.edge("v", "V")
                        yield
                    sch.wait("t", "V", c_vs[len(ICH3) - 1])
                    c_h3 = sch.edge("v", "V")
                    sch.wait("s", "V", c_h3)
                    sch.wait("v", "S")
                    for q0 in range(0, 1408, QW):
                        yield from emit_quant(h3[:, q0:q0 + QW],
                                              hqt[:, half * 1408 + q0:
                                                  half * 1408 + q0 + QW], QW,
                                              h2t, vprev)
                    hq_done = sch.edge("v", "V")
                    yield
                sch.wait("y", "V", sch.counts["V"])
                pe_c = {}
                g_c = {}
                GS = ["G", "G2", "G3", "G4"]

                def wd_pref(i2, pe_c=pe_c):
                    guard = pe_c.get(i2 - 4) if i2 >= 4 else DWN["last"]
                    if guard:
                        sch.wait("g", "P", guard)
                    gs = GS[i2 % 4]
                    sch.dma("g", gs, lambda e, h, i2=i2: e.dma_start(
                        out=wdTs[:, (i2 % 4) * D:(i2 % 4 + 1) * D],
                        in_=wdq_h[i2]).then_inc(h, 16))
                    return sch.counts[gs]
                for i2 in range(4):
                    g_c[i2] = wd_pref(i2)
                WV = [list(range(6 * w, min(6 * w + 6, IC))) for w in range(4)]

                def dn_tp(icb):
                    sch.dma("y", "X", lambda e, h, icb=icb: e.dma_start_transpose(
                        out=dn2[:, (icb % 12) * 128:(icb % 12 + 1) * 128],
                        in_=hqt[:, icb * 128:(icb + 1) * 128]).then_inc(h, 16))

                def issue_wave(w, guard):
                    if guard is not None:
                        sch.wait("y", "P", guard)
                    for icb in WV[w]:
                        dn_tp(icb)
                issue_wave(0, None)
                issue_wave(1, None)
                for w in range(4):
                    sch.wait("t", "X", sch.counts["X"])
                    for icb in WV[w]:
                        sch.wait("t", GS[icb % 4], g_c[icb])
                        for hh in range(2):
                            pq = pC if hh == 0 else pD
                            sch.emit("t", lambda t, icb=icb, hh=hh, pq=pq:
                                     t.matmul(
                                pq[:, 0:512],
                                dn2[:, (icb % 12) * 128:(icb % 12 + 1) * 128],
                                wdTs[:, (icb % 4) * D + hh * 512:
                                     (icb % 4) * D + (hh + 1) * 512],
                                start=(icb == 0), stop=(icb == IC - 1),
                                skip_group_check=True))
                        pe_c[icb] = sch.edge("t", "P")
                        if icb + 4 < IC:
                            g_c[icb + 4] = wd_pref(icb + 4)
                        if icb % 3 == 2:
                            yield
                    if w + 2 < 4:
                        issue_wave(w + 2, pe_c[WV[w][-1]])
                DWN["last"] = pe_c[IC - 1]
                c_p = pe_c[IC - 1]
                sch.wait("v", "P", c_p)

                def f_gb(v, tl=tl):
                    for hh in range(2):
                        pq = pC if hh == 0 else pD
                        v.tensor_tensor(
                            yt[:, tl * D + hh * 512: tl * D + (hh + 1) * 512],
                            pq[:, 0:512], gbt[:, hh * 512:(hh + 1) * 512],
                            OP.mult)
                    v.drain()
                sch.emit("v", f_gb)
                c_vg = sch.edge("v", "V")
                sch.wait("t", "V", c_vg)
                yield
            FFP[k] = sch.counts["P"]

            def f_out(v):
                v.tensor_tensor(xt, xt, yt[:], OP.add)
                v.drain()
            sch.emit("v", f_out)
            out_done = sch.edge("v", "V")
            sch.wait("y", "V", out_done)
            for pl in range(2):
                sch.dma("y", "O", lambda e, h, pl=pl, k=k: e.dma_start(
                    out=out_h[:, k * W + pl * 32: k * W + pl * 32 + 32, :],
                    in_=xt[:, pl * D:(pl + 1) * D]).then_inc(h, 16))

        for _ in pre_gen(0, fT[:]):
            pass
        xt0, h2t0 = slot(0)
        vcur0 = vtok[:, 0:2 * D]
        sch.emit("v", lambda v: v.tensor_copy(vcur0, h2t0))
        sch.emit("v", lambda v: v.drain())
        VRD[0] = v_ready0 = sch.edge("v", "V")
        sch.wait("y", "V", v_ready0)
        sch.wait("s", "V", v_ready0)
        xpre()
        xbar_tiles(vcur0, vT3)
        for _ in pre_gen(1, fT[:]):
            pass
        sch.emit("t", lambda t: t.matmul(
            pC[:, 0:512], sel_sb[:, 0:128], sel_sb[:, 0:512],
            start=True, stop=True, skip_group_check=True))
        emit_qp0(1)

        def pump(gen, n):
            for _ in range(n):
                try:
                    next(gen)
                except StopIteration:
                    return False
            return True

        for k in range(1, NCH):
            gens = [post_gen(k - 1)]
            if k + 1 < NCH:
                gens.append(pre_gen(k + 1))
            gen = _ichain(*gens)
            st = {"s_evac": {}, "v_col": {}}
            alive = True
            for c in range(W):
                emit_scan_step(k, c, st)
                if alive:
                    alive = pump(gen, 2)
            while alive:
                alive = pump(gen, 8)
            emit_vtail(k)
            if k + 1 < NCH:
                emit_qp0(k + 1)
        for _ in post_gen(NCH - 1):
            pass

        sch.wait("y", "O")
        block = es.enter_context(nc.Block())
        sch.play(block)
    return nc


_T_FULL = 2048


def kernel(L0_attn_feature, L0_attn_wq_w, L0_attn_wq_b, L0_attn_w2_w,
           L0_attn_w2_b, L0_attn_ln_g, L0_attn_ln_b, L0_ffn_w_gate,
           L0_ffn_w_up, L0_ffn_w_down, L0_c, L0_wmod):
    x = np.ascontiguousarray(np.asarray(L0_attn_feature, dtype=np.float32))
    consts = make_constants()
    shared = {
        "wq": np.ascontiguousarray(np.asarray(L0_attn_wq_w, np.float32)),
        "wqb": np.ascontiguousarray(np.asarray(L0_attn_wq_b, np.float32)),
        "w2": np.ascontiguousarray(np.asarray(L0_attn_w2_w, np.float32)),
        "lng": np.ascontiguousarray(np.asarray(L0_attn_ln_g, np.float32)),
        "lnb": np.ascontiguousarray(np.asarray(L0_attn_ln_b, np.float32)),
        "wg": np.ascontiguousarray(np.asarray(L0_ffn_w_gate, np.float32)),
        "wu": np.ascontiguousarray(np.asarray(L0_ffn_w_up, np.float32)),
        "wd": np.ascontiguousarray(np.asarray(L0_ffn_w_down, np.float32)),
        "wmod": np.ascontiguousarray(np.asarray(L0_wmod, np.float32)),
        "sel": consts["sel"],
        "b3init": consts["b3init"],
    }
    cv = np.ascontiguousarray(np.asarray(L0_c, np.float32))
    nc = build_program(_T_FULL)
    in_maps = []
    for core in range(N_CORES):
        m = dict(shared)
        m["x"] = np.ascontiguousarray(x[core * BPC:(core + 1) * BPC])
        m["cvec"] = np.ascontiguousarray(cv[core * BPC:(core + 1) * BPC])
        in_maps.append(m)
    res = run_bass_kernel_spmd(nc, in_maps, core_ids=list(range(N_CORES)))
    out = np.concatenate([np.asarray(r["out"]) for r in res.results], axis=0)
    return out.astype(np.float32)


# version 13
# speedup vs baseline: 1.0040x; 1.0040x over previous
import numpy as np
from itertools import chain as _ichain
import ml_dtypes
from contextlib import ExitStack

import concourse.bass as bass
import concourse.mybir as mybir
from concourse.bass_utils import run_bass_kernel_spmd

F32 = mybir.dt.float32
BF16 = mybir.dt.bfloat16
AT = mybir.ActivationFunctionType
OP = mybir.AluOpType
AX = mybir.AxisListType

B_TOT = 32
N_CORES = 8
BPC = 4
D = 1024
I_FF = 2816
W = 64
LN_EPS = 1e-5
DC = D // 128
IC = I_FF // 128
NBI = I_FF // 16

ENGS = ("v", "s", "t", "g", "y")


def _bf(x):
    return np.asarray(x, dtype=ml_dtypes.bfloat16)


def make_constants():
    sel = np.zeros((32, 128, 128), np.float32)
    for vv in range(32):
        for m in range(128):
            sel[vv, (m // 32) * 32 + vv, m] = 1.0
    b3i = np.zeros((128, 2 * W), np.float32)
    for p in range(128):
        for pl in range(2):
            b3i[p, pl * W + pl * 32 + (p % 32)] = 1.0
    return {"sel": _bf(sel.reshape(32 * 128, 128)),
            "b3init": _bf(b3i)}


class Sched:
    def __init__(self, nc):
        self.nc = nc
        self.ops = {e: [] for e in ENGS}
        self.sems = {}
        self.counts = {}

    def add_sem(self, es, name):
        h = es.enter_context(self.nc.semaphore(name))
        self.sems[name] = h
        self.counts[name] = 0

    def emit(self, eng, fn):
        self.ops[eng].append(fn)

    def tick(self, sem, n=1):
        self.counts[sem] += n
        return self.counts[sem]

    def edge(self, eng, sem, n=1):
        h = self.sems[sem]
        self.counts[sem] += n
        self.ops[eng].append(lambda e, h=h, n=n: e.drain().then_inc(h, n))
        return self.counts[sem]

    def dma(self, eng, sem, fn):
        h = self.sems[sem]
        self.counts[sem] += 16
        self.ops[eng].append(lambda e, h=h: fn(e, h))
        return self.counts[sem]

    def wait(self, eng, sem, count=None):
        h = self.sems[sem]
        c = self.counts[sem] if count is None else count
        if c > 0:
            self.ops[eng].append(lambda e, h=h, c=c: e.wait_ge(h, c))

    def barrier(self):
        names = {"v": "V", "s": "S", "t": "P", "g": "Q", "y": "Y"}
        for eng, sem in names.items():
            self.edge(eng, sem)
        snap = {sem: self.counts[sem] for sem in names.values()}
        for eng in names:
            for sem, c in snap.items():
                self.wait(eng, sem, c)

    def play(self, block):
        ops = self.ops

        @block.vector
        def _(v):
            for fn in ops["v"]:
                fn(v)

        @block.scalar
        def _(s):
            for fn in ops["s"]:
                fn(s)

        @block.tensor
        def _(t):
            for fn in ops["t"]:
                fn(t)

        @block.gpsimd
        def _(g):
            for fn in ops["g"]:
                fn(g)

        @block.sync
        def _(y):
            for fn in ops["y"]:
                fn(y)


def build_program(T):
    nc = bass.Bass("TRN2", target_bir_lowering=False)
    NCH = T // W
    assert T % W == 0 and NCH >= 2

    x_h = nc.dram_tensor("x", [BPC, T, D], F32, kind="ExternalInput").ap()
    wq_h = nc.dram_tensor("wq", [D, D], F32, kind="ExternalInput").ap()
    wqb_h = nc.dram_tensor("wqb", [D], F32, kind="ExternalInput").ap()
    w2_h = nc.dram_tensor("w2", [D], F32, kind="ExternalInput").ap()
    lng_h = nc.dram_tensor("lng", [D], F32, kind="ExternalInput").ap()
    lnb_h = nc.dram_tensor("lnb", [D], F32, kind="ExternalInput").ap()
    wg_h = nc.dram_tensor("wg", [I_FF, D], F32, kind="ExternalInput").ap()
    wu_h = nc.dram_tensor("wu", [I_FF, D], F32, kind="ExternalInput").ap()
    wd_h = nc.dram_tensor("wd", [D, I_FF], F32, kind="ExternalInput").ap()
    cvec_h = nc.dram_tensor("cvec", [BPC, D], F32, kind="ExternalInput").ap()
    wmod_h = nc.dram_tensor("wmod", [D, 6 * D], F32, kind="ExternalInput").ap()
    sel_h = nc.dram_tensor("sel", [32 * 128, 128], BF16, kind="ExternalInput").ap()
    b3i_h = nc.dram_tensor("b3init", [128, 2 * W], BF16, kind="ExternalInput").ap()
    out_h = nc.dram_tensor("out", [BPC, T, D], F32, kind="ExternalOutput").ap()
    wdq_h = nc.dram_tensor("wdq", [IC, 128, D], BF16, kind="Internal").ap()

    es = ExitStack()
    SB = lambda n, s, d=F32: es.enter_context(nc.sbuf_tensor(n, s, d))
    PS = lambda n, s, d=F32: es.enter_context(nc.psum_tensor(n, s, d))

    with es:
        QW = 1408
        wq_sb = SB("wq_sb", [128, DC * D], BF16)
        wgT = SB("wgT", [128, DC * I_FF], BF16)
        wuT = SB("wuT", [128, DC * I_FF], BF16)
        wdTs = SB("wdTs", [128, 4 * D], BF16)
        dn2 = SB("dn2", [128, 12 * 128], BF16)
        tstg = SB("tstg", [128, 2 * 128], BF16)
        wqb_f = SB("wqb_f", [128, DC], F32)
        w2_f = SB("w2_f", [128, DC], F32)
        lng_f = SB("lng_f", [128, DC], F32)
        lnb_f = SB("lnb_f", [128, DC], F32)
        modf = SB("modf", [128, 6 * BPC * DC], F32)
        lngga = SB("lngga", [128, BPC * DC], F32)
        lnbga = SB("lnbga", [128, BPC * DC], F32)
        silc = SB("silc", [128, BPC * DC], BF16)
        sel_sb = SB("sel_sb", [128, 32 * 128], BF16)
        b3i_sb = SB("b3i_sb", [128, 2 * W], BF16)
        xtd = SB("xtd", [128, 2 * 2 * D], F32)
        fT = SB("fT", [128, DC * 256], BF16)
        stat = SB("stat", [128, 16], F32)
        wb3 = SB("wb3", [128, 4 * 128], BF16)
        eh = SB("eh", [128, W + 1], BF16)
        ebc = SB("ebc", [128, W + 1], F32)
        wnrm = SB("wnrm", [128, W], BF16)
        zac = SB("zac", [128, 4], F32)
        col32 = SB("col32", [128, 4], F32)
        vtok = SB("vtok", [128, 2 * 2 * D], BF16)
        vT = SB("vT", [128, DC * 256], BF16)
        yt = SB("yt", [128, 2 * D], BF16)
        h2d = SB("h2d", [128, 2 * 2 * D], BF16)
        h3 = SB("h3", [128, I_FF // 2], BF16)
        hqt = SB("hqt", [128, I_FF], BF16)

        qsc = SB("qsc", [128, QW // 16], F32)
        qrs = SB("qrs", [128, QW // 16], F32)
        qa = SB("qa", [128, QW], BF16)
        gbt = SB("gbt", [128, D], BF16)
        cat = SB("cat", [128, D], BF16)
        sat = SB("sat", [128, D], BF16)
        cbt = SB("cbt", [128, D], BF16)
        sbt = SB("sbt", [128, D], BF16)
        lgt = SB("lgt", [128, D], BF16)
        lbt = SB("lbt", [128, D], BF16)
        junk = SB("junk", [128, 4 * W], BF16)
        sgn = SB("sgn", [128, QW], BF16)
        ones1 = SB("ones1", [128, 4], F32)

        qT = yt[:, 0:2 * D]
        wmst = hqt[:, 0:1536]
        xq = yt[:]
        wnat = vtok[:, 0:I_FF]
        cstg = xtd[:, 0:BPC * DC]
        ms1 = xtd[:, 64:64 + BPC * DC]
        ms2 = xtd[:, 128:128 + BPC * DC]

        pA = PS("pA", [128, 512], F32)
        pB = PS("pB", [128, 512], F32)
        pC = PS("pC", [128, 512], F32)
        pD = PS("pD", [128, 512], F32)
        pbc = [PS("pbc0", [128, W + 1], F32), PS("pbc1", [128, W + 1], F32)]

        sch = Sched(nc)
        for sname in ("L", "G", "G2", "G3", "G4", "V", "S", "P", "Q", "X", "X2", "O", "Y"):
            sch.add_sem(es, sname)

        wb3v = wb3[:].rearrange("p (pl c) -> p pl c", pl=4)
        modv = modf[:].rearrange("p (j b dc) -> p j b dc", j=6, b=BPC)
        fT3 = fT[:].rearrange("p (dc t) -> p dc t", dc=DC)
        qT3 = qT.rearrange("p (dc t) -> p dc t", dc=DC)
        vT3 = vT[:].rearrange("p (dc t) -> p dc t", dc=DC)
        silc3 = silc[:].rearrange("p (b dc) -> p b dc", b=BPC)
        ms1v = ms1.rearrange("p (b dc) -> p b dc", b=BPC)
        ms2v = ms2.rearrange("p (b dc) -> p b dc", b=BPC)

        def ldvec(dst, src):
            sch.dma("y", "L", lambda e, h, dst=dst, src=src: e.dma_start(
                out=dst[:], in_=src.rearrange("(dc p) -> p dc", p=128),
                allow_slow_non_contiguous=True).then_inc(h, 16))

        ldvec(wqb_f, wqb_h)
        ldvec(w2_f, w2_h)
        ldvec(lng_f, lng_h)
        ldvec(lnb_f, lnb_h)
        sch.dma("y", "L", lambda e, h: e.dma_start(
            out=sel_sb[:].rearrange("p (v m) -> p v m", v=32),
            in_=sel_h.rearrange("(v p) m -> p v m", p=128)).then_inc(h, 16))
        sch.dma("y", "L", lambda e, h: e.dma_start(out=b3i_sb[:], in_=b3i_h).then_inc(h, 16))
        sch.dma("y", "L", lambda e, h: e.dma_start(
            out=cstg.rearrange("p (b dc) -> p b dc", b=BPC),
            in_=cvec_h.rearrange("b (dc p) -> p b dc", p=128),
            allow_slow_non_contiguous=True).then_inc(h, 16))
        vec_ld = sch.counts["L"]
        sch.dma("g", "G", lambda e, h: e.dma_start(
            out=wq_sb[:].rearrange("p (dc o) -> p dc o", dc=DC),
            in_=wq_h.rearrange("(dc p) o -> p dc o", p=128)).then_inc(h, 16))
        wq_ld = sch.counts["G"]

        sch.wait("s", "L", vec_ld)
        sch.emit("s", lambda s: s.activation(silc[:], cstg, AT.Silu))
        silc_done = sch.edge("s", "S")

        first_mm = [True]
        for dc in range(DC):
            for qtr in range(4):
                sch.dma("g", "G", lambda e, h, dc=dc, qtr=qtr: e.dma_start(
                    out=wmst[:], in_=wmod_h[dc * 128:(dc + 1) * 128,
                                            qtr * 1536:(qtr + 1) * 1536]).then_inc(h, 16))
                sch.wait("t", "G")
                if dc == 0 and qtr == 0:
                    sch.wait("t", "S", silc_done)
                for mcl in range(12):
                    mc = qtr * 12 + mcl
                    sch.emit("t", lambda t, dc=dc, mc=mc, mcl=mcl, f=first_mm[0]:
                             t.matmul(
                        pA[:, mc * 4:(mc + 1) * 4],
                        wmst[:, mcl * 128:(mcl + 1) * 128],
                        silc3[:, :, dc],
                        start=f, stop=(dc == DC - 1),
                        skip_group_check=True))
                    first_mm[0] = False
                pe_c = sch.edge("t", "P")
                sch.wait("g", "P", pe_c)
        mod_mm = sch.edge("t", "P")

        sch.wait("v", "P", mod_mm)
        pAv = pA[:, 0:192].rearrange("p (j dc b) -> p j b dc", j=6, dc=DC)
        sch.emit("v", lambda v: v.tensor_copy(modv[:, :, :, :], pAv))
        sch.emit("v", lambda v: v.drain())

        def gprime(j):
            def f(v, j=j):
                m = modv[:, j]
                v.tensor_scalar(ms1v, m, 0.0, 2.0, OP.is_ge, OP.mult)
                v.drain()
                v.tensor_scalar(ms2v, ms1v, 1.0, None, OP.subtract)
                v.drain()
                v.tensor_scalar(ms1v, m, -1.0, None, OP.mult)
                v.drain()
                v.tensor_tensor(ms1v, ms1v, m, OP.max)
                v.drain()
                v.tensor_scalar(ms1v, ms1v, 0.25, None, OP.max)
                v.drain()
                v.tensor_tensor(m, ms1v, ms2v, OP.mult)
                v.drain()
            sch.emit("v", f)

        gprime(2)
        gprime(5)
        sch.emit("v", lambda v: v.tensor_scalar(modv[:, 1], modv[:, 1], 1.0, None, OP.add))
        sch.emit("v", lambda v: v.tensor_scalar(modv[:, 4], modv[:, 4], 1.0, None, OP.add))
        sch.emit("v", lambda v: v.drain())

        def f_lngga(v):
            for b in range(BPC):
                v.tensor_tensor(lngga[:, b * DC:(b + 1) * DC], lng_f[:], modv[:, 2, b], OP.mult)
                v.tensor_tensor(lnbga[:, b * DC:(b + 1) * DC], lnb_f[:], modv[:, 2, b], OP.mult)
            v.memset(ones1[:], 1.0)
            v.drain()
        sch.emit("v", f_lngga)
        sch.emit("v", lambda v: v.memset(eh[:, W:W + 1], 1.0))
        sch.emit("v", lambda v: v.drain())
        mod_done = sch.edge("v", "V")

        def emit_quant(src_ap, dst_ap, NC, scrB=None, scrB2=None):
            nb = NC // 16
            src3 = src_ap.rearrange("p (nb k) -> p nb k", k=16)
            sc3 = qsc[:, 0:nb].rearrange("p (nb o) -> p nb o", o=1)

            def v1(v):
                v.tensor_reduce(sc3, src3, AX.X, OP.max, apply_absolute_value=True)
                v.drain()
                v.tensor_scalar(qsc[:, 0:nb], qsc[:, 0:nb], 1.0 / 6.0, 1e-12,
                                OP.mult, OP.max)
                v.drain()
                v.reciprocal(qrs[:, 0:nb], qsc[:, 0:nb])
                v.drain()
            sch.emit("v", v1)

            def vsg(v):
                v.tensor_scalar(sgn[:, 0:NC], src_ap, 0.0, 2.0, OP.is_ge, OP.mult)
                v.drain()
                v.tensor_scalar(sgn[:, 0:NC], sgn[:, 0:NC], 1.0, None, OP.subtract)
                v.drain()
            sch.emit("v", vsg)
            yield

            def v2a(v):
                rsb = qrs[:, 0:nb].rearrange("p (nb o) -> p nb o", o=1).broadcast_to([128, nb, 16])
                a13 = qa[:, 0:NC].rearrange("p (nb k) -> p nb k", k=16)
                v.tensor_tensor(a13, src3, rsb, OP.mult)
                v.drain()
                v.tensor_tensor(qa[:, 0:NC], qa[:, 0:NC], sgn[:, 0:NC], OP.mult)
                v.drain()
            sch.emit("v", v2a)
            yield
            a = dst_ap
            bb = [scrB[:, 0:NC], scrB2[:, 0:NC]]
            vv = qa[:, 0:NC]
            steps = [(1.25, 0.5), (1.75, 0.5), (2.5, 1.0), (3.5, 1.0), (5.0, 2.0)]

            def v2b(v):
                v.tensor_scalar(a, vv, 0.25, 0.5, OP.is_gt, OP.mult)
                v.tensor_scalar(bb[0], vv, 0.75, 0.5, OP.is_gt, OP.mult)
                v.drain()
                for i, (thr, wgt) in list(enumerate(steps))[:2]:
                    v.tensor_tensor(a, a, bb[i % 2], OP.add)
                    v.tensor_scalar(bb[(i + 1) % 2], vv, thr, wgt, OP.is_gt, OP.mult)
                    v.drain()
            sch.emit("v", v2b)
            yield

            def v2c(v):
                for i, (thr, wgt) in list(enumerate(steps))[2:]:
                    v.tensor_tensor(a, a, bb[i % 2], OP.add)
                    v.tensor_scalar(bb[(i + 1) % 2], vv, thr, wgt, OP.is_gt, OP.mult)
                    v.drain()
                v.tensor_tensor(a, a, bb[len(steps) % 2], OP.add)
                v.drain()
            sch.emit("v", v2c)
            yield

            def v2d(v):
                v.tensor_tensor(a, a, sgn[:, 0:NC], OP.mult)
                v.drain()
                scb = qsc[:, 0:nb].rearrange("p (nb o) -> p nb o", o=1).broadcast_to([128, nb, 16])
                dst3 = dst_ap.rearrange("p (nb k) -> p nb k", k=16)
                v.tensor_tensor(dst3, dst3, scb, OP.mult)
                v.drain()
            sch.emit("v", v2d)

        sch.wait("g", "V", mod_done)

        def weight_pipe(src_h, nslab, ncols, dst, dst_col, to_dram=False):
            for sl in range(nslab):
                sch.dma("g", "G", lambda e, h, sl=sl: e.dma_start(
                    out=wnat[:, 0:ncols], in_=src_h[sl * 128:(sl + 1) * 128, :]).then_inc(h, 16))
                sch.wait("v", "G")
                sch.wait("s", "G")
                for q0 in range(0, ncols, QW):
                    qw = min(QW, ncols - q0)
                    for _ in emit_quant(wnat[:, q0:q0 + qw],
                                        hqt[:, q0:q0 + qw], qw, h3, yt):
                        pass
                cq = sch.edge("v", "V")
                sch.wait("y", "V", cq)
                sch.wait("s", "V", cq)
                if to_dram:
                    for cb in range(ncols // 128):
                        if cb >= 2:
                            sch.wait("y", "O", sch.counts["O"])
                        sch.dma("y", "X", lambda e, h, sl=sl, cb=cb:
                                e.dma_start_transpose(
                            out=tstg[:, (cb % 2) * 128:(cb % 2 + 1) * 128],
                            in_=hqt[:, cb * 128:(cb + 1) * 128]).then_inc(h, 16))
                        sch.wait("y", "X", sch.counts["X"])
                        sch.dma("y", "O", lambda e, h, sl=sl, cb=cb: e.dma_start(
                            out=wdq_h[cb, :, sl * 128:(sl + 1) * 128],
                            in_=tstg[:, (cb % 2) * 128:(cb % 2 + 1) * 128]
                        ).then_inc(h, 16))
                    cx, co = sch.counts["X"], sch.counts["O"]
                    for e in ("v", "s", "g", "y"):
                        sch.wait(e, "X", cx)
                    sch.wait("y", "O", co)
                else:
                    for cb in range(ncols // 128):
                        eng, sem = ("y", "X") if cb % 2 == 0 else ("s", "X2")
                        sch.dma(eng, sem, lambda e, h, sl=sl, cb=cb:
                                e.dma_start_transpose(
                            out=dst[:, dst_col(cb, sl):dst_col(cb, sl) + 128],
                            in_=hqt[:, cb * 128:(cb + 1) * 128]).then_inc(h, 16))
                    cx, cx2 = sch.counts["X"], sch.counts["X2"]
                    for e in ("v", "s", "g", "y"):
                        sch.wait(e, "X", cx)
                        sch.wait(e, "X2", cx2)

        weight_pipe(wg_h, IC, D, wgT, lambda cb, sl: cb * I_FF + sl * 128)
        weight_pipe(wu_h, IC, D, wuT, lambda cb, sl: cb * I_FF + sl * 128)
        weight_pipe(wd_h, DC, I_FF, None, None, to_dram=True)
        sch.wait("t", "G")
        sch.barrier()

        def emit_rstd(nsrc, ndst):
            vs = stat[:, nsrc:nsrc + 2]
            t_ = stat[:, 4:6]
            r_ = stat[:, 6:8]
            u_ = stat[:, 8:10]
            w_ = stat[:, 10:12]
            y_ = stat[:, ndst:ndst + 2]

            def f1(v):
                v.tensor_scalar(t_, vs, 1.0 / D, LN_EPS, OP.mult, OP.add)
                v.drain()
                v.reciprocal(r_, t_)
                v.drain()
            sch.emit("v", f1)
            c_v = sch.edge("v", "V")
            sch.wait("g", "V", c_v)

            def f2(g):
                g.tensor_scalar(u_, t_, 2.0, None, OP.is_gt)
                g.drain()
                g.tensor_scalar(w_, u_, 0.58, 0.42, OP.mult, OP.add)
                g.tensor_scalar(u_, u_, -0.395, 0.62, OP.mult, OP.add)
                g.drain()
                g.tensor_tensor(u_, u_, t_, OP.mult)
                g.drain()
                g.tensor_tensor(u_, u_, w_, OP.add)
                g.drain()
                g.tensor_tensor(y_, r_, u_, OP.mult)
                g.drain()
                for _ in range(3):
                    g.tensor_tensor(u_, y_, y_, OP.mult)
                    g.drain()
                    g.tensor_tensor(u_, u_, t_, OP.mult)
                    g.drain()
                    g.tensor_scalar(u_, u_, -0.5, 1.5, OP.mult, OP.add)
                    g.drain()
                    g.tensor_tensor(y_, y_, u_, OP.mult)
                    g.drain()
            sch.emit("g", f2)
            c_g = sch.edge("g", "Q")
            sch.wait("v", "Q", c_g)

        def xpre():
            sch.wait("y", "X")
            sch.wait("y", "X2")

        def xw(eng, cp=None):
            cx, cx2 = cp if cp else (sch.counts["X"], sch.counts["X2"])
            sch.wait(eng, "X", cx)
            sch.wait(eng, "X2", cx2)

        def xbar_tiles(src_tok, dstT3):
            for pl in range(2):
                for dc in range(DC):
                    sch.dma("y", "X", lambda e, h, pl=pl, dc=dc: e.dma_start_transpose(
                        out=dstT3[:, dc, pl * 128:(pl + 1) * 128],
                        in_=src_tok[:, pl * D + dc * 128: pl * D + (dc + 1) * 128]
                    ).then_inc(h, 16))
            return (sch.counts["X"], sch.counts["X2"])

        def xbar_back(srcT3, dst_tok):
            for pl in range(2):
                for dc in range(DC):
                    sch.dma("y", "X", lambda e, h, pl=pl, dc=dc: e.dma_start_transpose(
                        out=dst_tok[:, pl * D + dc * 128: pl * D + (dc + 1) * 128],
                        in_=srcT3[:, dc, pl * 128:(pl + 1) * 128]).then_inc(h, 16))
            return (sch.counts["X"], sch.counts["X2"])

        def _mod_bcast(j):
            a = modv[:, j].transpose([0, 2, 1]).unsqueeze(3)
            return a.broadcast_to([128, DC, BPC, 32])

        def feat_affine(T3, j_scale, j_shift):
            def f(v):
                for pl in range(2):
                    T5 = T3.rearrange("p dc (pl b q) -> p dc pl b q",
                                      pl=2, b=BPC)[:, :, pl]
                    v.tensor_tensor(T5, T5, _mod_bcast(j_scale), OP.mult)
                v.drain()
                if j_shift is not None:
                    for pl in range(2):
                        T5 = T3.rearrange("p dc (pl b q) -> p dc pl b q",
                                          pl=2, b=BPC)[:, :, pl]
                        v.tensor_tensor(T5, T5, _mod_bcast(j_shift), OP.add)
                    v.drain()
            sch.emit("v", f)

        def emit_ln(src_ap, dst_bf_ap, musl, varsl, rstdsl, sq_scratch):
            s2 = src_ap.rearrange("p (pl d) -> p pl d", pl=2)

            def f0(v):
                v.tensor_reduce(stat[:, musl:musl + 2].unsqueeze(2), s2, AX.X, OP.add)
                v.drain()
                v.tensor_scalar(stat[:, musl:musl + 2], stat[:, musl:musl + 2],
                                1.0 / D, None, OP.mult)
                v.drain()
            sch.emit("v", f0)
            yield
            for pl in range(2):
                s_ = src_ap[:, pl * D:(pl + 1) * D]
                d_ = dst_bf_ap[:, pl * D:(pl + 1) * D]

                def f(v, s_=s_, d_=d_, pl=pl):
                    v.tensor_scalar(d_, s_, stat[:, musl + pl:musl + pl + 1], None,
                                    OP.subtract)
                    v.drain()
                    v.scalar_tensor_tensor(sq_scratch[:, pl * D:(pl + 1) * D], d_, 1.0,
                                           d_, OP.mult, OP.mult,
                                           accum_out=stat[:, varsl + pl:varsl + pl + 1])
                    v.drain()
                sch.emit("v", f)
                yield
            emit_rstd(varsl, rstdsl)
            yield

            def fn(v):
                for pl in range(2):
                    v.tensor_scalar(dst_bf_ap[:, pl * D:(pl + 1) * D],
                                    dst_bf_ap[:, pl * D:(pl + 1) * D],
                                    stat[:, rstdsl + pl:rstdsl + pl + 1], None, OP.mult)
                v.drain()
            sch.emit("v", fn)

        def feat_affine_ln(T3, vec_a, vec_b):
            def bc(vec):
                a = vec[:].rearrange("p (b dc) -> p b dc", b=BPC).transpose(
                    [0, 2, 1]).unsqueeze(3)
                return a.broadcast_to([128, DC, BPC, 32])

            def f(v):
                for op, vec in ((OP.mult, vec_a), (OP.add, vec_b)):
                    for pl in range(2):
                        T5 = T3.rearrange("p dc (pl b q) -> p dc pl b q",
                                          pl=2, b=BPC)[:, :, pl]
                        v.tensor_tensor(T5, T5, bc(vec), op)
                    v.drain()
            sch.emit("v", f)

        def _vec_bcast(vec):
            a = vec[:].rearrange("p (b dc) -> p b dc", b=BPC).transpose(
                [0, 2, 1]).unsqueeze(3)
            return a.broadcast_to([128, DC, BPC, 32])

        for bsrc, dst in ((_mod_bcast(5), gbt), (_mod_bcast(1), cat),
                          (_mod_bcast(0), sat), (_mod_bcast(4), cbt),
                          (_mod_bcast(3), sbt), (_vec_bcast(lngga), lgt),
                          (_vec_bcast(lnbga), lbt)):
            def f_gfm(v, bsrc=bsrc):
                v.memset(fT[:], 1.0)
                v.drain()
                T5 = fT3.rearrange("p dc (pl b q) -> p dc pl b q",
                                   pl=2, b=BPC)[:, :, 0]
                v.tensor_tensor(T5, T5, bsrc, OP.mult)
                v.drain()
            sch.emit("v", f_gfm)
            c_v = sch.edge("v", "V")
            sch.wait("y", "V", c_v)
            xpre()
            for dc in range(DC):
                sch.dma("y", "X", lambda e, h, dc=dc, dst=dst:
                        e.dma_start_transpose(
                    out=dst[:, dc * 128:(dc + 1) * 128],
                    in_=fT3[:, dc, 0:128]).then_inc(h, 16))
            xw("v")

        def tok_affine(dst, srcv, mtile, atile):
            d3 = dst.rearrange("p (pl f) -> p pl f", pl=2)
            s3 = srcv.rearrange("p (pl f) -> p pl f", pl=2)
            mb = mtile[:].unsqueeze(1).broadcast_to([128, 2, D])
            ab = atile[:].unsqueeze(1).broadcast_to([128, 2, D])

            def f1(v):
                v.tensor_tensor(d3, s3, mb, OP.mult)
                v.drain()
            sch.emit("v", f1)
            yield

            def f2(v):
                v.tensor_tensor(d3, d3, ab, OP.add)
                v.drain()
            sch.emit("v", f2)

        IHALF = [[(0, 512), (512, 1024), (1024, 1408)],
                 [(1408, 1920), (1920, 2432), (2432, 2816)]]
        P0D = {}
        SCI = {}
        VRD = {}
        FFP = {}
        QMP = {}
        DWN = {"last": None}

        def slot(k):
            par = k % 2
            return (xtd[:, par * 2 * D:(par + 1) * 2 * D],
                    h2d[:, par * 2 * D:(par + 1) * 2 * D])

        def pre_gen(k, lnscr=None):
            if lnscr is None:
                lnscr = vT[:]
            xt, h2t = slot(k)
            sch.wait("s", "V")
            sch.wait("s", "O")
            for pl in range(2):
                sch.dma("s", "L", lambda e, h, pl=pl, k=k: e.dma_start(
                    out=xt[:, pl * D:(pl + 1) * D],
                    in_=x_h[:, k * W + pl * 32: k * W + pl * 32 + 32, :]
                ).then_inc(h, 16))
            sch.wait("v", "L")
            yield
            yield from emit_ln(xt, yt[:], 0, 2, 12, lnscr)
            yield
            yield from tok_affine(h2t, yt[:], cat, sat)
            c_hT = sch.edge("v", "V")
            yield
            if k >= 1:
                sch.wait("y", "V", c_hT)
                if (k - 2) in FFP:
                    sch.wait("y", "P", FFP[k - 2])
                xpre()
                cp = xbar_tiles(h2t, fT3)
                xw("t", cp)
                yield
            if k >= 1:
                for oc in range(DC):
                    pq = pA if oc % 2 == 0 else pB
                    for dc in range(DC):
                        sch.emit("t", lambda t, oc=oc, dc=dc, pq=pq: t.matmul(
                            pq[:, 0:256],
                            wq_sb[:, dc * D + oc * 128: dc * D + (oc + 1) * 128],
                            fT3[:, dc, :],
                            start=(dc == 0), stop=(dc == DC - 1),
                            skip_group_check=True))
                    c_p = sch.edge("t", "P")
                    sch.wait("v", "P", c_p)

                    def fq(v, oc=oc, pq=pq):
                        v.tensor_scalar(qT3[:, oc, :], pq[:, 0:256],
                                        wqb_f[:, oc:oc + 1], w2_f[:, oc:oc + 1],
                                        OP.add, OP.mult)
                        v.drain()
                    sch.emit("v", fq)
                    c_vq = sch.edge("v", "V")
                    sch.wait("t", "V", c_vq)
                    if oc % 2 == 1:
                        yield
                QMP[k] = sch.counts["P"]

        def emit_qp0(k):
            xw("t")
            sch.wait("t", "V")
            vT4 = vT3[:, :, :].rearrange("p dc (pl q) -> p dc pl q", pl=2)
            first = True
            for b in range(BPC):
                for plr in range(2):
                    for dc in range(DC):
                        sch.emit("t", lambda t, b=b, plr=plr, dc=dc, first=first:
                                 t.matmul(
                            pC[b * 32:(b + 1) * 32, plr * 64: plr * 64 + 64],
                            qT3[:, dc, plr * 128 + b * 32: plr * 128 + b * 32 + 32],
                            vT4[:, dc, :, b * 32:(b + 1) * 32],
                            start=(dc == 0 and first), stop=(dc == DC - 1),
                            skip_group_check=True, tile_position=(0, b * 32)))
                    first = False
            P0D[k] = p0_done = sch.edge("t", "P")
            sch.wait("v", "P", p0_done)

            def f_p0(v):
                v.tensor_copy(wb3v[:, 0:2, 0:W], pC[:, 0:128].rearrange(
                    "p (pl c) -> p pl c", pl=2))
                v.tensor_copy(wb3v[:, 2:4, 0:W], b3i_sb[:].rearrange(
                    "p (pl c) -> p pl c", pl=2))
                v.drain()
            sch.emit("v", f_p0)
            SCI[k] = sch.edge("v", "V")

        def emit_scan_step(k, c, st):
            s_evac = st["s_evac"]
            v_col = st["v_col"]
            plc = c // 32
            pl0 = 0 if c < 32 else 1
            npl = 4 - pl0
            sch.wait("s", "V", v_col.get(c - 1, SCI[k]))
            cs = sch.tick("S")
            sch.emit("s", lambda s, plc=plc, c=c, cs=cs: s.activation(
                eh[:, 0:W], wb3v[:, plc, c:c + W], AT.Exp).then_inc(
                sch.sems["S"], 1))
            sch.wait("t", "S", cs)
            if c >= 2:
                sch.wait("t", "V", s_evac[c - 2])
            cp = sch.tick("P")
            sch.emit("t", lambda t, c=c: t.matmul(
                pbc[c % 2][:], sel_sb[:, (c % 32) * 128:(c % 32 + 1) * 128],
                eh[:], start=True, stop=True).then_inc(sch.sems["P"], 1))
            sch.wait("v", "P", cp)
            ce1 = sch.tick("V")
            sch.emit("v", lambda v, c=c: v.tensor_scalar(
                ebc[:], pbc[c % 2][:], 1.0, 0.0, OP.mult, OP.add,
                accum_out=zac[:, 0:1]).then_inc(sch.sems["V"], 1))
            s_evac[c] = ce1
            sch.wait("v", "V", ce1)
            ce2 = sch.tick("V")
            sch.emit("v", lambda v: v.reciprocal(
                zac[:, 2:3], zac[:, 0:1]).then_inc(sch.sems["V"], 1))
            if c >= 1:
                sch.wait("v", "V", v_col[c - 1])
            ce3 = sch.tick("V")

            def f_tt(v, c=c, pl0=pl0, npl=npl):
                wn = ebc[:, 0:W].rearrange("p (o k) -> p o k", o=1
                                           ).broadcast_to([128, npl, W])
                jv = junk[:].rearrange("p (pl k) -> p pl k", pl=4)[:, pl0:4]
                v.tensor_tensor(jv, wb3v[:, pl0:4, c:c + W], wn,
                                OP.mult).then_inc(sch.sems["V"], 1)
            sch.emit("v", f_tt)
            sch.wait("v", "V", ce3)
            ce4 = sch.tick("V")

            def f_rd(v, pl0=pl0, npl=npl):
                jv = junk[:].rearrange("p (pl k) -> p pl k", pl=4)[:, pl0:4]
                v.tensor_reduce(col32[:, pl0:4].unsqueeze(2), jv,
                                AX.X, OP.add).then_inc(sch.sems["V"], 1)
            sch.emit("v", f_rd)
            sch.wait("v", "V", ce2)
            sch.wait("v", "V", ce4)
            ce5 = sch.tick("V")

            def f_sc(v, c=c, pl0=pl0):
                v.tensor_scalar(wb3v[:, pl0:4, W + c:W + c + 1],
                                col32[:, pl0:4].unsqueeze(2),
                                zac[:, 2:3], None, OP.mult).then_inc(
                    sch.sems["V"], 1)
            sch.emit("v", f_sc)
            v_col[c] = ce5

        def emit_vtail(k):
            par = k % 2
            vcur = vtok[:, par * 2 * D:(par + 1) * 2 * D]
            vprev = vtok[:, (1 - par) * 2 * D:(2 - par) * 2 * D]
            vprev3 = vprev.rearrange("p (pl d) -> p pl d", pl=2)
            sch.wait("t", "V")
            pmap = {(0, 0): pA, (0, 1): pB, (1, 0): pC, (1, 1): pD}
            for b in range(BPC):
                for dh in range(2):
                    for plo in range(2):
                        pq = pmap[(dh, plo)]
                        for pli in range(2):
                            sch.emit("t", lambda t, b=b, dh=dh, plo=plo, pli=pli,
                                     pq=pq: t.matmul(
                                pq[b * 32:(b + 1) * 32, 0:512],
                                wb3v[b * 32:(b + 1) * 32, 2 + pli,
                                     W + plo * 32: W + plo * 32 + 32],
                                vprev3[b * 32:(b + 1) * 32, pli,
                                       dh * 512:(dh + 1) * 512],
                                start=(pli == 0), stop=(pli == 1),
                                skip_group_check=True,
                                tile_position=(b * 32, b * 32)))
            vmm_done = sch.edge("t", "P")
            sch.wait("v", "P", vmm_done)

            def f_vev(v, vcur=vcur):
                for dh in range(2):
                    for plo in range(2):
                        v.tensor_copy(
                            vcur[:, plo * D + dh * 512: plo * D + (dh + 1) * 512],
                            pmap[(dh, plo)][:])
                v.drain()
            sch.emit("v", f_vev)
            VRD[k] = v_ready = sch.edge("v", "V")
            if k + 1 < NCH:
                sch.wait("y", "V", v_ready)
                sch.wait("s", "V", v_ready)
                xpre()
                xbar_tiles(vcur, vT3)

        def post_gen(k):
            par = k % 2
            xt, h2t = slot(k)
            vcur = vtok[:, par * 2 * D:(par + 1) * 2 * D]
            vprev = vtok[:, (1 - par) * 2 * D:(2 - par) * 2 * D]
            sch.wait("s", "V", VRD[k])
            if (k + 1) in P0D:
                sch.wait("s", "P", P0D[k + 1])
            sch.emit("s", lambda s, vcur=vcur: s.activation(yt[:], vcur, AT.Tanh))
            tanh_done = sch.edge("s", "S")
            sch.wait("v", "S", tanh_done)

            def f_y1(v):
                v.tensor_scalar(yt[:], yt[:], 1.0, None, OP.add)
                v.drain()
            sch.emit("v", f_y1)
            yield

            def f_y2(v):
                v.tensor_tensor(yt[:], yt[:], h2t, OP.mult)
                v.drain()
            sch.emit("v", f_y2)
            yield
            yield from emit_ln(yt[:], h2t, 0, 2, 12, vT[:])
            yield
            yield from tok_affine(yt[:], h2t, lgt, lbt)
            yield

            def f_res(v):
                v.tensor_tensor(xt, xt, yt[:], OP.add)
                v.drain()
            sch.emit("v", f_res)
            yield from emit_ln(xt, h2t, 0, 2, 12, vT[:])
            yield
            yield from tok_affine(h2t, h2t, cbt, sbt)
            yield
            for q0 in range(0, 2 * D, QW):
                qw = min(QW, 2 * D - q0)
                yield from emit_quant(h2t[:, q0:q0 + qw], xq[:, q0:q0 + qw],
                                      qw, h3, hqt)
                yield
            c_v = sch.edge("v", "V")
            sch.wait("y", "V", c_v)
            if (k + 1) in QMP:
                sch.wait("y", "P", QMP[k + 1])
            xpre()
            cp3 = xbar_tiles(xq, fT3)
            xw("t", cp3)
            yield
            for tl in range(2):
                for half in range(2):
                    sch.wait("s", "V")
                    ICH3 = IHALF[half]

                    def gu_mms(j, dcs, tl=tl, ICH3=ICH3):
                        i0, i1 = ICH3[j]
                        iw = i1 - i0
                        pg = pA if j % 2 == 0 else pB
                        pu = pC if j % 2 == 0 else pD
                        for dc in dcs:
                            sch.emit("t", lambda t, tl=tl, i0=i0, iw=iw,
                                     dc=dc, pg=pg: t.matmul(
                                pg[:, 0:iw],
                                fT3[:, dc, tl * 128:(tl + 1) * 128],
                                wgT[:, dc * I_FF + i0: dc * I_FF + i0 + iw],
                                start=(dc == 0), stop=(dc == DC - 1),
                                skip_group_check=True))
                            sch.emit("t", lambda t, tl=tl, i0=i0, iw=iw,
                                     dc=dc, pu=pu: t.matmul(
                                pu[:, 0:iw],
                                fT3[:, dc, tl * 128:(tl + 1) * 128],
                                wuT[:, dc * I_FF + i0: dc * I_FF + i0 + iw],
                                start=(dc == 0), stop=(dc == DC - 1),
                                skip_group_check=True))
                        return sch.edge("t", "P")

                    c_p = {}
                    c_vs = {}
                    gu_mms(0, range(0, 4))
                    yield
                    c_p[0] = gu_mms(0, range(4, DC))
                    yield
                    for j in range(len(ICH3)):
                        if j + 1 < len(ICH3):
                            if j >= 1:
                                sch.wait("t", "V", c_vs[j - 1])
                            gu_mms(j + 1, range(0, 4))
                            yield
                            c_p[j + 1] = gu_mms(j + 1, range(4, DC))
                        i0, i1 = ICH3[j]
                        iw = i1 - i0
                        hoff = half * 1408
                        pg = pA if j % 2 == 0 else pB
                        pu = pC if j % 2 == 0 else pD
                        so = (j % 2) * 512
                        sch.wait("s", "P", c_p[j])
                        sch.emit("s", lambda s, iw=iw, pg=pg, so=so:
                                 s.activation(h2t[:, so:so + iw],
                                              pg[:, 0:iw], AT.Tanh,
                                              scale=0.5))
                        sch.emit("s", lambda s: s.drain())
                        sch.emit("s", lambda s, iw=iw, so=so: s.activation(
                            h2t[:, so:so + iw], h2t[:, so:so + iw], AT.Copy,
                            bias=0.5, scale=0.5))
                        c_s = sch.edge("s", "S")
                        sch.wait("v", "S", c_s)

                        def f_su(v, i0=i0, iw=iw, hoff=hoff, pg=pg, pu=pu,
                                 so=so, h2t=h2t):
                            v.tensor_tensor(h2t[:, D + so:D + so + iw],
                                            pg[:, 0:iw],
                                            h2t[:, so:so + iw], OP.mult)
                            v.drain()
                            v.tensor_tensor(h3[:, i0 - hoff:i0 - hoff + iw],
                                            h2t[:, D + so:D + so + iw],
                                            pu[:, 0:iw], OP.mult)
                            v.drain()
                        sch.emit("v", f_su)
                        c_vs[j] = sch.edge("v", "V")
                        yield
                    sch.wait("t", "V", c_vs[len(ICH3) - 1])
                    c_h3 = sch.edge("v", "V")
                    sch.wait("s", "V", c_h3)
                    sch.wait("v", "S")
                    for q0 in range(0, 1408, QW):
                        yield from emit_quant(h3[:, q0:q0 + QW],
                                              hqt[:, half * 1408 + q0:
                                                  half * 1408 + q0 + QW], QW,
                                              h2t, vprev)
                    hq_done = sch.edge("v", "V")
                    yield
                sch.wait("y", "V", sch.counts["V"])
                pe_c = {}
                g_c = {}
                GS = ["G", "G2", "G3", "G4"]

                def wd_pref(i2, pe_c=pe_c):
                    guard = pe_c.get(i2 - 4) if i2 >= 4 else DWN["last"]
                    if guard:
                        sch.wait("g", "P", guard)
                    gs = GS[i2 % 4]
                    sch.dma("g", gs, lambda e, h, i2=i2: e.dma_start(
                        out=wdTs[:, (i2 % 4) * D:(i2 % 4 + 1) * D],
                        in_=wdq_h[i2]).then_inc(h, 16))
                    return sch.counts[gs]
                for i2 in range(4):
                    g_c[i2] = wd_pref(i2)
                WV = [list(range(6 * w, min(6 * w + 6, IC))) for w in range(4)]

                def dn_tp(icb):
                    sch.dma("y", "X", lambda e, h, icb=icb: e.dma_start_transpose(
                        out=dn2[:, (icb % 12) * 128:(icb % 12 + 1) * 128],
                        in_=hqt[:, icb * 128:(icb + 1) * 128]).then_inc(h, 16))

                def issue_wave(w, guard):
                    if guard is not None:
                        sch.wait("y", "P", guard)
                    for icb in WV[w]:
                        dn_tp(icb)
                issue_wave(0, None)
                issue_wave(1, None)
                for w in range(4):
                    sch.wait("t", "X", sch.counts["X"])
                    for icb in WV[w]:
                        sch.wait("t", GS[icb % 4], g_c[icb])
                        for hh in range(2):
                            pq = pC if hh == 0 else pD
                            sch.emit("t", lambda t, icb=icb, hh=hh, pq=pq:
                                     t.matmul(
                                pq[:, 0:512],
                                dn2[:, (icb % 12) * 128:(icb % 12 + 1) * 128],
                                wdTs[:, (icb % 4) * D + hh * 512:
                                     (icb % 4) * D + (hh + 1) * 512],
                                start=(icb == 0), stop=(icb == IC - 1),
                                skip_group_check=True))
                        pe_c[icb] = sch.edge("t", "P")
                        if icb + 4 < IC:
                            g_c[icb + 4] = wd_pref(icb + 4)
                        if icb % 3 == 2:
                            yield
                    if w + 2 < 4:
                        issue_wave(w + 2, pe_c[WV[w][-1]])
                DWN["last"] = pe_c[IC - 1]
                c_p = pe_c[IC - 1]
                sch.wait("v", "P", c_p)

                def f_gb(v, tl=tl):
                    for hh in range(2):
                        pq = pC if hh == 0 else pD
                        v.tensor_tensor(
                            yt[:, tl * D + hh * 512: tl * D + (hh + 1) * 512],
                            pq[:, 0:512], gbt[:, hh * 512:(hh + 1) * 512],
                            OP.mult)
                    v.drain()
                sch.emit("v", f_gb)
                c_vg = sch.edge("v", "V")
                sch.wait("t", "V", c_vg)
                yield
            FFP[k] = sch.counts["P"]

            def f_out(v):
                v.tensor_tensor(xt, xt, yt[:], OP.add)
                v.drain()
            sch.emit("v", f_out)
            out_done = sch.edge("v", "V")
            sch.wait("y", "V", out_done)
            for pl in range(2):
                sch.dma("y", "O", lambda e, h, pl=pl, k=k: e.dma_start(
                    out=out_h[:, k * W + pl * 32: k * W + pl * 32 + 32, :],
                    in_=xt[:, pl * D:(pl + 1) * D]).then_inc(h, 16))

        for _ in pre_gen(0, fT[:]):
            pass
        xt0, h2t0 = slot(0)
        vcur0 = vtok[:, 0:2 * D]
        sch.emit("v", lambda v: v.tensor_copy(vcur0, h2t0))
        sch.emit("v", lambda v: v.drain())
        VRD[0] = v_ready0 = sch.edge("v", "V")
        sch.wait("y", "V", v_ready0)
        sch.wait("s", "V", v_ready0)
        xpre()
        xbar_tiles(vcur0, vT3)
        for _ in pre_gen(1, fT[:]):
            pass
        sch.emit("t", lambda t: t.matmul(
            pC[:, 0:512], sel_sb[:, 0:128], sel_sb[:, 0:512],
            start=True, stop=True, skip_group_check=True))
        emit_qp0(1)

        def pump(gen, n):
            for _ in range(n):
                try:
                    next(gen)
                except StopIteration:
                    return False
            return True

        for k in range(1, NCH):
            gens = [post_gen(k - 1)]
            if k + 1 < NCH:
                gens.append(pre_gen(k + 1))
            gen = _ichain(*gens)
            st = {"s_evac": {}, "v_col": {}}
            alive = True
            for c in range(W):
                emit_scan_step(k, c, st)
                if alive:
                    alive = pump(gen, 2)
            while alive:
                alive = pump(gen, 8)
            emit_vtail(k)
            if k + 1 < NCH:
                emit_qp0(k + 1)
        for _ in post_gen(NCH - 1):
            pass

        sch.wait("y", "O")
        block = es.enter_context(nc.Block())
        sch.play(block)
    return nc


_T_FULL = 2048


def kernel(L0_attn_feature, L0_attn_wq_w, L0_attn_wq_b, L0_attn_w2_w,
           L0_attn_w2_b, L0_attn_ln_g, L0_attn_ln_b, L0_ffn_w_gate,
           L0_ffn_w_up, L0_ffn_w_down, L0_c, L0_wmod):
    x = np.ascontiguousarray(np.asarray(L0_attn_feature, dtype=np.float32))
    consts = make_constants()
    shared = {
        "wq": np.ascontiguousarray(np.asarray(L0_attn_wq_w, np.float32)),
        "wqb": np.ascontiguousarray(np.asarray(L0_attn_wq_b, np.float32)),
        "w2": np.ascontiguousarray(np.asarray(L0_attn_w2_w, np.float32)),
        "lng": np.ascontiguousarray(np.asarray(L0_attn_ln_g, np.float32)),
        "lnb": np.ascontiguousarray(np.asarray(L0_attn_ln_b, np.float32)),
        "wg": np.ascontiguousarray(np.asarray(L0_ffn_w_gate, np.float32)),
        "wu": np.ascontiguousarray(np.asarray(L0_ffn_w_up, np.float32)),
        "wd": np.ascontiguousarray(np.asarray(L0_ffn_w_down, np.float32)),
        "wmod": np.ascontiguousarray(np.asarray(L0_wmod, np.float32)),
        "sel": consts["sel"],
        "b3init": consts["b3init"],
    }
    cv = np.ascontiguousarray(np.asarray(L0_c, np.float32))
    nc = build_program(_T_FULL)
    in_maps = []
    for core in range(N_CORES):
        m = dict(shared)
        m["x"] = np.ascontiguousarray(x[core * BPC:(core + 1) * BPC])
        m["cvec"] = np.ascontiguousarray(cv[core * BPC:(core + 1) * BPC])
        in_maps.append(m)
    res = run_bass_kernel_spmd(nc, in_maps, core_ids=list(range(N_CORES)))
    out = np.concatenate([np.asarray(r["out"]) for r in res.results], axis=0)
    return out.astype(np.float32)


# version 14
# speedup vs baseline: 1.0255x; 1.0214x over previous
import numpy as np
from itertools import chain as _ichain
import ml_dtypes
from contextlib import ExitStack

import concourse.bass as bass
import concourse.mybir as mybir
from concourse.bass_utils import run_bass_kernel_spmd

F32 = mybir.dt.float32
BF16 = mybir.dt.bfloat16
AT = mybir.ActivationFunctionType
OP = mybir.AluOpType
AX = mybir.AxisListType

B_TOT = 32
N_CORES = 8
BPC = 4
D = 1024
I_FF = 2816
W = 64
LN_EPS = 1e-5
DC = D // 128
IC = I_FF // 128
NBI = I_FF // 16

ENGS = ("v", "s", "t", "g", "y")


def _bf(x):
    return np.asarray(x, dtype=ml_dtypes.bfloat16)


def make_constants():
    sel = np.zeros((32, 128, 128), np.float32)
    for vv in range(32):
        for m in range(128):
            sel[vv, (m // 32) * 32 + vv, m] = 1.0
    b3i = np.zeros((128, 2 * W), np.float32)
    for p in range(128):
        for pl in range(2):
            b3i[p, pl * W + pl * 32 + (p % 32)] = 1.0
    return {"sel": _bf(sel.reshape(32 * 128, 128)),
            "b3init": _bf(b3i)}


class Sched:
    def __init__(self, nc):
        self.nc = nc
        self.ops = {e: [] for e in ENGS}
        self.sems = {}
        self.counts = {}

    def add_sem(self, es, name):
        h = es.enter_context(self.nc.semaphore(name))
        self.sems[name] = h
        self.counts[name] = 0

    def emit(self, eng, fn):
        self.ops[eng].append(fn)

    def tick(self, sem, n=1):
        self.counts[sem] += n
        return self.counts[sem]

    def edge(self, eng, sem, n=1):
        h = self.sems[sem]
        self.counts[sem] += n
        self.ops[eng].append(lambda e, h=h, n=n: e.drain().then_inc(h, n))
        return self.counts[sem]

    def dma(self, eng, sem, fn):
        h = self.sems[sem]
        self.counts[sem] += 16
        self.ops[eng].append(lambda e, h=h: fn(e, h))
        return self.counts[sem]

    def wait(self, eng, sem, count=None):
        h = self.sems[sem]
        c = self.counts[sem] if count is None else count
        if c > 0:
            self.ops[eng].append(lambda e, h=h, c=c: e.wait_ge(h, c))

    def barrier(self):
        names = {"v": "V", "s": "S", "t": "P", "g": "Q", "y": "Y"}
        for eng, sem in names.items():
            self.edge(eng, sem)
        snap = {sem: self.counts[sem] for sem in names.values()}
        for eng in names:
            for sem, c in snap.items():
                self.wait(eng, sem, c)

    def play(self, block):
        ops = self.ops

        @block.vector
        def _(v):
            for fn in ops["v"]:
                fn(v)

        @block.scalar
        def _(s):
            for fn in ops["s"]:
                fn(s)

        @block.tensor
        def _(t):
            for fn in ops["t"]:
                fn(t)

        @block.gpsimd
        def _(g):
            for fn in ops["g"]:
                fn(g)

        @block.sync
        def _(y):
            for fn in ops["y"]:
                fn(y)


def build_program(T):
    nc = bass.Bass("TRN2", target_bir_lowering=False)
    NCH = T // W
    assert T % W == 0 and NCH >= 2

    x_h = nc.dram_tensor("x", [BPC, T, D], F32, kind="ExternalInput").ap()
    wq_h = nc.dram_tensor("wq", [D, D], F32, kind="ExternalInput").ap()
    wqb_h = nc.dram_tensor("wqb", [D], F32, kind="ExternalInput").ap()
    w2_h = nc.dram_tensor("w2", [D], F32, kind="ExternalInput").ap()
    lng_h = nc.dram_tensor("lng", [D], F32, kind="ExternalInput").ap()
    lnb_h = nc.dram_tensor("lnb", [D], F32, kind="ExternalInput").ap()
    wg_h = nc.dram_tensor("wg", [I_FF, D], F32, kind="ExternalInput").ap()
    wu_h = nc.dram_tensor("wu", [I_FF, D], F32, kind="ExternalInput").ap()
    wd_h = nc.dram_tensor("wd", [D, I_FF], F32, kind="ExternalInput").ap()
    cvec_h = nc.dram_tensor("cvec", [BPC, D], F32, kind="ExternalInput").ap()
    wmod_h = nc.dram_tensor("wmod", [D, 6 * D], F32, kind="ExternalInput").ap()
    sel_h = nc.dram_tensor("sel", [32 * 128, 128], BF16, kind="ExternalInput").ap()
    b3i_h = nc.dram_tensor("b3init", [128, 2 * W], BF16, kind="ExternalInput").ap()
    out_h = nc.dram_tensor("out", [BPC, T, D], F32, kind="ExternalOutput").ap()
    wdq_h = nc.dram_tensor("wdq", [IC, 128, D], BF16, kind="Internal").ap()

    es = ExitStack()
    SB = lambda n, s, d=F32: es.enter_context(nc.sbuf_tensor(n, s, d))
    PS = lambda n, s, d=F32: es.enter_context(nc.psum_tensor(n, s, d))

    with es:
        QW = 1408
        wq_sb = SB("wq_sb", [128, DC * D], BF16)
        wgT = SB("wgT", [128, DC * I_FF], BF16)
        wuT = SB("wuT", [128, DC * I_FF], BF16)
        wdTs = SB("wdTs", [128, 4 * D], BF16)
        dn2 = SB("dn2", [128, 12 * 128], BF16)
        tstg = SB("tstg", [128, 2 * 128], BF16)
        wqb_f = SB("wqb_f", [128, DC], F32)
        w2_f = SB("w2_f", [128, DC], F32)
        lng_f = SB("lng_f", [128, DC], F32)
        lnb_f = SB("lnb_f", [128, DC], F32)
        modf = SB("modf", [128, 6 * BPC * DC], F32)
        lngga = SB("lngga", [128, BPC * DC], F32)
        lnbga = SB("lnbga", [128, BPC * DC], F32)
        silc = SB("silc", [128, BPC * DC], BF16)
        sel_sb = SB("sel_sb", [128, 32 * 128], BF16)
        b3i_sb = SB("b3i_sb", [128, 2 * W], BF16)
        xtd = SB("xtd", [128, 2 * 2 * D], F32)
        fT = SB("fT", [128, DC * 256], BF16)
        stat = SB("stat", [128, 16], F32)
        wb3 = SB("wb3", [128, 4 * 128], BF16)
        eh = SB("eh", [128, W + 1], BF16)
        ebc = SB("ebc", [128, W + 1], F32)
        wnrm = SB("wnrm", [128, W], BF16)
        zac = SB("zac", [128, 4], F32)
        col32 = SB("col32", [128, 4], F32)
        vtok = SB("vtok", [128, 2 * 2 * D], BF16)
        vT = SB("vT", [128, DC * 256], BF16)
        yt = SB("yt", [128, 2 * D], BF16)
        h2d = SB("h2d", [128, 2 * 2 * D], BF16)
        h3 = SB("h3", [128, I_FF // 2], BF16)
        hqt = SB("hqt", [128, I_FF], BF16)

        qsc = SB("qsc", [128, QW // 16], F32)
        qrs = SB("qrs", [128, QW // 16], F32)
        qa = SB("qa", [128, QW], BF16)
        gbt = SB("gbt", [128, D], BF16)
        cat = SB("cat", [128, D], BF16)
        sat = SB("sat", [128, D], BF16)
        cbt = SB("cbt", [128, D], BF16)
        sbt = SB("sbt", [128, D], BF16)
        lgt = SB("lgt", [128, D], BF16)
        lbt = SB("lbt", [128, D], BF16)
        junk = SB("junk", [128, 4 * W], BF16)
        sgn = SB("sgn", [128, QW], BF16)
        ones1 = SB("ones1", [128, 4], F32)

        qT = yt[:, 0:2 * D]
        wmst = hqt[:, 0:1536]
        xq = yt[:]
        wnat = vtok[:, 0:I_FF]
        cstg = xtd[:, 0:BPC * DC]
        ms1 = xtd[:, 64:64 + BPC * DC]
        ms2 = xtd[:, 128:128 + BPC * DC]

        pA = PS("pA", [128, 512], F32)
        pB = PS("pB", [128, 512], F32)
        pC = PS("pC", [128, 512], F32)
        pD = PS("pD", [128, 512], F32)
        pbc = [PS("pbc0", [128, W + 1], F32), PS("pbc1", [128, W + 1], F32)]

        sch = Sched(nc)
        for sname in ("L", "G", "G2", "G3", "G4", "V", "S", "P", "Q", "X", "X2", "O", "Y"):
            sch.add_sem(es, sname)

        wb3v = wb3[:].rearrange("p (pl c) -> p pl c", pl=4)
        modv = modf[:].rearrange("p (j b dc) -> p j b dc", j=6, b=BPC)
        fT3 = fT[:].rearrange("p (dc t) -> p dc t", dc=DC)
        qT3 = qT.rearrange("p (dc t) -> p dc t", dc=DC)
        vT3 = vT[:].rearrange("p (dc t) -> p dc t", dc=DC)
        silc3 = silc[:].rearrange("p (b dc) -> p b dc", b=BPC)
        ms1v = ms1.rearrange("p (b dc) -> p b dc", b=BPC)
        ms2v = ms2.rearrange("p (b dc) -> p b dc", b=BPC)

        def ldvec(dst, src):
            sch.dma("y", "L", lambda e, h, dst=dst, src=src: e.dma_start(
                out=dst[:], in_=src.rearrange("(dc p) -> p dc", p=128),
                allow_slow_non_contiguous=True).then_inc(h, 16))

        ldvec(wqb_f, wqb_h)
        ldvec(w2_f, w2_h)
        ldvec(lng_f, lng_h)
        ldvec(lnb_f, lnb_h)
        sch.dma("y", "L", lambda e, h: e.dma_start(
            out=sel_sb[:].rearrange("p (v m) -> p v m", v=32),
            in_=sel_h.rearrange("(v p) m -> p v m", p=128)).then_inc(h, 16))
        sch.dma("y", "L", lambda e, h: e.dma_start(out=b3i_sb[:], in_=b3i_h).then_inc(h, 16))
        sch.dma("y", "L", lambda e, h: e.dma_start(
            out=cstg.rearrange("p (b dc) -> p b dc", b=BPC),
            in_=cvec_h.rearrange("b (dc p) -> p b dc", p=128),
            allow_slow_non_contiguous=True).then_inc(h, 16))
        vec_ld = sch.counts["L"]
        sch.dma("g", "G", lambda e, h: e.dma_start(
            out=wq_sb[:].rearrange("p (dc o) -> p dc o", dc=DC),
            in_=wq_h.rearrange("(dc p) o -> p dc o", p=128)).then_inc(h, 16))
        wq_ld = sch.counts["G"]

        sch.wait("s", "L", vec_ld)
        sch.emit("s", lambda s: s.activation(silc[:], cstg, AT.Silu))
        silc_done = sch.edge("s", "S")

        first_mm = [True]
        for dc in range(DC):
            for qtr in range(4):
                sch.dma("g", "G", lambda e, h, dc=dc, qtr=qtr: e.dma_start(
                    out=wmst[:], in_=wmod_h[dc * 128:(dc + 1) * 128,
                                            qtr * 1536:(qtr + 1) * 1536]).then_inc(h, 16))
                sch.wait("t", "G")
                if dc == 0 and qtr == 0:
                    sch.wait("t", "S", silc_done)
                for mcl in range(12):
                    mc = qtr * 12 + mcl
                    sch.emit("t", lambda t, dc=dc, mc=mc, mcl=mcl, f=first_mm[0]:
                             t.matmul(
                        pA[:, mc * 4:(mc + 1) * 4],
                        wmst[:, mcl * 128:(mcl + 1) * 128],
                        silc3[:, :, dc],
                        start=f, stop=(dc == DC - 1),
                        skip_group_check=True))
                    first_mm[0] = False
                pe_c = sch.edge("t", "P")
                sch.wait("g", "P", pe_c)
        mod_mm = sch.edge("t", "P")

        sch.wait("v", "P", mod_mm)
        pAv = pA[:, 0:192].rearrange("p (j dc b) -> p j b dc", j=6, dc=DC)
        sch.emit("v", lambda v: v.tensor_copy(modv[:, :, :, :], pAv))
        sch.emit("v", lambda v: v.drain())

        def gprime(j):
            def f(v, j=j):
                m = modv[:, j]
                v.tensor_scalar(ms1v, m, 0.0, 2.0, OP.is_ge, OP.mult)
                v.drain()
                v.tensor_scalar(ms2v, ms1v, 1.0, None, OP.subtract)
                v.drain()
                v.tensor_scalar(ms1v, m, -1.0, None, OP.mult)
                v.drain()
                v.tensor_tensor(ms1v, ms1v, m, OP.max)
                v.drain()
                v.tensor_scalar(ms1v, ms1v, 0.25, None, OP.max)
                v.drain()
                v.tensor_tensor(m, ms1v, ms2v, OP.mult)
                v.drain()
            sch.emit("v", f)

        gprime(2)
        gprime(5)
        sch.emit("v", lambda v: v.tensor_scalar(modv[:, 1], modv[:, 1], 1.0, None, OP.add))
        sch.emit("v", lambda v: v.tensor_scalar(modv[:, 4], modv[:, 4], 1.0, None, OP.add))
        sch.emit("v", lambda v: v.drain())

        def f_lngga(v):
            for b in range(BPC):
                v.tensor_tensor(lngga[:, b * DC:(b + 1) * DC], lng_f[:], modv[:, 2, b], OP.mult)
                v.tensor_tensor(lnbga[:, b * DC:(b + 1) * DC], lnb_f[:], modv[:, 2, b], OP.mult)
            v.memset(ones1[:], 1.0)
            v.drain()
        sch.emit("v", f_lngga)
        sch.emit("v", lambda v: v.memset(eh[:, W:W + 1], 1.0))
        sch.emit("v", lambda v: v.drain())
        mod_done = sch.edge("v", "V")

        def emit_quant(src_ap, dst_ap, NC, scrB=None, scrB2=None):
            nb = NC // 16
            src3 = src_ap.rearrange("p (nb k) -> p nb k", k=16)
            sc3 = qsc[:, 0:nb].rearrange("p (nb o) -> p nb o", o=1)

            def v1(v):
                v.tensor_reduce(sc3, src3, AX.X, OP.max, apply_absolute_value=True)
                v.drain()
                v.tensor_scalar(qsc[:, 0:nb], qsc[:, 0:nb], 1.0 / 6.0, 1e-12,
                                OP.mult, OP.max)
                v.drain()
                v.reciprocal(qrs[:, 0:nb], qsc[:, 0:nb])
                v.drain()
            sch.emit("v", v1)

            def vsg(v):
                v.tensor_scalar(sgn[:, 0:NC], src_ap, 0.0, 2.0, OP.is_ge, OP.mult)
                v.drain()
                v.tensor_scalar(sgn[:, 0:NC], sgn[:, 0:NC], 1.0, None, OP.subtract)
                v.drain()
            sch.emit("v", vsg)
            yield

            def v2a(v):
                rsb = qrs[:, 0:nb].rearrange("p (nb o) -> p nb o", o=1).broadcast_to([128, nb, 16])
                a13 = qa[:, 0:NC].rearrange("p (nb k) -> p nb k", k=16)
                v.tensor_tensor(a13, src3, rsb, OP.mult)
                v.drain()
                v.tensor_tensor(qa[:, 0:NC], qa[:, 0:NC], sgn[:, 0:NC], OP.mult)
                v.drain()
            sch.emit("v", v2a)
            yield
            a = dst_ap
            bb = [scrB[:, 0:NC], scrB2[:, 0:NC]]
            vv = qa[:, 0:NC]
            steps = [(1.25, 0.5), (1.75, 0.5), (2.5, 1.0), (3.5, 1.0), (5.0, 2.0)]

            def v2b(v):
                v.tensor_scalar(a, vv, 0.25, 0.5, OP.is_gt, OP.mult)
                v.tensor_scalar(bb[0], vv, 0.75, 0.5, OP.is_gt, OP.mult)
                v.drain()
                for i, (thr, wgt) in list(enumerate(steps))[:2]:
                    v.tensor_tensor(a, a, bb[i % 2], OP.add)
                    v.tensor_scalar(bb[(i + 1) % 2], vv, thr, wgt, OP.is_gt, OP.mult)
                    v.drain()
            sch.emit("v", v2b)
            yield

            def v2c(v):
                for i, (thr, wgt) in list(enumerate(steps))[2:]:
                    v.tensor_tensor(a, a, bb[i % 2], OP.add)
                    v.tensor_scalar(bb[(i + 1) % 2], vv, thr, wgt, OP.is_gt, OP.mult)
                    v.drain()
                v.tensor_tensor(a, a, bb[len(steps) % 2], OP.add)
                v.drain()
            sch.emit("v", v2c)
            yield

            def v2d(v):
                v.tensor_tensor(a, a, sgn[:, 0:NC], OP.mult)
                v.drain()
                scb = qsc[:, 0:nb].rearrange("p (nb o) -> p nb o", o=1).broadcast_to([128, nb, 16])
                dst3 = dst_ap.rearrange("p (nb k) -> p nb k", k=16)
                v.tensor_tensor(dst3, dst3, scb, OP.mult)
                v.drain()
            sch.emit("v", v2d)

        sch.wait("g", "V", mod_done)

        wnat2 = h2d[:, 0:I_FF]

        def weight_pipe(src_h, nslab, ncols, dst, dst_col, to_dram=False):
            bufs = [wnat, wnat2]
            cq_hist = {}

            def slab_load(sl):
                gs = "G" if sl % 2 == 0 else "G2"
                if sl >= 2:
                    sch.wait("g", "V", cq_hist[sl - 2])
                sch.dma("g", gs, lambda e, h, sl=sl: e.dma_start(
                    out=bufs[sl % 2][:, 0:ncols],
                    in_=src_h[sl * 128:(sl + 1) * 128, :]).then_inc(h, 16))
                return sch.counts[gs]
            gtot = {0: slab_load(0)}
            for sl in range(nslab):
                if sl + 1 < nslab:
                    gtot[sl + 1] = slab_load(sl + 1)
                sch.wait("v", "G" if sl % 2 == 0 else "G2", gtot[sl])
                sch.wait("s", "G" if sl % 2 == 0 else "G2", gtot[sl])
                for q0 in range(0, ncols, QW):
                    qw = min(QW, ncols - q0)
                    for _ in emit_quant(bufs[sl % 2][:, q0:q0 + qw],
                                        hqt[:, q0:q0 + qw], qw, h3, yt):
                        pass
                cq_hist[sl] = sch.counts["V"] + 1
                cq = sch.edge("v", "V")
                assert cq == cq_hist[sl]
                sch.wait("y", "V", cq)
                sch.wait("s", "V", cq)
                if to_dram:
                    for cb in range(ncols // 128):
                        if cb >= 2:
                            sch.wait("y", "O", sch.counts["O"])
                        sch.dma("y", "X", lambda e, h, sl=sl, cb=cb:
                                e.dma_start_transpose(
                            out=tstg[:, (cb % 2) * 128:(cb % 2 + 1) * 128],
                            in_=hqt[:, cb * 128:(cb + 1) * 128]).then_inc(h, 16))
                        sch.wait("y", "X", sch.counts["X"])
                        sch.dma("y", "O", lambda e, h, sl=sl, cb=cb: e.dma_start(
                            out=wdq_h[cb, :, sl * 128:(sl + 1) * 128],
                            in_=tstg[:, (cb % 2) * 128:(cb % 2 + 1) * 128]
                        ).then_inc(h, 16))
                    cx, co = sch.counts["X"], sch.counts["O"]
                    for e in ("v", "s", "g", "y"):
                        sch.wait(e, "X", cx)
                    sch.wait("y", "O", co)
                else:
                    for cb in range(ncols // 128):
                        eng, sem = ("y", "X") if cb % 2 == 0 else ("s", "X2")
                        sch.dma(eng, sem, lambda e, h, sl=sl, cb=cb:
                                e.dma_start_transpose(
                            out=dst[:, dst_col(cb, sl):dst_col(cb, sl) + 128],
                            in_=hqt[:, cb * 128:(cb + 1) * 128]).then_inc(h, 16))
                    cx, cx2 = sch.counts["X"], sch.counts["X2"]
                    for e in ("v", "s", "g", "y"):
                        sch.wait(e, "X", cx)
                        sch.wait(e, "X2", cx2)

        weight_pipe(wg_h, IC, D, wgT, lambda cb, sl: cb * I_FF + sl * 128)
        weight_pipe(wu_h, IC, D, wuT, lambda cb, sl: cb * I_FF + sl * 128)
        weight_pipe(wd_h, DC, I_FF, None, None, to_dram=True)
        sch.wait("t", "G")
        sch.barrier()

        def emit_rstd(nsrc, ndst):
            vs = stat[:, nsrc:nsrc + 2]
            t_ = stat[:, 4:6]
            r_ = stat[:, 6:8]
            u_ = stat[:, 8:10]
            w_ = stat[:, 10:12]
            y_ = stat[:, ndst:ndst + 2]

            def f1(v):
                v.tensor_scalar(t_, vs, 1.0 / D, LN_EPS, OP.mult, OP.add)
                v.drain()
                v.reciprocal(r_, t_)
                v.drain()
            sch.emit("v", f1)
            c_v = sch.edge("v", "V")
            sch.wait("g", "V", c_v)

            def f2(g):
                g.tensor_scalar(u_, t_, 2.0, None, OP.is_gt)
                g.drain()
                g.tensor_scalar(w_, u_, 0.58, 0.42, OP.mult, OP.add)
                g.tensor_scalar(u_, u_, -0.395, 0.62, OP.mult, OP.add)
                g.drain()
                g.tensor_tensor(u_, u_, t_, OP.mult)
                g.drain()
                g.tensor_tensor(u_, u_, w_, OP.add)
                g.drain()
                g.tensor_tensor(y_, r_, u_, OP.mult)
                g.drain()
                for _ in range(3):
                    g.tensor_tensor(u_, y_, y_, OP.mult)
                    g.drain()
                    g.tensor_tensor(u_, u_, t_, OP.mult)
                    g.drain()
                    g.tensor_scalar(u_, u_, -0.5, 1.5, OP.mult, OP.add)
                    g.drain()
                    g.tensor_tensor(y_, y_, u_, OP.mult)
                    g.drain()
            sch.emit("g", f2)
            c_g = sch.edge("g", "Q")
            sch.wait("v", "Q", c_g)

        def xpre():
            sch.wait("y", "X")
            sch.wait("y", "X2")

        def xw(eng, cp=None):
            cx, cx2 = cp if cp else (sch.counts["X"], sch.counts["X2"])
            sch.wait(eng, "X", cx)
            sch.wait(eng, "X2", cx2)

        def xbar_tiles(src_tok, dstT3):
            for pl in range(2):
                for dc in range(DC):
                    sch.dma("y", "X", lambda e, h, pl=pl, dc=dc: e.dma_start_transpose(
                        out=dstT3[:, dc, pl * 128:(pl + 1) * 128],
                        in_=src_tok[:, pl * D + dc * 128: pl * D + (dc + 1) * 128]
                    ).then_inc(h, 16))
            return (sch.counts["X"], sch.counts["X2"])

        def xbar_back(srcT3, dst_tok):
            for pl in range(2):
                for dc in range(DC):
                    sch.dma("y", "X", lambda e, h, pl=pl, dc=dc: e.dma_start_transpose(
                        out=dst_tok[:, pl * D + dc * 128: pl * D + (dc + 1) * 128],
                        in_=srcT3[:, dc, pl * 128:(pl + 1) * 128]).then_inc(h, 16))
            return (sch.counts["X"], sch.counts["X2"])

        def _mod_bcast(j):
            a = modv[:, j].transpose([0, 2, 1]).unsqueeze(3)
            return a.broadcast_to([128, DC, BPC, 32])

        def feat_affine(T3, j_scale, j_shift):
            def f(v):
                for pl in range(2):
                    T5 = T3.rearrange("p dc (pl b q) -> p dc pl b q",
                                      pl=2, b=BPC)[:, :, pl]
                    v.tensor_tensor(T5, T5, _mod_bcast(j_scale), OP.mult)
                v.drain()
                if j_shift is not None:
                    for pl in range(2):
                        T5 = T3.rearrange("p dc (pl b q) -> p dc pl b q",
                                          pl=2, b=BPC)[:, :, pl]
                        v.tensor_tensor(T5, T5, _mod_bcast(j_shift), OP.add)
                    v.drain()
            sch.emit("v", f)

        def emit_ln(src_ap, dst_bf_ap, musl, varsl, rstdsl, sq_scratch):
            s2 = src_ap.rearrange("p (pl d) -> p pl d", pl=2)

            def f0(v):
                v.tensor_reduce(stat[:, musl:musl + 2].unsqueeze(2), s2, AX.X, OP.add)
                v.drain()
                v.tensor_scalar(stat[:, musl:musl + 2], stat[:, musl:musl + 2],
                                1.0 / D, None, OP.mult)
                v.drain()
            sch.emit("v", f0)
            yield
            for pl in range(2):
                s_ = src_ap[:, pl * D:(pl + 1) * D]
                d_ = dst_bf_ap[:, pl * D:(pl + 1) * D]

                def f(v, s_=s_, d_=d_, pl=pl):
                    v.tensor_scalar(d_, s_, stat[:, musl + pl:musl + pl + 1], None,
                                    OP.subtract)
                    v.drain()
                    v.scalar_tensor_tensor(sq_scratch[:, pl * D:(pl + 1) * D], d_, 1.0,
                                           d_, OP.mult, OP.mult,
                                           accum_out=stat[:, varsl + pl:varsl + pl + 1])
                    v.drain()
                sch.emit("v", f)
                yield
            emit_rstd(varsl, rstdsl)
            yield

            def fn(v):
                for pl in range(2):
                    v.tensor_scalar(dst_bf_ap[:, pl * D:(pl + 1) * D],
                                    dst_bf_ap[:, pl * D:(pl + 1) * D],
                                    stat[:, rstdsl + pl:rstdsl + pl + 1], None, OP.mult)
                v.drain()
            sch.emit("v", fn)

        def feat_affine_ln(T3, vec_a, vec_b):
            def bc(vec):
                a = vec[:].rearrange("p (b dc) -> p b dc", b=BPC).transpose(
                    [0, 2, 1]).unsqueeze(3)
                return a.broadcast_to([128, DC, BPC, 32])

            def f(v):
                for op, vec in ((OP.mult, vec_a), (OP.add, vec_b)):
                    for pl in range(2):
                        T5 = T3.rearrange("p dc (pl b q) -> p dc pl b q",
                                          pl=2, b=BPC)[:, :, pl]
                        v.tensor_tensor(T5, T5, bc(vec), op)
                    v.drain()
            sch.emit("v", f)

        def _vec_bcast(vec):
            a = vec[:].rearrange("p (b dc) -> p b dc", b=BPC).transpose(
                [0, 2, 1]).unsqueeze(3)
            return a.broadcast_to([128, DC, BPC, 32])

        for bsrc, dst in ((_mod_bcast(5), gbt), (_mod_bcast(1), cat),
                          (_mod_bcast(0), sat), (_mod_bcast(4), cbt),
                          (_mod_bcast(3), sbt), (_vec_bcast(lngga), lgt),
                          (_vec_bcast(lnbga), lbt)):
            def f_gfm(v, bsrc=bsrc):
                v.memset(fT[:], 1.0)
                v.drain()
                T5 = fT3.rearrange("p dc (pl b q) -> p dc pl b q",
                                   pl=2, b=BPC)[:, :, 0]
                v.tensor_tensor(T5, T5, bsrc, OP.mult)
                v.drain()
            sch.emit("v", f_gfm)
            c_v = sch.edge("v", "V")
            sch.wait("y", "V", c_v)
            xpre()
            for dc in range(DC):
                sch.dma("y", "X", lambda e, h, dc=dc, dst=dst:
                        e.dma_start_transpose(
                    out=dst[:, dc * 128:(dc + 1) * 128],
                    in_=fT3[:, dc, 0:128]).then_inc(h, 16))
            xw("v")

        def tok_affine(dst, srcv, mtile, atile):
            d3 = dst.rearrange("p (pl f) -> p pl f", pl=2)
            s3 = srcv.rearrange("p (pl f) -> p pl f", pl=2)
            mb = mtile[:].unsqueeze(1).broadcast_to([128, 2, D])
            ab = atile[:].unsqueeze(1).broadcast_to([128, 2, D])

            def f1(v):
                v.tensor_tensor(d3, s3, mb, OP.mult)
                v.drain()
            sch.emit("v", f1)
            yield

            def f2(v):
                v.tensor_tensor(d3, d3, ab, OP.add)
                v.drain()
            sch.emit("v", f2)

        IHALF = [[(0, 512), (512, 1024), (1024, 1408)],
                 [(1408, 1920), (1920, 2432), (2432, 2816)]]
        P0D = {}
        SCI = {}
        VRD = {}
        FFP = {}
        QMP = {}
        DWN = {"last": None}

        def slot(k):
            par = k % 2
            return (xtd[:, par * 2 * D:(par + 1) * 2 * D],
                    h2d[:, par * 2 * D:(par + 1) * 2 * D])

        def pre_gen(k, lnscr=None):
            if lnscr is None:
                lnscr = vT[:]
            xt, h2t = slot(k)
            sch.wait("s", "V")
            sch.wait("s", "O")
            for pl in range(2):
                sch.dma("s", "L", lambda e, h, pl=pl, k=k: e.dma_start(
                    out=xt[:, pl * D:(pl + 1) * D],
                    in_=x_h[:, k * W + pl * 32: k * W + pl * 32 + 32, :]
                ).then_inc(h, 16))
            sch.wait("v", "L")
            yield
            yield from emit_ln(xt, yt[:], 0, 2, 12, lnscr)
            yield
            yield from tok_affine(h2t, yt[:], cat, sat)
            c_hT = sch.edge("v", "V")
            yield
            if k >= 1:
                sch.wait("y", "V", c_hT)
                if (k - 2) in FFP:
                    sch.wait("y", "P", FFP[k - 2])
                xpre()
                cp = xbar_tiles(h2t, fT3)
                xw("t", cp)
                yield
            if k >= 1:
                for oc in range(DC):
                    pq = pA if oc % 2 == 0 else pB
                    for dc in range(DC):
                        sch.emit("t", lambda t, oc=oc, dc=dc, pq=pq: t.matmul(
                            pq[:, 0:256],
                            wq_sb[:, dc * D + oc * 128: dc * D + (oc + 1) * 128],
                            fT3[:, dc, :],
                            start=(dc == 0), stop=(dc == DC - 1),
                            skip_group_check=True))
                    c_p = sch.edge("t", "P")
                    sch.wait("v", "P", c_p)

                    def fq(v, oc=oc, pq=pq):
                        v.tensor_scalar(qT3[:, oc, :], pq[:, 0:256],
                                        wqb_f[:, oc:oc + 1], w2_f[:, oc:oc + 1],
                                        OP.add, OP.mult)
                        v.drain()
                    sch.emit("v", fq)
                    c_vq = sch.edge("v", "V")
                    sch.wait("t", "V", c_vq)
                    if oc % 2 == 1:
                        yield
                QMP[k] = sch.counts["P"]

        def emit_qp0(k):
            xw("t")
            sch.wait("t", "V")
            vT4 = vT3[:, :, :].rearrange("p dc (pl q) -> p dc pl q", pl=2)
            first = True
            for b in range(BPC):
                for plr in range(2):
                    for dc in range(DC):
                        sch.emit("t", lambda t, b=b, plr=plr, dc=dc, first=first:
                                 t.matmul(
                            pC[b * 32:(b + 1) * 32, plr * 64: plr * 64 + 64],
                            qT3[:, dc, plr * 128 + b * 32: plr * 128 + b * 32 + 32],
                            vT4[:, dc, :, b * 32:(b + 1) * 32],
                            start=(dc == 0 and first), stop=(dc == DC - 1),
                            skip_group_check=True, tile_position=(0, b * 32)))
                    first = False
            P0D[k] = p0_done = sch.edge("t", "P")
            sch.wait("v", "P", p0_done)

            def f_p0(v):
                v.tensor_copy(wb3v[:, 0:2, 0:W], pC[:, 0:128].rearrange(
                    "p (pl c) -> p pl c", pl=2))
                v.tensor_copy(wb3v[:, 2:4, 0:W], b3i_sb[:].rearrange(
                    "p (pl c) -> p pl c", pl=2))
                v.drain()
            sch.emit("v", f_p0)
            SCI[k] = sch.edge("v", "V")

        def emit_scan_step(k, c, st):
            s_evac = st["s_evac"]
            v_col = st["v_col"]
            plc = c // 32
            pl0 = 0 if c < 32 else 1
            npl = 4 - pl0
            sch.wait("s", "V", v_col.get(c - 1, SCI[k]))
            cs = sch.tick("S")
            sch.emit("s", lambda s, plc=plc, c=c, cs=cs: s.activation(
                eh[:, 0:W], wb3v[:, plc, c:c + W], AT.Exp).then_inc(
                sch.sems["S"], 1))
            sch.wait("t", "S", cs)
            if c >= 2:
                sch.wait("t", "V", s_evac[c - 2])
            cp = sch.tick("P")
            sch.emit("t", lambda t, c=c: t.matmul(
                pbc[c % 2][:], sel_sb[:, (c % 32) * 128:(c % 32 + 1) * 128],
                eh[:], start=True, stop=True).then_inc(sch.sems["P"], 1))
            sch.wait("v", "P", cp)
            ce1 = sch.tick("V")
            sch.emit("v", lambda v, c=c: v.tensor_scalar(
                ebc[:], pbc[c % 2][:], 1.0, 0.0, OP.mult, OP.add,
                accum_out=zac[:, 0:1]).then_inc(sch.sems["V"], 1))
            s_evac[c] = ce1
            sch.wait("v", "V", ce1)
            ce2 = sch.tick("V")
            sch.emit("v", lambda v: v.reciprocal(
                zac[:, 2:3], zac[:, 0:1]).then_inc(sch.sems["V"], 1))
            if c >= 1:
                sch.wait("v", "V", v_col[c - 1])
            ce3 = sch.tick("V")

            def f_tt(v, c=c, pl0=pl0, npl=npl):
                wn = ebc[:, 0:W].rearrange("p (o k) -> p o k", o=1
                                           ).broadcast_to([128, npl, W])
                jv = junk[:].rearrange("p (pl k) -> p pl k", pl=4)[:, pl0:4]
                v.tensor_tensor(jv, wb3v[:, pl0:4, c:c + W], wn,
                                OP.mult).then_inc(sch.sems["V"], 1)
            sch.emit("v", f_tt)
            sch.wait("v", "V", ce3)
            ce4 = sch.tick("V")

            def f_rd(v, pl0=pl0, npl=npl):
                jv = junk[:].rearrange("p (pl k) -> p pl k", pl=4)[:, pl0:4]
                v.tensor_reduce(col32[:, pl0:4].unsqueeze(2), jv,
                                AX.X, OP.add).then_inc(sch.sems["V"], 1)
            sch.emit("v", f_rd)
            sch.wait("v", "V", ce2)
            sch.wait("v", "V", ce4)
            ce5 = sch.tick("V")

            def f_sc(v, c=c, pl0=pl0):
                v.tensor_scalar(wb3v[:, pl0:4, W + c:W + c + 1],
                                col32[:, pl0:4].unsqueeze(2),
                                zac[:, 2:3], None, OP.mult).then_inc(
                    sch.sems["V"], 1)
            sch.emit("v", f_sc)
            v_col[c] = ce5

        def emit_vtail(k):
            par = k % 2
            vcur = vtok[:, par * 2 * D:(par + 1) * 2 * D]
            vprev = vtok[:, (1 - par) * 2 * D:(2 - par) * 2 * D]
            vprev3 = vprev.rearrange("p (pl d) -> p pl d", pl=2)
            sch.wait("t", "V")
            pmap = {(0, 0): pA, (0, 1): pB, (1, 0): pC, (1, 1): pD}
            for b in range(BPC):
                for dh in range(2):
                    for plo in range(2):
                        pq = pmap[(dh, plo)]
                        for pli in range(2):
                            sch.emit("t", lambda t, b=b, dh=dh, plo=plo, pli=pli,
                                     pq=pq: t.matmul(
                                pq[b * 32:(b + 1) * 32, 0:512],
                                wb3v[b * 32:(b + 1) * 32, 2 + pli,
                                     W + plo * 32: W + plo * 32 + 32],
                                vprev3[b * 32:(b + 1) * 32, pli,
                                       dh * 512:(dh + 1) * 512],
                                start=(pli == 0), stop=(pli == 1),
                                skip_group_check=True,
                                tile_position=(b * 32, b * 32)))
            vmm_done = sch.edge("t", "P")
            sch.wait("v", "P", vmm_done)

            def f_vev(v, vcur=vcur):
                for dh in range(2):
                    for plo in range(2):
                        v.tensor_copy(
                            vcur[:, plo * D + dh * 512: plo * D + (dh + 1) * 512],
                            pmap[(dh, plo)][:])
                v.drain()
            sch.emit("v", f_vev)
            VRD[k] = v_ready = sch.edge("v", "V")
            if k + 1 < NCH:
                sch.wait("y", "V", v_ready)
                sch.wait("s", "V", v_ready)
                xpre()
                xbar_tiles(vcur, vT3)

        def post_gen(k):
            par = k % 2
            xt, h2t = slot(k)
            vcur = vtok[:, par * 2 * D:(par + 1) * 2 * D]
            vprev = vtok[:, (1 - par) * 2 * D:(2 - par) * 2 * D]
            sch.wait("s", "V", VRD[k])
            if (k + 1) in P0D:
                sch.wait("s", "P", P0D[k + 1])
            sch.emit("s", lambda s, vcur=vcur: s.activation(yt[:], vcur, AT.Tanh))
            tanh_done = sch.edge("s", "S")
            sch.wait("v", "S", tanh_done)

            def f_y1(v):
                v.tensor_scalar(yt[:], yt[:], 1.0, None, OP.add)
                v.drain()
            sch.emit("v", f_y1)
            yield

            def f_y2(v):
                v.tensor_tensor(yt[:], yt[:], h2t, OP.mult)
                v.drain()
            sch.emit("v", f_y2)
            yield
            yield from emit_ln(yt[:], h2t, 0, 2, 12, vT[:])
            yield
            yield from tok_affine(yt[:], h2t, lgt, lbt)
            yield

            def f_res(v):
                v.tensor_tensor(xt, xt, yt[:], OP.add)
                v.drain()
            sch.emit("v", f_res)
            yield from emit_ln(xt, h2t, 0, 2, 12, vT[:])
            yield
            yield from tok_affine(h2t, h2t, cbt, sbt)
            yield
            for q0 in range(0, 2 * D, QW):
                qw = min(QW, 2 * D - q0)
                yield from emit_quant(h2t[:, q0:q0 + qw], xq[:, q0:q0 + qw],
                                      qw, h3, hqt)
                yield
            c_v = sch.edge("v", "V")
            sch.wait("y", "V", c_v)
            if (k + 1) in QMP:
                sch.wait("y", "P", QMP[k + 1])
            xpre()
            cp3 = xbar_tiles(xq, fT3)
            xw("t", cp3)
            yield
            for tl in range(2):
                for half in range(2):
                    sch.wait("s", "V")
                    ICH3 = IHALF[half]

                    def gu_mms(j, dcs, edge=False, tl=tl, ICH3=ICH3):
                        i0, i1 = ICH3[j]
                        iw = i1 - i0
                        pg = pA if j % 2 == 0 else pB
                        pu = pC if j % 2 == 0 else pD
                        for dc in dcs:
                            sch.emit("t", lambda t, tl=tl, i0=i0, iw=iw,
                                     dc=dc, pg=pg: t.matmul(
                                pg[:, 0:iw],
                                fT3[:, dc, tl * 128:(tl + 1) * 128],
                                wgT[:, dc * I_FF + i0: dc * I_FF + i0 + iw],
                                start=(dc == 0), stop=(dc == DC - 1),
                                skip_group_check=True))
                            sch.emit("t", lambda t, tl=tl, i0=i0, iw=iw,
                                     dc=dc, pu=pu: t.matmul(
                                pu[:, 0:iw],
                                fT3[:, dc, tl * 128:(tl + 1) * 128],
                                wuT[:, dc * I_FF + i0: dc * I_FF + i0 + iw],
                                start=(dc == 0), stop=(dc == DC - 1),
                                skip_group_check=True))
                        if edge:
                            return sch.edge("t", "P")
                        return None

                    c_p = {}
                    c_vs = {}
                    gu_mms(0, range(0, 4))
                    yield
                    c_p[0] = gu_mms(0, range(4, DC), edge=True)
                    yield
                    for j in range(len(ICH3)):
                        if j + 1 < len(ICH3):
                            if j >= 1:
                                sch.wait("t", "V", c_vs[j - 1])
                            gu_mms(j + 1, range(0, 4))
                            yield
                            c_p[j + 1] = gu_mms(j + 1, range(4, DC), edge=True)
                        i0, i1 = ICH3[j]
                        iw = i1 - i0
                        hoff = half * 1408
                        pg = pA if j % 2 == 0 else pB
                        pu = pC if j % 2 == 0 else pD
                        so = (j % 2) * 512
                        sch.wait("s", "P", c_p[j])
                        sch.emit("s", lambda s, iw=iw, pg=pg, so=so:
                                 s.activation(h2t[:, so:so + iw],
                                              pg[:, 0:iw], AT.Tanh,
                                              scale=0.5))
                        sch.emit("s", lambda s: s.drain())
                        sch.emit("s", lambda s, iw=iw, so=so: s.activation(
                            h2t[:, so:so + iw], h2t[:, so:so + iw], AT.Copy,
                            bias=0.5, scale=0.5))
                        c_s = sch.edge("s", "S")
                        sch.wait("v", "S", c_s)

                        def f_su(v, i0=i0, iw=iw, hoff=hoff, pg=pg, pu=pu,
                                 so=so, h2t=h2t):
                            v.tensor_tensor(h2t[:, D + so:D + so + iw],
                                            pg[:, 0:iw],
                                            h2t[:, so:so + iw], OP.mult)
                            v.drain()
                            v.tensor_tensor(h3[:, i0 - hoff:i0 - hoff + iw],
                                            h2t[:, D + so:D + so + iw],
                                            pu[:, 0:iw], OP.mult)
                            v.drain()
                        sch.emit("v", f_su)
                        c_vs[j] = sch.edge("v", "V")
                        yield
                    sch.wait("t", "V", c_vs[len(ICH3) - 1])
                    c_h3 = sch.edge("v", "V")
                    sch.wait("s", "V", c_h3)
                    sch.wait("v", "S")
                    for q0 in range(0, 1408, QW):
                        yield from emit_quant(h3[:, q0:q0 + QW],
                                              hqt[:, half * 1408 + q0:
                                                  half * 1408 + q0 + QW], QW,
                                              h2t, vprev)
                    hq_done = sch.edge("v", "V")
                    yield
                sch.wait("y", "V", sch.counts["V"])
                pe_c = {}
                g_c = {}
                GS = ["G", "G2", "G3", "G4"]

                def wd_pref(i2, pe_c=pe_c):
                    guard = pe_c.get(i2 - 4) if i2 >= 4 else DWN["last"]
                    if guard:
                        sch.wait("g", "P", guard)
                    gs = GS[i2 % 4]
                    sch.dma("g", gs, lambda e, h, i2=i2: e.dma_start(
                        out=wdTs[:, (i2 % 4) * D:(i2 % 4 + 1) * D],
                        in_=wdq_h[i2]).then_inc(h, 16))
                    return sch.counts[gs]
                for i2 in range(4):
                    g_c[i2] = wd_pref(i2)
                WV = [list(range(6 * w, min(6 * w + 6, IC))) for w in range(4)]

                def dn_tp(icb):
                    sch.dma("y", "X", lambda e, h, icb=icb: e.dma_start_transpose(
                        out=dn2[:, (icb % 12) * 128:(icb % 12 + 1) * 128],
                        in_=hqt[:, icb * 128:(icb + 1) * 128]).then_inc(h, 16))

                def issue_wave(w, guard):
                    if guard is not None:
                        sch.wait("y", "P", guard)
                    for icb in WV[w]:
                        dn_tp(icb)
                issue_wave(0, None)
                issue_wave(1, None)
                for w in range(4):
                    sch.wait("t", "X", sch.counts["X"])
                    for icb in WV[w]:
                        sch.wait("t", GS[icb % 4], g_c[icb])
                        for hh in range(2):
                            pq = pC if hh == 0 else pD
                            sch.emit("t", lambda t, icb=icb, hh=hh, pq=pq:
                                     t.matmul(
                                pq[:, 0:512],
                                dn2[:, (icb % 12) * 128:(icb % 12 + 1) * 128],
                                wdTs[:, (icb % 4) * D + hh * 512:
                                     (icb % 4) * D + (hh + 1) * 512],
                                start=(icb == 0), stop=(icb == IC - 1),
                                skip_group_check=True))
                        pe_c[icb] = sch.edge("t", "P")
                        if icb + 4 < IC:
                            g_c[icb + 4] = wd_pref(icb + 4)
                        if icb % 3 == 2:
                            yield
                    if w + 2 < 4:
                        issue_wave(w + 2, pe_c[WV[w][-1]])
                DWN["last"] = pe_c[IC - 1]
                c_p = pe_c[IC - 1]
                sch.wait("v", "P", c_p)

                def f_gb(v, tl=tl):
                    for hh in range(2):
                        pq = pC if hh == 0 else pD
                        v.tensor_tensor(
                            yt[:, tl * D + hh * 512: tl * D + (hh + 1) * 512],
                            pq[:, 0:512], gbt[:, hh * 512:(hh + 1) * 512],
                            OP.mult)
                    v.drain()
                sch.emit("v", f_gb)
                c_vg = sch.edge("v", "V")
                sch.wait("t", "V", c_vg)
                yield
            FFP[k] = sch.counts["P"]

            def f_out(v):
                v.tensor_tensor(xt, xt, yt[:], OP.add)
                v.drain()
            sch.emit("v", f_out)
            out_done = sch.edge("v", "V")
            sch.wait("y", "V", out_done)
            for pl in range(2):
                sch.dma("y", "O", lambda e, h, pl=pl, k=k: e.dma_start(
                    out=out_h[:, k * W + pl * 32: k * W + pl * 32 + 32, :],
                    in_=xt[:, pl * D:(pl + 1) * D]).then_inc(h, 16))

        for _ in pre_gen(0, fT[:]):
            pass
        xt0, h2t0 = slot(0)
        vcur0 = vtok[:, 0:2 * D]
        sch.emit("v", lambda v: v.tensor_copy(vcur0, h2t0))
        sch.emit("v", lambda v: v.drain())
        VRD[0] = v_ready0 = sch.edge("v", "V")
        sch.wait("y", "V", v_ready0)
        sch.wait("s", "V", v_ready0)
        xpre()
        xbar_tiles(vcur0, vT3)
        for _ in pre_gen(1, fT[:]):
            pass
        sch.emit("t", lambda t: t.matmul(
            pC[:, 0:512], sel_sb[:, 0:128], sel_sb[:, 0:512],
            start=True, stop=True, skip_group_check=True))
        emit_qp0(1)

        def pump(gen, n):
            for _ in range(n):
                try:
                    next(gen)
                except StopIteration:
                    return False
            return True

        for k in range(1, NCH):
            gens = [post_gen(k - 1)]
            if k + 1 < NCH:
                gens.append(pre_gen(k + 1))
            gen = _ichain(*gens)
            st = {"s_evac": {}, "v_col": {}}
            alive = True
            for c in range(W):
                emit_scan_step(k, c, st)
                if alive:
                    alive = pump(gen, 2)
            while alive:
                alive = pump(gen, 8)
            emit_vtail(k)
            if k + 1 < NCH:
                emit_qp0(k + 1)
        for _ in post_gen(NCH - 1):
            pass

        sch.wait("y", "O")
        block = es.enter_context(nc.Block())
        sch.play(block)
    return nc


_T_FULL = 2048


def kernel(L0_attn_feature, L0_attn_wq_w, L0_attn_wq_b, L0_attn_w2_w,
           L0_attn_w2_b, L0_attn_ln_g, L0_attn_ln_b, L0_ffn_w_gate,
           L0_ffn_w_up, L0_ffn_w_down, L0_c, L0_wmod):
    x = np.ascontiguousarray(np.asarray(L0_attn_feature, dtype=np.float32))
    consts = make_constants()
    shared = {
        "wq": np.ascontiguousarray(np.asarray(L0_attn_wq_w, np.float32)),
        "wqb": np.ascontiguousarray(np.asarray(L0_attn_wq_b, np.float32)),
        "w2": np.ascontiguousarray(np.asarray(L0_attn_w2_w, np.float32)),
        "lng": np.ascontiguousarray(np.asarray(L0_attn_ln_g, np.float32)),
        "lnb": np.ascontiguousarray(np.asarray(L0_attn_ln_b, np.float32)),
        "wg": np.ascontiguousarray(np.asarray(L0_ffn_w_gate, np.float32)),
        "wu": np.ascontiguousarray(np.asarray(L0_ffn_w_up, np.float32)),
        "wd": np.ascontiguousarray(np.asarray(L0_ffn_w_down, np.float32)),
        "wmod": np.ascontiguousarray(np.asarray(L0_wmod, np.float32)),
        "sel": consts["sel"],
        "b3init": consts["b3init"],
    }
    cv = np.ascontiguousarray(np.asarray(L0_c, np.float32))
    nc = build_program(_T_FULL)
    in_maps = []
    for core in range(N_CORES):
        m = dict(shared)
        m["x"] = np.ascontiguousarray(x[core * BPC:(core + 1) * BPC])
        m["cvec"] = np.ascontiguousarray(cv[core * BPC:(core + 1) * BPC])
        in_maps.append(m)
    res = run_bass_kernel_spmd(nc, in_maps, core_ids=list(range(N_CORES)))
    out = np.concatenate([np.asarray(r["out"]) for r in res.results], axis=0)
    return out.astype(np.float32)


# version 15
# speedup vs baseline: 1.1112x; 1.0836x over previous
import numpy as np
from itertools import chain as _ichain
import ml_dtypes
from contextlib import ExitStack

import concourse.bass as bass
import concourse.mybir as mybir
from concourse.bass_utils import run_bass_kernel_spmd

F32 = mybir.dt.float32
BF16 = mybir.dt.bfloat16
AT = mybir.ActivationFunctionType
OP = mybir.AluOpType
AX = mybir.AxisListType

B_TOT = 32
N_CORES = 8
BPC = 4
D = 1024
I_FF = 2816
W = 64
LN_EPS = 1e-5
DC = D // 128
IC = I_FF // 128
NBI = I_FF // 16

ENGS = ("v", "s", "t", "g", "y")


def _bf(x):
    return np.asarray(x, dtype=ml_dtypes.bfloat16)


def make_constants():
    sel = np.zeros((32, 128, 128), np.float32)
    for vv in range(32):
        for m in range(128):
            sel[vv, (m // 32) * 32 + vv, m] = 1.0
    b3i = np.zeros((128, 2 * W), np.float32)
    for p in range(128):
        for pl in range(2):
            b3i[p, pl * W + pl * 32 + (p % 32)] = 1.0
    return {"sel": _bf(sel.reshape(32 * 128, 128)),
            "b3init": _bf(b3i),
            "ident": _bf(np.eye(128, dtype=np.float32))}


class Sched:
    def __init__(self, nc):
        self.nc = nc
        self.ops = {e: [] for e in ENGS}
        self.sems = {}
        self.counts = {}

    def add_sem(self, es, name):
        h = es.enter_context(self.nc.semaphore(name))
        self.sems[name] = h
        self.counts[name] = 0

    def emit(self, eng, fn):
        self.ops[eng].append(fn)

    def tick(self, sem, n=1):
        self.counts[sem] += n
        return self.counts[sem]

    def edge(self, eng, sem, n=1):
        h = self.sems[sem]
        self.counts[sem] += n
        self.ops[eng].append(lambda e, h=h, n=n: e.drain().then_inc(h, n))
        return self.counts[sem]

    def dma(self, eng, sem, fn):
        h = self.sems[sem]
        self.counts[sem] += 16
        self.ops[eng].append(lambda e, h=h: fn(e, h))
        return self.counts[sem]

    def wait(self, eng, sem, count=None):
        h = self.sems[sem]
        c = self.counts[sem] if count is None else count
        if c > 0:
            self.ops[eng].append(lambda e, h=h, c=c: e.wait_ge(h, c))

    def barrier(self):
        names = {"v": "V", "s": "S", "t": "P", "g": "Q", "y": "Y"}
        for eng, sem in names.items():
            self.edge(eng, sem)
        snap = {sem: self.counts[sem] for sem in names.values()}
        for eng in names:
            for sem, c in snap.items():
                self.wait(eng, sem, c)

    def play(self, block):
        ops = self.ops

        @block.vector
        def _(v):
            for fn in ops["v"]:
                fn(v)

        @block.scalar
        def _(s):
            for fn in ops["s"]:
                fn(s)

        @block.tensor
        def _(t):
            for fn in ops["t"]:
                fn(t)

        @block.gpsimd
        def _(g):
            for fn in ops["g"]:
                fn(g)

        @block.sync
        def _(y):
            for fn in ops["y"]:
                fn(y)


def build_program(T):
    nc = bass.Bass("TRN2", target_bir_lowering=False)
    NCH = T // W
    assert T % W == 0 and NCH >= 2

    x_h = nc.dram_tensor("x", [BPC, T, D], F32, kind="ExternalInput").ap()
    wq_h = nc.dram_tensor("wq", [D, D], F32, kind="ExternalInput").ap()
    wqb_h = nc.dram_tensor("wqb", [D], F32, kind="ExternalInput").ap()
    w2_h = nc.dram_tensor("w2", [D], F32, kind="ExternalInput").ap()
    lng_h = nc.dram_tensor("lng", [D], F32, kind="ExternalInput").ap()
    lnb_h = nc.dram_tensor("lnb", [D], F32, kind="ExternalInput").ap()
    wg_h = nc.dram_tensor("wg", [I_FF, D], F32, kind="ExternalInput").ap()
    wu_h = nc.dram_tensor("wu", [I_FF, D], F32, kind="ExternalInput").ap()
    wd_h = nc.dram_tensor("wd", [D, I_FF], F32, kind="ExternalInput").ap()
    cvec_h = nc.dram_tensor("cvec", [BPC, D], F32, kind="ExternalInput").ap()
    wmod_h = nc.dram_tensor("wmod", [D, 6 * D], F32, kind="ExternalInput").ap()
    sel_h = nc.dram_tensor("sel", [32 * 128, 128], BF16, kind="ExternalInput").ap()
    b3i_h = nc.dram_tensor("b3init", [128, 2 * W], BF16, kind="ExternalInput").ap()
    id_h = nc.dram_tensor("ident", [128, 128], BF16, kind="ExternalInput").ap()
    out_h = nc.dram_tensor("out", [BPC, T, D], F32, kind="ExternalOutput").ap()
    wdq_h = nc.dram_tensor("wdq", [IC, 128, D], BF16, kind="Internal").ap()

    es = ExitStack()
    SB = lambda n, s, d=F32: es.enter_context(nc.sbuf_tensor(n, s, d))
    PS = lambda n, s, d=F32: es.enter_context(nc.psum_tensor(n, s, d))

    with es:
        QW = 1408
        wq_sb = SB("wq_sb", [128, DC * D], BF16)
        wgT = SB("wgT", [128, DC * I_FF], BF16)
        wuT = SB("wuT", [128, DC * I_FF], BF16)
        wdTs = SB("wdTs", [128, 4 * D], BF16)
        dn2 = SB("dn2", [128, 12 * 128], BF16)
        tstg = SB("tstg", [128, 2 * 128], BF16)
        wqb_f = SB("wqb_f", [128, DC], F32)
        w2_f = SB("w2_f", [128, DC], F32)
        lng_f = SB("lng_f", [128, DC], F32)
        lnb_f = SB("lnb_f", [128, DC], F32)
        modf = SB("modf", [128, 6 * BPC * DC], F32)
        lngga = SB("lngga", [128, BPC * DC], F32)
        lnbga = SB("lnbga", [128, BPC * DC], F32)
        silc = SB("silc", [128, BPC * DC], BF16)
        sel_sb = SB("sel_sb", [128, 32 * 128], BF16)
        b3i_sb = SB("b3i_sb", [128, 2 * W], BF16)
        id_sb = SB("id_sb", [128, 128], BF16)
        xtd = SB("xtd", [128, 2 * 2 * D], F32)
        fT = SB("fT", [128, DC * 256], BF16)
        stat = SB("stat", [128, 16], F32)
        wb3 = SB("wb3", [128, 4 * 128], BF16)
        eh = SB("eh", [128, W + 1], BF16)
        ebc = SB("ebc", [128, W + 1], F32)
        wnrm = SB("wnrm", [128, W], BF16)
        zac = SB("zac", [128, 4], F32)
        col32 = SB("col32", [128, 4], F32)
        vtok = SB("vtok", [128, 2 * 2 * D], BF16)
        vT = SB("vT", [128, DC * 256], BF16)
        yt = SB("yt", [128, 2 * D], BF16)
        h2d = SB("h2d", [128, 2 * 2 * D], BF16)
        h3 = SB("h3", [128, I_FF // 2], BF16)
        hqt = SB("hqt", [128, I_FF], BF16)

        qsc = SB("qsc", [128, QW // 16], F32)
        qrs = SB("qrs", [128, QW // 16], F32)
        qa = SB("qa", [128, QW], BF16)
        gbt = SB("gbt", [128, D], BF16)
        cat = SB("cat", [128, D], BF16)
        sat = SB("sat", [128, D], BF16)
        cbt = SB("cbt", [128, D], BF16)
        sbt = SB("sbt", [128, D], BF16)
        lgt = SB("lgt", [128, D], BF16)
        lbt = SB("lbt", [128, D], BF16)
        junk = SB("junk", [128, 4 * W], BF16)
        sgn = SB("sgn", [128, QW], BF16)
        ones1 = SB("ones1", [128, 4], F32)

        qT = yt[:, 0:2 * D]
        wmst = hqt[:, 0:1536]
        xq = yt[:]
        wnat = vtok[:, 0:I_FF]
        cstg = xtd[:, 0:BPC * DC]
        ms1 = xtd[:, 64:64 + BPC * DC]
        ms2 = xtd[:, 128:128 + BPC * DC]

        pA = PS("pA", [128, 512], F32)
        pB = PS("pB", [128, 512], F32)
        pC = PS("pC", [128, 512], F32)
        pD = PS("pD", [128, 512], F32)
        pbc = [PS("pbc0", [128, W + 1], F32), PS("pbc1", [128, W + 1], F32)]
        ptr = [PS("ptA", [128, 128], BF16), PS("ptB", [128, 128], BF16)]

        sch = Sched(nc)
        for sname in ("L", "G", "G2", "G3", "G4", "V", "S", "P", "Q", "X", "X2", "O", "Y"):
            sch.add_sem(es, sname)

        wb3v = wb3[:].rearrange("p (pl c) -> p pl c", pl=4)
        modv = modf[:].rearrange("p (j b dc) -> p j b dc", j=6, b=BPC)
        fT3 = fT[:].rearrange("p (dc t) -> p dc t", dc=DC)
        qT3 = qT.rearrange("p (dc t) -> p dc t", dc=DC)
        vT3 = vT[:].rearrange("p (dc t) -> p dc t", dc=DC)
        silc3 = silc[:].rearrange("p (b dc) -> p b dc", b=BPC)
        ms1v = ms1.rearrange("p (b dc) -> p b dc", b=BPC)
        ms2v = ms2.rearrange("p (b dc) -> p b dc", b=BPC)

        def ldvec(dst, src):
            sch.dma("y", "L", lambda e, h, dst=dst, src=src: e.dma_start(
                out=dst[:], in_=src.rearrange("(dc p) -> p dc", p=128),
                allow_slow_non_contiguous=True).then_inc(h, 16))

        ldvec(wqb_f, wqb_h)
        ldvec(w2_f, w2_h)
        ldvec(lng_f, lng_h)
        ldvec(lnb_f, lnb_h)
        sch.dma("y", "L", lambda e, h: e.dma_start(
            out=sel_sb[:].rearrange("p (v m) -> p v m", v=32),
            in_=sel_h.rearrange("(v p) m -> p v m", p=128)).then_inc(h, 16))
        sch.dma("y", "L", lambda e, h: e.dma_start(out=b3i_sb[:], in_=b3i_h).then_inc(h, 16))
        sch.dma("y", "L", lambda e, h: e.dma_start(out=id_sb[:], in_=id_h).then_inc(h, 16))
        sch.dma("y", "L", lambda e, h: e.dma_start(
            out=cstg.rearrange("p (b dc) -> p b dc", b=BPC),
            in_=cvec_h.rearrange("b (dc p) -> p b dc", p=128),
            allow_slow_non_contiguous=True).then_inc(h, 16))
        vec_ld = sch.counts["L"]
        sch.dma("g", "G", lambda e, h: e.dma_start(
            out=wq_sb[:].rearrange("p (dc o) -> p dc o", dc=DC),
            in_=wq_h.rearrange("(dc p) o -> p dc o", p=128)).then_inc(h, 16))
        wq_ld = sch.counts["G"]

        sch.wait("s", "L", vec_ld)
        sch.emit("s", lambda s: s.activation(silc[:], cstg, AT.Silu))
        silc_done = sch.edge("s", "S")

        first_mm = [True]
        for dc in range(DC):
            for qtr in range(4):
                sch.dma("g", "G", lambda e, h, dc=dc, qtr=qtr: e.dma_start(
                    out=wmst[:], in_=wmod_h[dc * 128:(dc + 1) * 128,
                                            qtr * 1536:(qtr + 1) * 1536]).then_inc(h, 16))
                sch.wait("t", "G")
                if dc == 0 and qtr == 0:
                    sch.wait("t", "S", silc_done)
                for mcl in range(12):
                    mc = qtr * 12 + mcl
                    sch.emit("t", lambda t, dc=dc, mc=mc, mcl=mcl, f=first_mm[0]:
                             t.matmul(
                        pA[:, mc * 4:(mc + 1) * 4],
                        wmst[:, mcl * 128:(mcl + 1) * 128],
                        silc3[:, :, dc],
                        start=f, stop=(dc == DC - 1),
                        skip_group_check=True))
                    first_mm[0] = False
                pe_c = sch.edge("t", "P")
                sch.wait("g", "P", pe_c)
        mod_mm = sch.edge("t", "P")

        sch.wait("v", "P", mod_mm)
        pAv = pA[:, 0:192].rearrange("p (j dc b) -> p j b dc", j=6, dc=DC)
        sch.emit("v", lambda v: v.tensor_copy(modv[:, :, :, :], pAv))
        sch.emit("v", lambda v: v.drain())

        def gprime(j):
            def f(v, j=j):
                m = modv[:, j]
                v.tensor_scalar(ms1v, m, 0.0, 2.0, OP.is_ge, OP.mult)
                v.drain()
                v.tensor_scalar(ms2v, ms1v, 1.0, None, OP.subtract)
                v.drain()
                v.tensor_scalar(ms1v, m, -1.0, None, OP.mult)
                v.drain()
                v.tensor_tensor(ms1v, ms1v, m, OP.max)
                v.drain()
                v.tensor_scalar(ms1v, ms1v, 0.25, None, OP.max)
                v.drain()
                v.tensor_tensor(m, ms1v, ms2v, OP.mult)
                v.drain()
            sch.emit("v", f)

        gprime(2)
        gprime(5)
        sch.emit("v", lambda v: v.tensor_scalar(modv[:, 1], modv[:, 1], 1.0, None, OP.add))
        sch.emit("v", lambda v: v.tensor_scalar(modv[:, 4], modv[:, 4], 1.0, None, OP.add))
        sch.emit("v", lambda v: v.drain())

        def f_lngga(v):
            for b in range(BPC):
                v.tensor_tensor(lngga[:, b * DC:(b + 1) * DC], lng_f[:], modv[:, 2, b], OP.mult)
                v.tensor_tensor(lnbga[:, b * DC:(b + 1) * DC], lnb_f[:], modv[:, 2, b], OP.mult)
            v.memset(ones1[:], 1.0)
            v.drain()
        sch.emit("v", f_lngga)
        sch.emit("v", lambda v: v.memset(eh[:, W:W + 1], 1.0))
        sch.emit("v", lambda v: v.drain())
        mod_done = sch.edge("v", "V")

        def emit_quant(src_ap, dst_ap, NC, scrB=None, scrB2=None):
            nb = NC // 16
            src3 = src_ap.rearrange("p (nb k) -> p nb k", k=16)
            sc3 = qsc[:, 0:nb].rearrange("p (nb o) -> p nb o", o=1)

            def v1(v):
                v.tensor_reduce(sc3, src3, AX.X, OP.max, apply_absolute_value=True)
                v.drain()
                v.tensor_scalar(qsc[:, 0:nb], qsc[:, 0:nb], 1.0 / 6.0, 1e-12,
                                OP.mult, OP.max)
                v.drain()
                v.reciprocal(qrs[:, 0:nb], qsc[:, 0:nb])
                v.drain()
            sch.emit("v", v1)

            def vsg(v):
                v.tensor_scalar(sgn[:, 0:NC], src_ap, 0.0, 2.0, OP.is_ge, OP.mult)
                v.drain()
                v.tensor_scalar(sgn[:, 0:NC], sgn[:, 0:NC], 1.0, None, OP.subtract)
                v.drain()
            sch.emit("v", vsg)
            yield

            def v2a(v):
                rsb = qrs[:, 0:nb].rearrange("p (nb o) -> p nb o", o=1).broadcast_to([128, nb, 16])
                a13 = qa[:, 0:NC].rearrange("p (nb k) -> p nb k", k=16)
                v.tensor_tensor(a13, src3, rsb, OP.mult)
                v.drain()
                v.tensor_tensor(qa[:, 0:NC], qa[:, 0:NC], sgn[:, 0:NC], OP.mult)
                v.drain()
            sch.emit("v", v2a)
            yield
            a = dst_ap
            bb = [scrB[:, 0:NC], scrB2[:, 0:NC]]
            vv = qa[:, 0:NC]
            steps = [(1.25, 0.5), (1.75, 0.5), (2.5, 1.0), (3.5, 1.0), (5.0, 2.0)]

            def v2b(v):
                v.tensor_scalar(a, vv, 0.25, 0.5, OP.is_gt, OP.mult)
                v.tensor_scalar(bb[0], vv, 0.75, 0.5, OP.is_gt, OP.mult)
                v.drain()
                for i, (thr, wgt) in list(enumerate(steps))[:2]:
                    v.tensor_tensor(a, a, bb[i % 2], OP.add)
                    v.tensor_scalar(bb[(i + 1) % 2], vv, thr, wgt, OP.is_gt, OP.mult)
                    v.drain()
            sch.emit("v", v2b)
            yield

            def v2c(v):
                for i, (thr, wgt) in list(enumerate(steps))[2:]:
                    v.tensor_tensor(a, a, bb[i % 2], OP.add)
                    v.tensor_scalar(bb[(i + 1) % 2], vv, thr, wgt, OP.is_gt, OP.mult)
                    v.drain()
                v.tensor_tensor(a, a, bb[len(steps) % 2], OP.add)
                v.drain()
            sch.emit("v", v2c)
            yield

            def v2d(v):
                v.tensor_tensor(a, a, sgn[:, 0:NC], OP.mult)
                v.drain()
                scb = qsc[:, 0:nb].rearrange("p (nb o) -> p nb o", o=1).broadcast_to([128, nb, 16])
                dst3 = dst_ap.rearrange("p (nb k) -> p nb k", k=16)
                v.tensor_tensor(dst3, dst3, scb, OP.mult)
                v.drain()
            sch.emit("v", v2d)

        sch.wait("g", "V", mod_done)

        wnat2 = h2d[:, 0:I_FF]

        def weight_pipe(src_h, nslab, ncols, dst, dst_col, to_dram=False):
            bufs = [wnat, wnat2]
            cq_hist = {}

            def slab_load(sl):
                gs = "G" if sl % 2 == 0 else "G2"
                if sl >= 2:
                    sch.wait("g", "V", cq_hist[sl - 2])
                sch.dma("g", gs, lambda e, h, sl=sl: e.dma_start(
                    out=bufs[sl % 2][:, 0:ncols],
                    in_=src_h[sl * 128:(sl + 1) * 128, :]).then_inc(h, 16))
                return sch.counts[gs]
            gtot = {0: slab_load(0)}
            for sl in range(nslab):
                if sl + 1 < nslab:
                    gtot[sl + 1] = slab_load(sl + 1)
                sch.wait("v", "G" if sl % 2 == 0 else "G2", gtot[sl])
                sch.wait("s", "G" if sl % 2 == 0 else "G2", gtot[sl])
                for q0 in range(0, ncols, QW):
                    qw = min(QW, ncols - q0)
                    for _ in emit_quant(bufs[sl % 2][:, q0:q0 + qw],
                                        hqt[:, q0:q0 + qw], qw, h3, yt):
                        pass
                cq_hist[sl] = sch.counts["V"] + 1
                cq = sch.edge("v", "V")
                assert cq == cq_hist[sl]
                sch.wait("y", "V", cq)
                sch.wait("s", "V", cq)
                if to_dram:
                    for cb in range(ncols // 128):
                        if cb >= 2:
                            sch.wait("y", "O", sch.counts["O"])
                        sch.dma("y", "X", lambda e, h, sl=sl, cb=cb:
                                e.dma_start_transpose(
                            out=tstg[:, (cb % 2) * 128:(cb % 2 + 1) * 128],
                            in_=hqt[:, cb * 128:(cb + 1) * 128]).then_inc(h, 16))
                        sch.wait("y", "X", sch.counts["X"])
                        sch.dma("y", "O", lambda e, h, sl=sl, cb=cb: e.dma_start(
                            out=wdq_h[cb, :, sl * 128:(sl + 1) * 128],
                            in_=tstg[:, (cb % 2) * 128:(cb % 2 + 1) * 128]
                        ).then_inc(h, 16))
                    cx, co = sch.counts["X"], sch.counts["O"]
                    for e in ("v", "s", "g", "y"):
                        sch.wait(e, "X", cx)
                    sch.wait("y", "O", co)
                else:
                    for cb in range(ncols // 128):
                        eng, sem = ("y", "X") if cb % 2 == 0 else ("s", "X2")
                        sch.dma(eng, sem, lambda e, h, sl=sl, cb=cb:
                                e.dma_start_transpose(
                            out=dst[:, dst_col(cb, sl):dst_col(cb, sl) + 128],
                            in_=hqt[:, cb * 128:(cb + 1) * 128]).then_inc(h, 16))
                    cx, cx2 = sch.counts["X"], sch.counts["X2"]
                    for e in ("v", "s", "g", "y"):
                        sch.wait(e, "X", cx)
                        sch.wait(e, "X2", cx2)

        weight_pipe(wg_h, IC, D, wgT, lambda cb, sl: cb * I_FF + sl * 128)
        weight_pipe(wu_h, IC, D, wuT, lambda cb, sl: cb * I_FF + sl * 128)
        weight_pipe(wd_h, DC, I_FF, None, None, to_dram=True)
        sch.wait("t", "G")
        sch.barrier()

        def emit_rstd(nsrc, ndst):
            vs = stat[:, nsrc:nsrc + 2]
            t_ = stat[:, 4:6]
            r_ = stat[:, 6:8]
            u_ = stat[:, 8:10]
            w_ = stat[:, 10:12]
            y_ = stat[:, ndst:ndst + 2]

            def f1(v):
                v.tensor_scalar(t_, vs, 1.0 / D, LN_EPS, OP.mult, OP.add)
                v.drain()
                v.reciprocal(r_, t_)
                v.drain()
            sch.emit("v", f1)
            c_v = sch.edge("v", "V")
            sch.wait("g", "V", c_v)

            def f2(g):
                g.tensor_scalar(u_, t_, 2.0, None, OP.is_gt)
                g.drain()
                g.tensor_scalar(w_, u_, 0.58, 0.42, OP.mult, OP.add)
                g.tensor_scalar(u_, u_, -0.395, 0.62, OP.mult, OP.add)
                g.drain()
                g.tensor_tensor(u_, u_, t_, OP.mult)
                g.drain()
                g.tensor_tensor(u_, u_, w_, OP.add)
                g.drain()
                g.tensor_tensor(y_, r_, u_, OP.mult)
                g.drain()
                for _ in range(3):
                    g.tensor_tensor(u_, y_, y_, OP.mult)
                    g.drain()
                    g.tensor_tensor(u_, u_, t_, OP.mult)
                    g.drain()
                    g.tensor_scalar(u_, u_, -0.5, 1.5, OP.mult, OP.add)
                    g.drain()
                    g.tensor_tensor(y_, y_, u_, OP.mult)
                    g.drain()
            sch.emit("g", f2)
            c_g = sch.edge("g", "Q")
            sch.wait("v", "Q", c_g)

        def xpre():
            sch.wait("y", "X")
            sch.wait("y", "X2")

        def xw(eng, cp=None):
            cx, cx2 = cp if cp else (sch.counts["X"], sch.counts["X2"])
            sch.wait(eng, "X", cx)
            sch.wait(eng, "X2", cx2)

        def xbar_tiles(src_tok, dstT3):
            for pl in range(2):
                for dc in range(DC):
                    sch.dma("y", "X", lambda e, h, pl=pl, dc=dc: e.dma_start_transpose(
                        out=dstT3[:, dc, pl * 128:(pl + 1) * 128],
                        in_=src_tok[:, pl * D + dc * 128: pl * D + (dc + 1) * 128]
                    ).then_inc(h, 16))
            return (sch.counts["X"], sch.counts["X2"])

        def xbar_back(srcT3, dst_tok):
            for pl in range(2):
                for dc in range(DC):
                    sch.dma("y", "X", lambda e, h, pl=pl, dc=dc: e.dma_start_transpose(
                        out=dst_tok[:, pl * D + dc * 128: pl * D + (dc + 1) * 128],
                        in_=srcT3[:, dc, pl * 128:(pl + 1) * 128]).then_inc(h, 16))
            return (sch.counts["X"], sch.counts["X2"])

        PTS = {"s2": [0, 0]}

        def _pe_xbar_one(src_tok, dstT3, pl, dc, i):
            sch.wait("t", "S", PTS["s2"][i % 2])
            cp = sch.tick("P")
            sch.emit("t", lambda t, pl=pl, dc=dc, i=i, cp=cp: t.transpose(
                ptr[i % 2][:],
                src_tok[:, pl * D + dc * 128: pl * D + (dc + 1) * 128],
                id_sb[:]).then_inc(sch.sems["P"], 1))
            sch.wait("s", "P", cp)
            cs = sch.tick("S")
            sch.emit("s", lambda s, pl=pl, dc=dc, i=i, cs=cs: s.activation(
                dstT3[:, dc, pl * 128:(pl + 1) * 128], ptr[i % 2][:],
                AT.Copy).then_inc(sch.sems["S"], 1))
            PTS["s2"][i % 2] = cs
            return cs

        def pe_xbar(src_tok, dstT3):
            i = 0
            cs = 0
            for pl in range(2):
                for dc in range(DC):
                    cs = _pe_xbar_one(src_tok, dstT3, pl, dc, i)
                    i += 1
                    if i % 4 == 0:
                        yield
            PTS["last"] = cs

        def pe_xbar_now(src_tok, dstT3):
            i = 0
            cs = 0
            for pl in range(2):
                for dc in range(DC):
                    cs = _pe_xbar_one(src_tok, dstT3, pl, dc, i)
                    i += 1
            PTS["last"] = cs
            return cs

        def _mod_bcast(j):
            a = modv[:, j].transpose([0, 2, 1]).unsqueeze(3)
            return a.broadcast_to([128, DC, BPC, 32])

        def feat_affine(T3, j_scale, j_shift):
            def f(v):
                for pl in range(2):
                    T5 = T3.rearrange("p dc (pl b q) -> p dc pl b q",
                                      pl=2, b=BPC)[:, :, pl]
                    v.tensor_tensor(T5, T5, _mod_bcast(j_scale), OP.mult)
                v.drain()
                if j_shift is not None:
                    for pl in range(2):
                        T5 = T3.rearrange("p dc (pl b q) -> p dc pl b q",
                                          pl=2, b=BPC)[:, :, pl]
                        v.tensor_tensor(T5, T5, _mod_bcast(j_shift), OP.add)
                    v.drain()
            sch.emit("v", f)

        def emit_ln(src_ap, dst_bf_ap, musl, varsl, rstdsl, sq_scratch):
            s2 = src_ap.rearrange("p (pl d) -> p pl d", pl=2)

            def f0(v):
                v.tensor_reduce(stat[:, musl:musl + 2].unsqueeze(2), s2, AX.X, OP.add)
                v.drain()
                v.tensor_scalar(stat[:, musl:musl + 2], stat[:, musl:musl + 2],
                                1.0 / D, None, OP.mult)
                v.drain()
            sch.emit("v", f0)
            yield
            for pl in range(2):
                s_ = src_ap[:, pl * D:(pl + 1) * D]
                d_ = dst_bf_ap[:, pl * D:(pl + 1) * D]

                def f(v, s_=s_, d_=d_, pl=pl):
                    v.tensor_scalar(d_, s_, stat[:, musl + pl:musl + pl + 1], None,
                                    OP.subtract)
                    v.drain()
                    v.scalar_tensor_tensor(sq_scratch[:, pl * D:(pl + 1) * D], d_, 1.0,
                                           d_, OP.mult, OP.mult,
                                           accum_out=stat[:, varsl + pl:varsl + pl + 1])
                    v.drain()
                sch.emit("v", f)
                yield
            emit_rstd(varsl, rstdsl)
            yield

            def fn(v):
                for pl in range(2):
                    v.tensor_scalar(dst_bf_ap[:, pl * D:(pl + 1) * D],
                                    dst_bf_ap[:, pl * D:(pl + 1) * D],
                                    stat[:, rstdsl + pl:rstdsl + pl + 1], None, OP.mult)
                v.drain()
            sch.emit("v", fn)

        def feat_affine_ln(T3, vec_a, vec_b):
            def bc(vec):
                a = vec[:].rearrange("p (b dc) -> p b dc", b=BPC).transpose(
                    [0, 2, 1]).unsqueeze(3)
                return a.broadcast_to([128, DC, BPC, 32])

            def f(v):
                for op, vec in ((OP.mult, vec_a), (OP.add, vec_b)):
                    for pl in range(2):
                        T5 = T3.rearrange("p dc (pl b q) -> p dc pl b q",
                                          pl=2, b=BPC)[:, :, pl]
                        v.tensor_tensor(T5, T5, bc(vec), op)
                    v.drain()
            sch.emit("v", f)

        def _vec_bcast(vec):
            a = vec[:].rearrange("p (b dc) -> p b dc", b=BPC).transpose(
                [0, 2, 1]).unsqueeze(3)
            return a.broadcast_to([128, DC, BPC, 32])

        for bsrc, dst in ((_mod_bcast(5), gbt), (_mod_bcast(1), cat),
                          (_mod_bcast(0), sat), (_mod_bcast(4), cbt),
                          (_mod_bcast(3), sbt), (_vec_bcast(lngga), lgt),
                          (_vec_bcast(lnbga), lbt)):
            def f_gfm(v, bsrc=bsrc):
                v.memset(fT[:], 1.0)
                v.drain()
                T5 = fT3.rearrange("p dc (pl b q) -> p dc pl b q",
                                   pl=2, b=BPC)[:, :, 0]
                v.tensor_tensor(T5, T5, bsrc, OP.mult)
                v.drain()
            sch.emit("v", f_gfm)
            c_v = sch.edge("v", "V")
            sch.wait("y", "V", c_v)
            xpre()
            for dc in range(DC):
                sch.dma("y", "X", lambda e, h, dc=dc, dst=dst:
                        e.dma_start_transpose(
                    out=dst[:, dc * 128:(dc + 1) * 128],
                    in_=fT3[:, dc, 0:128]).then_inc(h, 16))
            xw("v")

        def tok_affine(dst, srcv, mtile, atile):
            d3 = dst.rearrange("p (pl f) -> p pl f", pl=2)
            s3 = srcv.rearrange("p (pl f) -> p pl f", pl=2)
            mb = mtile[:].unsqueeze(1).broadcast_to([128, 2, D])
            ab = atile[:].unsqueeze(1).broadcast_to([128, 2, D])

            def f1(v):
                v.tensor_tensor(d3, s3, mb, OP.mult)
                v.drain()
            sch.emit("v", f1)
            yield

            def f2(v):
                v.tensor_tensor(d3, d3, ab, OP.add)
                v.drain()
            sch.emit("v", f2)

        IHALF = [[(0, 512), (512, 1024), (1024, 1408)],
                 [(1408, 1920), (1920, 2432), (2432, 2816)]]
        P0D = {}
        SCI = {}
        VRD = {}
        FFP = {}
        QMP = {}
        DWN = {"last": None}

        def slot(k):
            par = k % 2
            return (xtd[:, par * 2 * D:(par + 1) * 2 * D],
                    h2d[:, par * 2 * D:(par + 1) * 2 * D])

        def pre_gen(k, lnscr=None):
            if lnscr is None:
                lnscr = vT[:]
            xt, h2t = slot(k)
            sch.wait("s", "V")
            sch.wait("s", "O")
            for pl in range(2):
                sch.dma("s", "L", lambda e, h, pl=pl, k=k: e.dma_start(
                    out=xt[:, pl * D:(pl + 1) * D],
                    in_=x_h[:, k * W + pl * 32: k * W + pl * 32 + 32, :]
                ).then_inc(h, 16))
            sch.wait("v", "L")
            yield
            yield from emit_ln(xt, yt[:], 0, 2, 12, lnscr)
            yield
            yield from tok_affine(h2t, yt[:], cat, sat)
            c_hT = sch.edge("v", "V")
            yield
            if k >= 1:
                sch.wait("t", "V", c_hT)
                if (k - 2) in FFP:
                    sch.wait("s", "P", FFP[k - 2])
                yield from pe_xbar(h2t, fT3)
                sch.wait("t", "S", PTS["last"])
                yield
            if k >= 1:
                for oc in range(DC):
                    pq = pA if oc % 2 == 0 else pB
                    for dc in range(DC):
                        sch.emit("t", lambda t, oc=oc, dc=dc, pq=pq: t.matmul(
                            pq[:, 0:256],
                            wq_sb[:, dc * D + oc * 128: dc * D + (oc + 1) * 128],
                            fT3[:, dc, :],
                            start=(dc == 0), stop=(dc == DC - 1),
                            skip_group_check=True))
                    c_p = sch.edge("t", "P")
                    sch.wait("v", "P", c_p)

                    def fq(v, oc=oc, pq=pq):
                        v.tensor_scalar(qT3[:, oc, :], pq[:, 0:256],
                                        wqb_f[:, oc:oc + 1], w2_f[:, oc:oc + 1],
                                        OP.add, OP.mult)
                        v.drain()
                    sch.emit("v", fq)
                    c_vq = sch.edge("v", "V")
                    sch.wait("t", "V", c_vq)
                    if oc % 2 == 1:
                        yield
                QMP[k] = sch.counts["P"]

        def emit_qp0(k):
            sch.wait("t", "S", PTS.get("last", 0))
            sch.wait("t", "V")
            vT4 = vT3[:, :, :].rearrange("p dc (pl q) -> p dc pl q", pl=2)
            first = True
            for b in range(BPC):
                for plr in range(2):
                    for dc in range(DC):
                        sch.emit("t", lambda t, b=b, plr=plr, dc=dc, first=first:
                                 t.matmul(
                            pC[b * 32:(b + 1) * 32, plr * 64: plr * 64 + 64],
                            qT3[:, dc, plr * 128 + b * 32: plr * 128 + b * 32 + 32],
                            vT4[:, dc, :, b * 32:(b + 1) * 32],
                            start=(dc == 0 and first), stop=(dc == DC - 1),
                            skip_group_check=True, tile_position=(0, b * 32)))
                    first = False
            P0D[k] = p0_done = sch.edge("t", "P")
            sch.wait("v", "P", p0_done)

            def f_p0(v):
                v.tensor_copy(wb3v[:, 0:2, 0:W], pC[:, 0:128].rearrange(
                    "p (pl c) -> p pl c", pl=2))
                v.tensor_copy(wb3v[:, 2:4, 0:W], b3i_sb[:].rearrange(
                    "p (pl c) -> p pl c", pl=2))
                v.drain()
            sch.emit("v", f_p0)
            SCI[k] = sch.edge("v", "V")

        def emit_scan_step(k, c, st):
            s_evac = st["s_evac"]
            v_col = st["v_col"]
            plc = c // 32
            pl0 = 0 if c < 32 else 1
            npl = 4 - pl0
            sch.wait("s", "V", v_col.get(c - 1, SCI[k]))
            cs = sch.tick("S")
            sch.emit("s", lambda s, plc=plc, c=c, cs=cs: s.activation(
                eh[:, 0:W], wb3v[:, plc, c:c + W], AT.Exp).then_inc(
                sch.sems["S"], 1))
            sch.wait("t", "S", cs)
            if c >= 2:
                sch.wait("t", "V", s_evac[c - 2])
            cp = sch.tick("P")
            sch.emit("t", lambda t, c=c: t.matmul(
                pbc[c % 2][:], sel_sb[:, (c % 32) * 128:(c % 32 + 1) * 128],
                eh[:], start=True, stop=True).then_inc(sch.sems["P"], 1))
            sch.wait("v", "P", cp)
            ce1 = sch.tick("V")
            sch.emit("v", lambda v, c=c: v.tensor_scalar(
                ebc[:], pbc[c % 2][:], 1.0, 0.0, OP.mult, OP.add,
                accum_out=zac[:, 0:1]).then_inc(sch.sems["V"], 1))
            s_evac[c] = ce1
            sch.wait("v", "V", ce1)
            ce2 = sch.tick("V")
            sch.emit("v", lambda v: v.reciprocal(
                zac[:, 2:3], zac[:, 0:1]).then_inc(sch.sems["V"], 1))
            if c >= 1:
                sch.wait("v", "V", v_col[c - 1])
            ce3 = sch.tick("V")

            def f_tt(v, c=c, pl0=pl0, npl=npl):
                wn = ebc[:, 0:W].rearrange("p (o k) -> p o k", o=1
                                           ).broadcast_to([128, npl, W])
                jv = junk[:].rearrange("p (pl k) -> p pl k", pl=4)[:, pl0:4]
                v.tensor_tensor(jv, wb3v[:, pl0:4, c:c + W], wn,
                                OP.mult).then_inc(sch.sems["V"], 1)
            sch.emit("v", f_tt)
            sch.wait("v", "V", ce3)
            ce4 = sch.tick("V")

            def f_rd(v, pl0=pl0, npl=npl):
                jv = junk[:].rearrange("p (pl k) -> p pl k", pl=4)[:, pl0:4]
                v.tensor_reduce(col32[:, pl0:4].unsqueeze(2), jv,
                                AX.X, OP.add).then_inc(sch.sems["V"], 1)
            sch.emit("v", f_rd)
            sch.wait("v", "V", ce2)
            sch.wait("v", "V", ce4)
            ce5 = sch.tick("V")

            def f_sc(v, c=c, pl0=pl0):
                v.tensor_scalar(wb3v[:, pl0:4, W + c:W + c + 1],
                                col32[:, pl0:4].unsqueeze(2),
                                zac[:, 2:3], None, OP.mult).then_inc(
                    sch.sems["V"], 1)
            sch.emit("v", f_sc)
            v_col[c] = ce5

        def emit_vtail(k):
            par = k % 2
            vcur = vtok[:, par * 2 * D:(par + 1) * 2 * D]
            vprev = vtok[:, (1 - par) * 2 * D:(2 - par) * 2 * D]
            vprev3 = vprev.rearrange("p (pl d) -> p pl d", pl=2)
            sch.wait("t", "V")
            pmap = {(0, 0): pA, (0, 1): pB, (1, 0): pC, (1, 1): pD}
            for b in range(BPC):
                for dh in range(2):
                    for plo in range(2):
                        pq = pmap[(dh, plo)]
                        for pli in range(2):
                            sch.emit("t", lambda t, b=b, dh=dh, plo=plo, pli=pli,
                                     pq=pq: t.matmul(
                                pq[b * 32:(b + 1) * 32, 0:512],
                                wb3v[b * 32:(b + 1) * 32, 2 + pli,
                                     W + plo * 32: W + plo * 32 + 32],
                                vprev3[b * 32:(b + 1) * 32, pli,
                                       dh * 512:(dh + 1) * 512],
                                start=(pli == 0), stop=(pli == 1),
                                skip_group_check=True,
                                tile_position=(b * 32, b * 32)))
            vmm_done = sch.edge("t", "P")
            sch.wait("v", "P", vmm_done)

            def f_vev(v, vcur=vcur):
                for dh in range(2):
                    for plo in range(2):
                        v.tensor_copy(
                            vcur[:, plo * D + dh * 512: plo * D + (dh + 1) * 512],
                            pmap[(dh, plo)][:])
                v.drain()
            sch.emit("v", f_vev)
            VRD[k] = v_ready = sch.edge("v", "V")
            if k + 1 < NCH:
                sch.wait("t", "V", v_ready)
                sch.wait("s", "V", v_ready)
                if k in P0D:
                    sch.wait("s", "P", P0D[k])
                pe_xbar_now(vcur, vT3)

        def post_gen(k):
            par = k % 2
            xt, h2t = slot(k)
            vcur = vtok[:, par * 2 * D:(par + 1) * 2 * D]
            vprev = vtok[:, (1 - par) * 2 * D:(2 - par) * 2 * D]
            sch.wait("s", "V", VRD[k])
            if (k + 1) in P0D:
                sch.wait("s", "P", P0D[k + 1])
            sch.emit("s", lambda s, vcur=vcur: s.activation(yt[:], vcur, AT.Tanh))
            tanh_done = sch.edge("s", "S")
            sch.wait("v", "S", tanh_done)

            def f_y1(v):
                v.tensor_scalar(yt[:], yt[:], 1.0, None, OP.add)
                v.drain()
            sch.emit("v", f_y1)
            yield

            def f_y2(v):
                v.tensor_tensor(yt[:], yt[:], h2t, OP.mult)
                v.drain()
            sch.emit("v", f_y2)
            yield
            yield from emit_ln(yt[:], h2t, 0, 2, 12, vT[:])
            yield
            yield from tok_affine(yt[:], h2t, lgt, lbt)
            yield

            def f_res(v):
                v.tensor_tensor(xt, xt, yt[:], OP.add)
                v.drain()
            sch.emit("v", f_res)
            yield from emit_ln(xt, h2t, 0, 2, 12, vT[:])
            yield
            yield from tok_affine(h2t, h2t, cbt, sbt)
            yield
            for q0 in range(0, 2 * D, QW):
                qw = min(QW, 2 * D - q0)
                yield from emit_quant(h2t[:, q0:q0 + qw], xq[:, q0:q0 + qw],
                                      qw, h3, hqt)
                yield
            c_v = sch.edge("v", "V")
            sch.wait("t", "V", c_v)
            if (k + 1) in QMP:
                sch.wait("s", "P", QMP[k + 1])
            yield from pe_xbar(xq, fT3)
            sch.wait("t", "S", PTS["last"])
            yield
            for tl in range(2):
                for half in range(2):
                    sch.wait("s", "V")
                    ICH3 = IHALF[half]

                    def gu_mms(j, dcs, edge=False, tl=tl, ICH3=ICH3):
                        i0, i1 = ICH3[j]
                        iw = i1 - i0
                        pg = pA if j % 2 == 0 else pB
                        pu = pC if j % 2 == 0 else pD
                        for dc in dcs:
                            sch.emit("t", lambda t, tl=tl, i0=i0, iw=iw,
                                     dc=dc, pg=pg: t.matmul(
                                pg[:, 0:iw],
                                fT3[:, dc, tl * 128:(tl + 1) * 128],
                                wgT[:, dc * I_FF + i0: dc * I_FF + i0 + iw],
                                start=(dc == 0), stop=(dc == DC - 1),
                                skip_group_check=True))
                            sch.emit("t", lambda t, tl=tl, i0=i0, iw=iw,
                                     dc=dc, pu=pu: t.matmul(
                                pu[:, 0:iw],
                                fT3[:, dc, tl * 128:(tl + 1) * 128],
                                wuT[:, dc * I_FF + i0: dc * I_FF + i0 + iw],
                                start=(dc == 0), stop=(dc == DC - 1),
                                skip_group_check=True))
                        if edge:
                            return sch.edge("t", "P")
                        return None

                    c_p = {}
                    c_vs = {}
                    gu_mms(0, range(0, 4))
                    yield
                    c_p[0] = gu_mms(0, range(4, DC), edge=True)
                    yield
                    for j in range(len(ICH3)):
                        if j + 1 < len(ICH3):
                            if j >= 1:
                                sch.wait("t", "V", c_vs[j - 1])
                            gu_mms(j + 1, range(0, 4))
                            yield
                            c_p[j + 1] = gu_mms(j + 1, range(4, DC), edge=True)
                        i0, i1 = ICH3[j]
                        iw = i1 - i0
                        hoff = half * 1408
                        pg = pA if j % 2 == 0 else pB
                        pu = pC if j % 2 == 0 else pD
                        so = (j % 2) * 512
                        sch.wait("s", "P", c_p[j])
                        sch.emit("s", lambda s, iw=iw, pg=pg, so=so:
                                 s.activation(h2t[:, so:so + iw],
                                              pg[:, 0:iw], AT.Tanh,
                                              scale=0.5))
                        sch.emit("s", lambda s: s.drain())
                        sch.emit("s", lambda s, iw=iw, so=so: s.activation(
                            h2t[:, so:so + iw], h2t[:, so:so + iw], AT.Copy,
                            bias=0.5, scale=0.5))
                        c_s = sch.edge("s", "S")
                        sch.wait("v", "S", c_s)

                        def f_su(v, i0=i0, iw=iw, hoff=hoff, pg=pg, pu=pu,
                                 so=so, h2t=h2t):
                            v.tensor_tensor(h2t[:, D + so:D + so + iw],
                                            pg[:, 0:iw],
                                            h2t[:, so:so + iw], OP.mult)
                            v.drain()
                            v.tensor_tensor(h3[:, i0 - hoff:i0 - hoff + iw],
                                            h2t[:, D + so:D + so + iw],
                                            pu[:, 0:iw], OP.mult)
                            v.drain()
                        sch.emit("v", f_su)
                        c_vs[j] = sch.edge("v", "V")
                        yield
                    sch.wait("t", "V", c_vs[len(ICH3) - 1])
                    c_h3 = sch.edge("v", "V")
                    sch.wait("s", "V", c_h3)
                    sch.wait("v", "S")
                    for q0 in range(0, 1408, QW):
                        yield from emit_quant(h3[:, q0:q0 + QW],
                                              hqt[:, half * 1408 + q0:
                                                  half * 1408 + q0 + QW], QW,
                                              h2t, vprev)
                    hq_done = sch.edge("v", "V")
                    yield
                sch.wait("y", "V", sch.counts["V"])
                pe_c = {}
                g_c = {}
                GS = ["G", "G2", "G3", "G4"]

                def wd_pref(i2, pe_c=pe_c):
                    guard = pe_c.get(i2 - 4) if i2 >= 4 else DWN["last"]
                    if guard:
                        sch.wait("g", "P", guard)
                    gs = GS[i2 % 4]
                    sch.dma("g", gs, lambda e, h, i2=i2: e.dma_start(
                        out=wdTs[:, (i2 % 4) * D:(i2 % 4 + 1) * D],
                        in_=wdq_h[i2]).then_inc(h, 16))
                    return sch.counts[gs]
                for i2 in range(4):
                    g_c[i2] = wd_pref(i2)
                WV = [list(range(6 * w, min(6 * w + 6, IC))) for w in range(4)]

                def dn_tp(icb):
                    sch.dma("y", "X", lambda e, h, icb=icb: e.dma_start_transpose(
                        out=dn2[:, (icb % 12) * 128:(icb % 12 + 1) * 128],
                        in_=hqt[:, icb * 128:(icb + 1) * 128]).then_inc(h, 16))

                def issue_wave(w, guard):
                    if guard is not None:
                        sch.wait("y", "P", guard)
                    for icb in WV[w]:
                        dn_tp(icb)
                issue_wave(0, None)
                issue_wave(1, None)
                for w in range(4):
                    sch.wait("t", "X", sch.counts["X"])
                    for icb in WV[w]:
                        sch.wait("t", GS[icb % 4], g_c[icb])
                        for hh in range(2):
                            pq = pC if hh == 0 else pD
                            sch.emit("t", lambda t, icb=icb, hh=hh, pq=pq:
                                     t.matmul(
                                pq[:, 0:512],
                                dn2[:, (icb % 12) * 128:(icb % 12 + 1) * 128],
                                wdTs[:, (icb % 4) * D + hh * 512:
                                     (icb % 4) * D + (hh + 1) * 512],
                                start=(icb == 0), stop=(icb == IC - 1),
                                skip_group_check=True))
                        pe_c[icb] = sch.edge("t", "P")
                        if icb + 4 < IC:
                            g_c[icb + 4] = wd_pref(icb + 4)
                        if icb % 3 == 2:
                            yield
                    if w + 2 < 4:
                        issue_wave(w + 2, pe_c[WV[w][-1]])
                DWN["last"] = pe_c[IC - 1]
                c_p = pe_c[IC - 1]
                sch.wait("v", "P", c_p)

                def f_gb(v, tl=tl):
                    for hh in range(2):
                        pq = pC if hh == 0 else pD
                        v.tensor_tensor(
                            yt[:, tl * D + hh * 512: tl * D + (hh + 1) * 512],
                            pq[:, 0:512], gbt[:, hh * 512:(hh + 1) * 512],
                            OP.mult)
                    v.drain()
                sch.emit("v", f_gb)
                c_vg = sch.edge("v", "V")
                sch.wait("t", "V", c_vg)
                yield
            FFP[k] = sch.counts["P"]

            def f_out(v):
                v.tensor_tensor(xt, xt, yt[:], OP.add)
                v.drain()
            sch.emit("v", f_out)
            out_done = sch.edge("v", "V")
            sch.wait("y", "V", out_done)
            for pl in range(2):
                sch.dma("y", "O", lambda e, h, pl=pl, k=k: e.dma_start(
                    out=out_h[:, k * W + pl * 32: k * W + pl * 32 + 32, :],
                    in_=xt[:, pl * D:(pl + 1) * D]).then_inc(h, 16))

        for _ in pre_gen(0, fT[:]):
            pass
        xt0, h2t0 = slot(0)
        vcur0 = vtok[:, 0:2 * D]
        sch.emit("v", lambda v: v.tensor_copy(vcur0, h2t0))
        sch.emit("v", lambda v: v.drain())
        VRD[0] = v_ready0 = sch.edge("v", "V")
        sch.wait("t", "V", v_ready0)
        sch.wait("s", "V", v_ready0)
        pe_xbar_now(vcur0, vT3)
        for _ in pre_gen(1, fT[:]):
            pass
        sch.emit("t", lambda t: t.matmul(
            pC[:, 0:512], sel_sb[:, 0:128], sel_sb[:, 0:512],
            start=True, stop=True, skip_group_check=True))
        emit_qp0(1)

        def pump(gen, n):
            for _ in range(n):
                try:
                    next(gen)
                except StopIteration:
                    return False
            return True

        for k in range(1, NCH):
            gens = [post_gen(k - 1)]
            if k + 1 < NCH:
                gens.append(pre_gen(k + 1))
            gen = _ichain(*gens)
            st = {"s_evac": {}, "v_col": {}}
            alive = True
            for c in range(W):
                emit_scan_step(k, c, st)
                if alive:
                    alive = pump(gen, 2)
            while alive:
                alive = pump(gen, 8)
            emit_vtail(k)
            if k + 1 < NCH:
                emit_qp0(k + 1)
        for _ in post_gen(NCH - 1):
            pass

        sch.wait("y", "O")
        block = es.enter_context(nc.Block())
        sch.play(block)
    return nc


_T_FULL = 2048


def kernel(L0_attn_feature, L0_attn_wq_w, L0_attn_wq_b, L0_attn_w2_w,
           L0_attn_w2_b, L0_attn_ln_g, L0_attn_ln_b, L0_ffn_w_gate,
           L0_ffn_w_up, L0_ffn_w_down, L0_c, L0_wmod):
    x = np.ascontiguousarray(np.asarray(L0_attn_feature, dtype=np.float32))
    consts = make_constants()
    shared = {
        "wq": np.ascontiguousarray(np.asarray(L0_attn_wq_w, np.float32)),
        "wqb": np.ascontiguousarray(np.asarray(L0_attn_wq_b, np.float32)),
        "w2": np.ascontiguousarray(np.asarray(L0_attn_w2_w, np.float32)),
        "lng": np.ascontiguousarray(np.asarray(L0_attn_ln_g, np.float32)),
        "lnb": np.ascontiguousarray(np.asarray(L0_attn_ln_b, np.float32)),
        "wg": np.ascontiguousarray(np.asarray(L0_ffn_w_gate, np.float32)),
        "wu": np.ascontiguousarray(np.asarray(L0_ffn_w_up, np.float32)),
        "wd": np.ascontiguousarray(np.asarray(L0_ffn_w_down, np.float32)),
        "wmod": np.ascontiguousarray(np.asarray(L0_wmod, np.float32)),
        "sel": consts["sel"],
        "b3init": consts["b3init"],
        "ident": consts["ident"],
    }
    cv = np.ascontiguousarray(np.asarray(L0_c, np.float32))
    nc = build_program(_T_FULL)
    in_maps = []
    for core in range(N_CORES):
        m = dict(shared)
        m["x"] = np.ascontiguousarray(x[core * BPC:(core + 1) * BPC])
        m["cvec"] = np.ascontiguousarray(cv[core * BPC:(core + 1) * BPC])
        in_maps.append(m)
    res = run_bass_kernel_spmd(nc, in_maps, core_ids=list(range(N_CORES)))
    out = np.concatenate([np.asarray(r["out"]) for r in res.results], axis=0)
    return out.astype(np.float32)


# version 16
# speedup vs baseline: 1.1355x; 1.0219x over previous
import numpy as np
from itertools import chain as _ichain
import ml_dtypes
from contextlib import ExitStack

import concourse.bass as bass
import concourse.mybir as mybir
from concourse.bass_utils import run_bass_kernel_spmd

F32 = mybir.dt.float32
BF16 = mybir.dt.bfloat16
AT = mybir.ActivationFunctionType
OP = mybir.AluOpType
AX = mybir.AxisListType

B_TOT = 32
N_CORES = 8
BPC = 4
D = 1024
I_FF = 2816
W = 64
LN_EPS = 1e-5
DC = D // 128
IC = I_FF // 128
NBI = I_FF // 16

ENGS = ("v", "s", "t", "g", "y")


def _bf(x):
    return np.asarray(x, dtype=ml_dtypes.bfloat16)


def make_constants():
    sel = np.zeros((32, 128, 128), np.float32)
    for vv in range(32):
        for m in range(128):
            sel[vv, (m // 32) * 32 + vv, m] = 1.0
    b3i = np.zeros((128, 2 * W), np.float32)
    for p in range(128):
        for pl in range(2):
            b3i[p, pl * W + pl * 32 + (p % 32)] = 1.0
    return {"sel": _bf(sel.reshape(32 * 128, 128)),
            "b3init": _bf(b3i),
            "ident": _bf(np.eye(128, dtype=np.float32))}


class Sched:
    def __init__(self, nc):
        self.nc = nc
        self.ops = {e: [] for e in ENGS}
        self.sems = {}
        self.counts = {}

    def add_sem(self, es, name):
        h = es.enter_context(self.nc.semaphore(name))
        self.sems[name] = h
        self.counts[name] = 0

    def emit(self, eng, fn):
        self.ops[eng].append(fn)

    def tick(self, sem, n=1):
        self.counts[sem] += n
        return self.counts[sem]

    def edge(self, eng, sem, n=1):
        h = self.sems[sem]
        self.counts[sem] += n
        self.ops[eng].append(lambda e, h=h, n=n: e.drain().then_inc(h, n))
        return self.counts[sem]

    def dma(self, eng, sem, fn):
        h = self.sems[sem]
        self.counts[sem] += 16
        self.ops[eng].append(lambda e, h=h: fn(e, h))
        return self.counts[sem]

    def wait(self, eng, sem, count=None):
        h = self.sems[sem]
        c = self.counts[sem] if count is None else count
        if c > 0:
            self.ops[eng].append(lambda e, h=h, c=c: e.wait_ge(h, c))

    def barrier(self):
        names = {"v": "V", "s": "S", "t": "P", "g": "Q", "y": "Y"}
        for eng, sem in names.items():
            self.edge(eng, sem)
        snap = {sem: self.counts[sem] for sem in names.values()}
        for eng in names:
            for sem, c in snap.items():
                self.wait(eng, sem, c)

    def play(self, block):
        ops = self.ops

        @block.vector
        def _(v):
            for fn in ops["v"]:
                fn(v)

        @block.scalar
        def _(s):
            for fn in ops["s"]:
                fn(s)

        @block.tensor
        def _(t):
            for fn in ops["t"]:
                fn(t)

        @block.gpsimd
        def _(g):
            for fn in ops["g"]:
                fn(g)

        @block.sync
        def _(y):
            for fn in ops["y"]:
                fn(y)


def build_program(T):
    nc = bass.Bass("TRN2", target_bir_lowering=False)
    NCH = T // W
    assert T % W == 0 and NCH >= 2

    x_h = nc.dram_tensor("x", [BPC, T, D], F32, kind="ExternalInput").ap()
    wq_h = nc.dram_tensor("wq", [D, D], F32, kind="ExternalInput").ap()
    wqb_h = nc.dram_tensor("wqb", [D], F32, kind="ExternalInput").ap()
    w2_h = nc.dram_tensor("w2", [D], F32, kind="ExternalInput").ap()
    lng_h = nc.dram_tensor("lng", [D], F32, kind="ExternalInput").ap()
    lnb_h = nc.dram_tensor("lnb", [D], F32, kind="ExternalInput").ap()
    wg_h = nc.dram_tensor("wg", [I_FF, D], F32, kind="ExternalInput").ap()
    wu_h = nc.dram_tensor("wu", [I_FF, D], F32, kind="ExternalInput").ap()
    wd_h = nc.dram_tensor("wd", [D, I_FF], F32, kind="ExternalInput").ap()
    cvec_h = nc.dram_tensor("cvec", [BPC, D], F32, kind="ExternalInput").ap()
    wmod_h = nc.dram_tensor("wmod", [D, 6 * D], F32, kind="ExternalInput").ap()
    sel_h = nc.dram_tensor("sel", [32 * 128, 128], BF16, kind="ExternalInput").ap()
    b3i_h = nc.dram_tensor("b3init", [128, 2 * W], BF16, kind="ExternalInput").ap()
    id_h = nc.dram_tensor("ident", [128, 128], BF16, kind="ExternalInput").ap()
    out_h = nc.dram_tensor("out", [BPC, T, D], F32, kind="ExternalOutput").ap()
    wdq_h = nc.dram_tensor("wdq", [IC, 128, D], BF16, kind="Internal").ap()

    es = ExitStack()
    SB = lambda n, s, d=F32: es.enter_context(nc.sbuf_tensor(n, s, d))
    PS = lambda n, s, d=F32: es.enter_context(nc.psum_tensor(n, s, d))

    with es:
        QW = 1408
        wq_sb = SB("wq_sb", [128, DC * D], BF16)
        wgT = SB("wgT", [128, DC * I_FF], BF16)
        wuT = SB("wuT", [128, DC * I_FF], BF16)
        wdTs = SB("wdTs", [128, 4 * D], BF16)
        dn2 = SB("dn2", [128, 12 * 128], BF16)
        tstg = SB("tstg", [128, 2 * 128], BF16)
        wqb_f = SB("wqb_f", [128, DC], F32)
        w2_f = SB("w2_f", [128, DC], F32)
        lng_f = SB("lng_f", [128, DC], F32)
        lnb_f = SB("lnb_f", [128, DC], F32)
        modf = SB("modf", [128, 6 * BPC * DC], F32)
        lngga = SB("lngga", [128, BPC * DC], F32)
        lnbga = SB("lnbga", [128, BPC * DC], F32)
        silc = SB("silc", [128, BPC * DC], BF16)
        sel_sb = SB("sel_sb", [128, 32 * 128], BF16)
        b3i_sb = SB("b3i_sb", [128, 2 * W], BF16)
        id_sb = SB("id_sb", [128, 128], BF16)
        xtd = SB("xtd", [128, 2 * 2 * D], F32)
        fT = SB("fT", [128, DC * 256], BF16)
        stat = SB("stat", [128, 16], F32)
        wb3 = SB("wb3", [128, 4 * 128], BF16)
        eh = SB("eh", [128, W + 1], BF16)
        ebc = SB("ebc", [128, W + 1], F32)
        wnrm = SB("wnrm", [128, W], BF16)
        zac = SB("zac", [128, 4], F32)
        col32 = SB("col32", [128, 4], F32)
        vtok = SB("vtok", [128, 2 * 2 * D], BF16)
        vT = SB("vT", [128, DC * 256], BF16)
        yt = SB("yt", [128, 2 * D], BF16)
        h2d = SB("h2d", [128, 2 * 2 * D], BF16)
        h3 = SB("h3", [128, I_FF // 2], BF16)
        hqt = SB("hqt", [128, I_FF], BF16)
        hqt2 = SB("hqt2", [128, I_FF], BF16)

        qsc = SB("qsc", [128, QW // 16], F32)
        qrs = SB("qrs", [128, QW // 16], F32)
        qa = SB("qa", [128, QW], BF16)
        gbt = SB("gbt", [128, D], BF16)
        cat = SB("cat", [128, D], BF16)
        sat = SB("sat", [128, D], BF16)
        cbt = SB("cbt", [128, D], BF16)
        sbt = SB("sbt", [128, D], BF16)
        lgt = SB("lgt", [128, D], BF16)
        lbt = SB("lbt", [128, D], BF16)
        junk = SB("junk", [128, 4 * W], BF16)
        sgn = SB("sgn", [128, QW], BF16)
        ones1 = SB("ones1", [128, 4], F32)

        qT = yt[:, 0:2 * D]
        wmst = hqt[:, 0:1536]
        xq = yt[:]
        wnat = vtok[:, 0:I_FF]
        cstg = xtd[:, 0:BPC * DC]
        ms1 = xtd[:, 64:64 + BPC * DC]
        ms2 = xtd[:, 128:128 + BPC * DC]

        pA = PS("pA", [128, 512], F32)
        pB = PS("pB", [128, 512], F32)
        pC = PS("pC", [128, 512], F32)
        pD = PS("pD", [128, 512], F32)
        pbc = [PS("pbc0", [128, W + 1], F32), PS("pbc1", [128, W + 1], F32)]
        ptr = [PS("ptA", [128, 128], BF16), PS("ptB", [128, 128], BF16)]

        sch = Sched(nc)
        for sname in ("L", "G", "G2", "G3", "G4", "V", "S", "P", "Q", "X", "X2", "O", "Y"):
            sch.add_sem(es, sname)

        wb3v = wb3[:].rearrange("p (pl c) -> p pl c", pl=4)
        modv = modf[:].rearrange("p (j b dc) -> p j b dc", j=6, b=BPC)
        fT3 = fT[:].rearrange("p (dc t) -> p dc t", dc=DC)
        qT3 = qT.rearrange("p (dc t) -> p dc t", dc=DC)
        vT3 = vT[:].rearrange("p (dc t) -> p dc t", dc=DC)
        silc3 = silc[:].rearrange("p (b dc) -> p b dc", b=BPC)
        ms1v = ms1.rearrange("p (b dc) -> p b dc", b=BPC)
        ms2v = ms2.rearrange("p (b dc) -> p b dc", b=BPC)

        def ldvec(dst, src):
            sch.dma("y", "L", lambda e, h, dst=dst, src=src: e.dma_start(
                out=dst[:], in_=src.rearrange("(dc p) -> p dc", p=128),
                allow_slow_non_contiguous=True).then_inc(h, 16))

        ldvec(wqb_f, wqb_h)
        ldvec(w2_f, w2_h)
        ldvec(lng_f, lng_h)
        ldvec(lnb_f, lnb_h)
        sch.dma("y", "L", lambda e, h: e.dma_start(
            out=sel_sb[:].rearrange("p (v m) -> p v m", v=32),
            in_=sel_h.rearrange("(v p) m -> p v m", p=128)).then_inc(h, 16))
        sch.dma("y", "L", lambda e, h: e.dma_start(out=b3i_sb[:], in_=b3i_h).then_inc(h, 16))
        sch.dma("y", "L", lambda e, h: e.dma_start(out=id_sb[:], in_=id_h).then_inc(h, 16))
        sch.dma("y", "L", lambda e, h: e.dma_start(
            out=cstg.rearrange("p (b dc) -> p b dc", b=BPC),
            in_=cvec_h.rearrange("b (dc p) -> p b dc", p=128),
            allow_slow_non_contiguous=True).then_inc(h, 16))
        vec_ld = sch.counts["L"]
        sch.dma("g", "G", lambda e, h: e.dma_start(
            out=wq_sb[:].rearrange("p (dc o) -> p dc o", dc=DC),
            in_=wq_h.rearrange("(dc p) o -> p dc o", p=128)).then_inc(h, 16))
        wq_ld = sch.counts["G"]

        sch.wait("s", "L", vec_ld)
        sch.emit("s", lambda s: s.activation(silc[:], cstg, AT.Silu))
        silc_done = sch.edge("s", "S")

        first_mm = [True]
        for dc in range(DC):
            for qtr in range(4):
                sch.dma("g", "G", lambda e, h, dc=dc, qtr=qtr: e.dma_start(
                    out=wmst[:], in_=wmod_h[dc * 128:(dc + 1) * 128,
                                            qtr * 1536:(qtr + 1) * 1536]).then_inc(h, 16))
                sch.wait("t", "G")
                if dc == 0 and qtr == 0:
                    sch.wait("t", "S", silc_done)
                for mcl in range(12):
                    mc = qtr * 12 + mcl
                    sch.emit("t", lambda t, dc=dc, mc=mc, mcl=mcl, f=first_mm[0]:
                             t.matmul(
                        pA[:, mc * 4:(mc + 1) * 4],
                        wmst[:, mcl * 128:(mcl + 1) * 128],
                        silc3[:, :, dc],
                        start=f, stop=(dc == DC - 1),
                        skip_group_check=True))
                    first_mm[0] = False
                pe_c = sch.edge("t", "P")
                sch.wait("g", "P", pe_c)
        mod_mm = sch.edge("t", "P")

        sch.wait("v", "P", mod_mm)
        pAv = pA[:, 0:192].rearrange("p (j dc b) -> p j b dc", j=6, dc=DC)
        sch.emit("v", lambda v: v.tensor_copy(modv[:, :, :, :], pAv))
        sch.emit("v", lambda v: v.drain())

        def gprime(j):
            def f(v, j=j):
                m = modv[:, j]
                v.tensor_scalar(ms1v, m, 0.0, 2.0, OP.is_ge, OP.mult)
                v.drain()
                v.tensor_scalar(ms2v, ms1v, 1.0, None, OP.subtract)
                v.drain()
                v.tensor_scalar(ms1v, m, -1.0, None, OP.mult)
                v.drain()
                v.tensor_tensor(ms1v, ms1v, m, OP.max)
                v.drain()
                v.tensor_scalar(ms1v, ms1v, 0.25, None, OP.max)
                v.drain()
                v.tensor_tensor(m, ms1v, ms2v, OP.mult)
                v.drain()
            sch.emit("v", f)

        gprime(2)
        gprime(5)
        sch.emit("v", lambda v: v.tensor_scalar(modv[:, 1], modv[:, 1], 1.0, None, OP.add))
        sch.emit("v", lambda v: v.tensor_scalar(modv[:, 4], modv[:, 4], 1.0, None, OP.add))
        sch.emit("v", lambda v: v.drain())

        def f_lngga(v):
            for b in range(BPC):
                v.tensor_tensor(lngga[:, b * DC:(b + 1) * DC], lng_f[:], modv[:, 2, b], OP.mult)
                v.tensor_tensor(lnbga[:, b * DC:(b + 1) * DC], lnb_f[:], modv[:, 2, b], OP.mult)
            v.memset(ones1[:], 1.0)
            v.drain()
        sch.emit("v", f_lngga)
        sch.emit("v", lambda v: v.memset(eh[:, W:W + 1], 1.0))
        sch.emit("v", lambda v: v.drain())
        mod_done = sch.edge("v", "V")

        def emit_quant(src_ap, dst_ap, NC, scrB=None, scrB2=None):
            nb = NC // 16
            src3 = src_ap.rearrange("p (nb k) -> p nb k", k=16)
            sc3 = qsc[:, 0:nb].rearrange("p (nb o) -> p nb o", o=1)

            def v1(v):
                v.tensor_reduce(sc3, src3, AX.X, OP.max, apply_absolute_value=True)
                v.drain()
                v.tensor_scalar(qsc[:, 0:nb], qsc[:, 0:nb], 1.0 / 6.0, 1e-12,
                                OP.mult, OP.max)
                v.drain()
                v.reciprocal(qrs[:, 0:nb], qsc[:, 0:nb])
                v.drain()
            sch.emit("v", v1)

            def vsg(v):
                v.tensor_scalar(sgn[:, 0:NC], src_ap, 0.0, 2.0, OP.is_ge, OP.mult)
                v.drain()
                v.tensor_scalar(sgn[:, 0:NC], sgn[:, 0:NC], 1.0, None, OP.subtract)
                v.drain()
            sch.emit("v", vsg)
            yield

            def v2a(v):
                rsb = qrs[:, 0:nb].rearrange("p (nb o) -> p nb o", o=1).broadcast_to([128, nb, 16])
                a13 = qa[:, 0:NC].rearrange("p (nb k) -> p nb k", k=16)
                v.tensor_tensor(a13, src3, rsb, OP.mult)
                v.drain()
                v.tensor_tensor(qa[:, 0:NC], qa[:, 0:NC], sgn[:, 0:NC], OP.mult)
                v.drain()
            sch.emit("v", v2a)
            yield
            a = dst_ap
            bb = [scrB[:, 0:NC], scrB2[:, 0:NC]]
            vv = qa[:, 0:NC]
            steps = [(1.25, 0.5), (1.75, 0.5), (2.5, 1.0), (3.5, 1.0), (5.0, 2.0)]

            def v2b(v):
                v.tensor_scalar(a, vv, 0.25, 0.5, OP.is_gt, OP.mult)
                v.tensor_scalar(bb[0], vv, 0.75, 0.5, OP.is_gt, OP.mult)
                v.drain()
                for i, (thr, wgt) in list(enumerate(steps))[:2]:
                    v.tensor_tensor(a, a, bb[i % 2], OP.add)
                    v.tensor_scalar(bb[(i + 1) % 2], vv, thr, wgt, OP.is_gt, OP.mult)
                    v.drain()
            sch.emit("v", v2b)
            yield

            def v2c(v):
                for i, (thr, wgt) in list(enumerate(steps))[2:]:
                    v.tensor_tensor(a, a, bb[i % 2], OP.add)
                    v.tensor_scalar(bb[(i + 1) % 2], vv, thr, wgt, OP.is_gt, OP.mult)
                    v.drain()
                v.tensor_tensor(a, a, bb[len(steps) % 2], OP.add)
                v.drain()
            sch.emit("v", v2c)
            yield

            def v2d(v):
                v.tensor_tensor(a, a, sgn[:, 0:NC], OP.mult)
                v.drain()
                scb = qsc[:, 0:nb].rearrange("p (nb o) -> p nb o", o=1).broadcast_to([128, nb, 16])
                dst3 = dst_ap.rearrange("p (nb k) -> p nb k", k=16)
                v.tensor_tensor(dst3, dst3, scb, OP.mult)
                v.drain()
            sch.emit("v", v2d)

        sch.wait("g", "V", mod_done)

        wnat2 = h2d[:, 0:I_FF]
        hqts = [hqt[:], hqt2[:]]

        def weight_pipe(src_h, nslab, ncols, dst, dst_col, to_dram=False):
            bufs = [wnat, wnat2]
            cq_hist = {}
            xh_c = {}
            sch.wait("g", "V")
            xh_c[-2] = xh_c[-1] = (sch.counts["X"], sch.counts["X2"],
                                   sch.counts["O"])

            def slab_load(sl):
                gs = "G" if sl % 2 == 0 else "G2"
                if sl >= 2:
                    sch.wait("g", "V", cq_hist[sl - 2])
                sch.dma("g", gs, lambda e, h, sl=sl: e.dma_start(
                    out=bufs[sl % 2][:, 0:ncols],
                    in_=src_h[sl * 128:(sl + 1) * 128, :]).then_inc(h, 16))
                return sch.counts[gs]
            gtot = {0: slab_load(0)}
            for sl in range(nslab):
                if sl + 1 < nslab:
                    gtot[sl + 1] = slab_load(sl + 1)
                sch.wait("v", "G" if sl % 2 == 0 else "G2", gtot[sl])
                sch.wait("s", "G" if sl % 2 == 0 else "G2", gtot[sl])
                hq = hqts[sl % 2]
                sch.wait("v", "X", xh_c[sl - 2][0])
                sch.wait("v", "X2", xh_c[sl - 2][1])
                sch.wait("v", "O", xh_c[sl - 2][2])
                for q0 in range(0, ncols, QW):
                    qw = min(QW, ncols - q0)
                    for _ in emit_quant(bufs[sl % 2][:, q0:q0 + qw],
                                        hq[:, q0:q0 + qw], qw, h3, yt):
                        pass
                cq_hist[sl] = sch.counts["V"] + 1
                cq = sch.edge("v", "V")
                assert cq == cq_hist[sl]
                sch.wait("y", "V", cq)
                sch.wait("s", "V", cq)
                if to_dram:
                    for cb in range(ncols // 128):
                        sch.wait("y", "O", sch.counts["O"])
                        sch.dma("y", "X", lambda e, h, sl=sl, cb=cb, hq=hq:
                                e.dma_start_transpose(
                            out=tstg[:, (cb % 2) * 128:(cb % 2 + 1) * 128],
                            in_=hq[:, cb * 128:(cb + 1) * 128]).then_inc(h, 16))
                        sch.wait("y", "X", sch.counts["X"])
                        sch.dma("y", "O", lambda e, h, sl=sl, cb=cb: e.dma_start(
                            out=wdq_h[cb, :, sl * 128:(sl + 1) * 128],
                            in_=tstg[:, (cb % 2) * 128:(cb % 2 + 1) * 128]
                        ).then_inc(h, 16))
                else:
                    sch.wait("y", "X")
                    sch.wait("s", "X2")
                    for cb in range(ncols // 128):
                        eng, sem = ("y", "X") if cb % 2 == 0 else ("s", "X2")
                        sch.dma(eng, sem, lambda e, h, sl=sl, cb=cb, hq=hq:
                                e.dma_start_transpose(
                            out=dst[:, dst_col(cb, sl):dst_col(cb, sl) + 128],
                            in_=hq[:, cb * 128:(cb + 1) * 128]).then_inc(h, 16))
                xh_c[sl] = (sch.counts["X"], sch.counts["X2"], sch.counts["O"])

        weight_pipe(wg_h, IC, D, wgT, lambda cb, sl: cb * I_FF + sl * 128)
        weight_pipe(wu_h, IC, D, wuT, lambda cb, sl: cb * I_FF + sl * 128)
        weight_pipe(wd_h, DC, I_FF, None, None, to_dram=True)
        for e in ("v", "s", "g", "y", "t"):
            sch.wait(e, "X")
            sch.wait(e, "X2")
            sch.wait(e, "O")
        sch.wait("t", "G")
        sch.barrier()

        def emit_rstd(nsrc, ndst):
            vs = stat[:, nsrc:nsrc + 2]
            t_ = stat[:, 4:6]
            r_ = stat[:, 6:8]
            u_ = stat[:, 8:10]
            w_ = stat[:, 10:12]
            y_ = stat[:, ndst:ndst + 2]

            def f1(v):
                v.tensor_scalar(t_, vs, 1.0 / D, LN_EPS, OP.mult, OP.add)
                v.drain()
                v.reciprocal(r_, t_)
                v.drain()
            sch.emit("v", f1)
            c_v = sch.edge("v", "V")
            sch.wait("g", "V", c_v)

            def f2(g):
                g.tensor_scalar(u_, t_, 2.0, None, OP.is_gt)
                g.drain()
                g.tensor_scalar(w_, u_, 0.58, 0.42, OP.mult, OP.add)
                g.tensor_scalar(u_, u_, -0.395, 0.62, OP.mult, OP.add)
                g.drain()
                g.tensor_tensor(u_, u_, t_, OP.mult)
                g.drain()
                g.tensor_tensor(u_, u_, w_, OP.add)
                g.drain()
                g.tensor_tensor(y_, r_, u_, OP.mult)
                g.drain()
                for _ in range(3):
                    g.tensor_tensor(u_, y_, y_, OP.mult)
                    g.drain()
                    g.tensor_tensor(u_, u_, t_, OP.mult)
                    g.drain()
                    g.tensor_scalar(u_, u_, -0.5, 1.5, OP.mult, OP.add)
                    g.drain()
                    g.tensor_tensor(y_, y_, u_, OP.mult)
                    g.drain()
            sch.emit("g", f2)
            c_g = sch.edge("g", "Q")
            sch.wait("v", "Q", c_g)

        def xpre():
            sch.wait("y", "X")
            sch.wait("y", "X2")

        def xw(eng, cp=None):
            cx, cx2 = cp if cp else (sch.counts["X"], sch.counts["X2"])
            sch.wait(eng, "X", cx)
            sch.wait(eng, "X2", cx2)

        def xbar_tiles(src_tok, dstT3):
            for pl in range(2):
                for dc in range(DC):
                    sch.dma("y", "X", lambda e, h, pl=pl, dc=dc: e.dma_start_transpose(
                        out=dstT3[:, dc, pl * 128:(pl + 1) * 128],
                        in_=src_tok[:, pl * D + dc * 128: pl * D + (dc + 1) * 128]
                    ).then_inc(h, 16))
            return (sch.counts["X"], sch.counts["X2"])

        def xbar_back(srcT3, dst_tok):
            for pl in range(2):
                for dc in range(DC):
                    sch.dma("y", "X", lambda e, h, pl=pl, dc=dc: e.dma_start_transpose(
                        out=dst_tok[:, pl * D + dc * 128: pl * D + (dc + 1) * 128],
                        in_=srcT3[:, dc, pl * 128:(pl + 1) * 128]).then_inc(h, 16))
            return (sch.counts["X"], sch.counts["X2"])

        PTS = {"s2": [0, 0]}

        def _pe_xbar_one(src_tok, dstT3, pl, dc, i):
            sch.wait("t", "S", PTS["s2"][i % 2])
            cp = sch.tick("P")
            sch.emit("t", lambda t, pl=pl, dc=dc, i=i, cp=cp: t.transpose(
                ptr[i % 2][:],
                src_tok[:, pl * D + dc * 128: pl * D + (dc + 1) * 128],
                id_sb[:]).then_inc(sch.sems["P"], 1))
            sch.wait("s", "P", cp)
            cs = sch.tick("S")
            sch.emit("s", lambda s, pl=pl, dc=dc, i=i, cs=cs: s.activation(
                dstT3[:, dc, pl * 128:(pl + 1) * 128], ptr[i % 2][:],
                AT.Copy).then_inc(sch.sems["S"], 1))
            PTS["s2"][i % 2] = cs
            return cs

        def pe_xbar(src_tok, dstT3):
            i = 0
            cs = 0
            for pl in range(2):
                for dc in range(DC):
                    cs = _pe_xbar_one(src_tok, dstT3, pl, dc, i)
                    i += 1
                    if i % 4 == 0:
                        yield
            PTS["last"] = cs

        def pe_xbar_now(src_tok, dstT3):
            i = 0
            cs = 0
            for pl in range(2):
                for dc in range(DC):
                    cs = _pe_xbar_one(src_tok, dstT3, pl, dc, i)
                    i += 1
            PTS["last"] = cs
            return cs

        def _mod_bcast(j):
            a = modv[:, j].transpose([0, 2, 1]).unsqueeze(3)
            return a.broadcast_to([128, DC, BPC, 32])

        def feat_affine(T3, j_scale, j_shift):
            def f(v):
                for pl in range(2):
                    T5 = T3.rearrange("p dc (pl b q) -> p dc pl b q",
                                      pl=2, b=BPC)[:, :, pl]
                    v.tensor_tensor(T5, T5, _mod_bcast(j_scale), OP.mult)
                v.drain()
                if j_shift is not None:
                    for pl in range(2):
                        T5 = T3.rearrange("p dc (pl b q) -> p dc pl b q",
                                          pl=2, b=BPC)[:, :, pl]
                        v.tensor_tensor(T5, T5, _mod_bcast(j_shift), OP.add)
                    v.drain()
            sch.emit("v", f)

        def emit_ln(src_ap, dst_bf_ap, musl, varsl, rstdsl, sq_scratch):
            s2 = src_ap.rearrange("p (pl d) -> p pl d", pl=2)

            def f0(v):
                v.tensor_reduce(stat[:, musl:musl + 2].unsqueeze(2), s2, AX.X, OP.add)
                v.drain()
                v.tensor_scalar(stat[:, musl:musl + 2], stat[:, musl:musl + 2],
                                1.0 / D, None, OP.mult)
                v.drain()
            sch.emit("v", f0)
            yield
            for pl in range(2):
                s_ = src_ap[:, pl * D:(pl + 1) * D]
                d_ = dst_bf_ap[:, pl * D:(pl + 1) * D]

                def f(v, s_=s_, d_=d_, pl=pl):
                    v.tensor_scalar(d_, s_, stat[:, musl + pl:musl + pl + 1], None,
                                    OP.subtract)
                    v.drain()
                    v.scalar_tensor_tensor(sq_scratch[:, pl * D:(pl + 1) * D], d_, 1.0,
                                           d_, OP.mult, OP.mult,
                                           accum_out=stat[:, varsl + pl:varsl + pl + 1])
                    v.drain()
                sch.emit("v", f)
                yield
            emit_rstd(varsl, rstdsl)
            yield

            def fn(v):
                for pl in range(2):
                    v.tensor_scalar(dst_bf_ap[:, pl * D:(pl + 1) * D],
                                    dst_bf_ap[:, pl * D:(pl + 1) * D],
                                    stat[:, rstdsl + pl:rstdsl + pl + 1], None, OP.mult)
                v.drain()
            sch.emit("v", fn)

        def feat_affine_ln(T3, vec_a, vec_b):
            def bc(vec):
                a = vec[:].rearrange("p (b dc) -> p b dc", b=BPC).transpose(
                    [0, 2, 1]).unsqueeze(3)
                return a.broadcast_to([128, DC, BPC, 32])

            def f(v):
                for op, vec in ((OP.mult, vec_a), (OP.add, vec_b)):
                    for pl in range(2):
                        T5 = T3.rearrange("p dc (pl b q) -> p dc pl b q",
                                          pl=2, b=BPC)[:, :, pl]
                        v.tensor_tensor(T5, T5, bc(vec), op)
                    v.drain()
            sch.emit("v", f)

        def _vec_bcast(vec):
            a = vec[:].rearrange("p (b dc) -> p b dc", b=BPC).transpose(
                [0, 2, 1]).unsqueeze(3)
            return a.broadcast_to([128, DC, BPC, 32])

        for bsrc, dst in ((_mod_bcast(5), gbt), (_mod_bcast(1), cat),
                          (_mod_bcast(0), sat), (_mod_bcast(4), cbt),
                          (_mod_bcast(3), sbt), (_vec_bcast(lngga), lgt),
                          (_vec_bcast(lnbga), lbt)):
            def f_gfm(v, bsrc=bsrc):
                v.memset(fT[:], 1.0)
                v.drain()
                T5 = fT3.rearrange("p dc (pl b q) -> p dc pl b q",
                                   pl=2, b=BPC)[:, :, 0]
                v.tensor_tensor(T5, T5, bsrc, OP.mult)
                v.drain()
            sch.emit("v", f_gfm)
            c_v = sch.edge("v", "V")
            sch.wait("y", "V", c_v)
            xpre()
            for dc in range(DC):
                sch.dma("y", "X", lambda e, h, dc=dc, dst=dst:
                        e.dma_start_transpose(
                    out=dst[:, dc * 128:(dc + 1) * 128],
                    in_=fT3[:, dc, 0:128]).then_inc(h, 16))
            xw("v")

        def tok_affine(dst, srcv, mtile, atile):
            d3 = dst.rearrange("p (pl f) -> p pl f", pl=2)
            s3 = srcv.rearrange("p (pl f) -> p pl f", pl=2)
            mb = mtile[:].unsqueeze(1).broadcast_to([128, 2, D])
            ab = atile[:].unsqueeze(1).broadcast_to([128, 2, D])

            def f1(v):
                v.tensor_tensor(d3, s3, mb, OP.mult)
                v.drain()
            sch.emit("v", f1)
            yield

            def f2(v):
                v.tensor_tensor(d3, d3, ab, OP.add)
                v.drain()
            sch.emit("v", f2)

        IHALF = [[(0, 512), (512, 1024), (1024, 1408)],
                 [(1408, 1920), (1920, 2432), (2432, 2816)]]
        P0D = {}
        SCI = {}
        VRD = {}
        FFP = {}
        QMP = {}
        DWN = {"last": None}

        def slot(k):
            par = k % 2
            return (xtd[:, par * 2 * D:(par + 1) * 2 * D],
                    h2d[:, par * 2 * D:(par + 1) * 2 * D])

        def pre_gen(k, lnscr=None):
            if lnscr is None:
                lnscr = vT[:]
            xt, h2t = slot(k)
            sch.wait("y", "V")
            sch.wait("y", "O")
            for pl in range(2):
                sch.dma("y", "L", lambda e, h, pl=pl, k=k: e.dma_start(
                    out=xt[:, pl * D:(pl + 1) * D],
                    in_=x_h[:, k * W + pl * 32: k * W + pl * 32 + 32, :]
                ).then_inc(h, 16))
            sch.wait("v", "L")
            yield
            yield from emit_ln(xt, yt[:], 0, 2, 12, lnscr)
            yield
            yield from tok_affine(h2t, yt[:], cat, sat)
            c_hT = sch.edge("v", "V")
            yield
            if k >= 1:
                sch.wait("t", "V", c_hT)
                if (k - 2) in FFP:
                    sch.wait("s", "P", FFP[k - 2])
                yield from pe_xbar(h2t, fT3)
                sch.wait("t", "S", PTS["last"])
                yield
            if k >= 1:
                for oc in range(DC):
                    pq = pA if oc % 2 == 0 else pB
                    for dc in range(DC):
                        sch.emit("t", lambda t, oc=oc, dc=dc, pq=pq: t.matmul(
                            pq[:, 0:256],
                            wq_sb[:, dc * D + oc * 128: dc * D + (oc + 1) * 128],
                            fT3[:, dc, :],
                            start=(dc == 0), stop=(dc == DC - 1),
                            skip_group_check=True))
                    c_p = sch.edge("t", "P")
                    sch.wait("v", "P", c_p)

                    def fq(v, oc=oc, pq=pq):
                        v.tensor_scalar(qT3[:, oc, :], pq[:, 0:256],
                                        wqb_f[:, oc:oc + 1], w2_f[:, oc:oc + 1],
                                        OP.add, OP.mult)
                        v.drain()
                    sch.emit("v", fq)
                    c_vq = sch.edge("v", "V")
                    sch.wait("t", "V", c_vq)
                    if oc % 2 == 1:
                        yield
                QMP[k] = sch.counts["P"]

        def emit_qp0(k):
            sch.wait("t", "S", PTS.get("last", 0))
            sch.wait("t", "V")
            vT4 = vT3[:, :, :].rearrange("p dc (pl q) -> p dc pl q", pl=2)
            first = True
            for b in range(BPC):
                for plr in range(2):
                    for dc in range(DC):
                        sch.emit("t", lambda t, b=b, plr=plr, dc=dc, first=first:
                                 t.matmul(
                            pC[b * 32:(b + 1) * 32, plr * 64: plr * 64 + 64],
                            qT3[:, dc, plr * 128 + b * 32: plr * 128 + b * 32 + 32],
                            vT4[:, dc, :, b * 32:(b + 1) * 32],
                            start=(dc == 0 and first), stop=(dc == DC - 1),
                            skip_group_check=True, tile_position=(0, b * 32)))
                    first = False
            P0D[k] = p0_done = sch.edge("t", "P")
            sch.wait("v", "P", p0_done)

            def f_p0(v):
                v.tensor_copy(wb3v[:, 0:2, 0:W], pC[:, 0:128].rearrange(
                    "p (pl c) -> p pl c", pl=2))
                v.tensor_copy(wb3v[:, 2:4, 0:W], b3i_sb[:].rearrange(
                    "p (pl c) -> p pl c", pl=2))
                v.drain()
            sch.emit("v", f_p0)
            SCI[k] = sch.edge("v", "V")

        def emit_scan_step(k, c, st):
            s_evac = st["s_evac"]
            v_col = st["v_col"]
            plc = c // 32
            pl0 = 0 if c < 32 else 1
            npl = 4 - pl0
            sch.wait("s", "V", v_col.get(c - 1, SCI[k]))
            cs = sch.tick("S")
            sch.emit("s", lambda s, plc=plc, c=c, cs=cs: s.activation(
                eh[:, 0:W], wb3v[:, plc, c:c + W], AT.Exp).then_inc(
                sch.sems["S"], 1))
            sch.wait("t", "S", cs)
            if c >= 2:
                sch.wait("t", "V", s_evac[c - 2])
            cp = sch.tick("P")
            sch.emit("t", lambda t, c=c: t.matmul(
                pbc[c % 2][:], sel_sb[:, (c % 32) * 128:(c % 32 + 1) * 128],
                eh[:], start=True, stop=True).then_inc(sch.sems["P"], 1))
            sch.wait("v", "P", cp)
            ce1 = sch.tick("V")
            sch.emit("v", lambda v, c=c: v.tensor_scalar(
                ebc[:], pbc[c % 2][:], 1.0, 0.0, OP.mult, OP.add,
                accum_out=zac[:, 0:1]).then_inc(sch.sems["V"], 1))
            s_evac[c] = ce1
            sch.wait("v", "V", ce1)
            ce2 = sch.tick("V")
            sch.emit("v", lambda v: v.reciprocal(
                zac[:, 2:3], zac[:, 0:1]).then_inc(sch.sems["V"], 1))
            if c >= 1:
                sch.wait("v", "V", v_col[c - 1])
            ce3 = sch.tick("V")

            def f_tt(v, c=c, pl0=pl0, npl=npl):
                wn = ebc[:, 0:W].rearrange("p (o k) -> p o k", o=1
                                           ).broadcast_to([128, npl, W])
                jv = junk[:].rearrange("p (pl k) -> p pl k", pl=4)[:, pl0:4]
                v.tensor_tensor(jv, wb3v[:, pl0:4, c:c + W], wn,
                                OP.mult).then_inc(sch.sems["V"], 1)
            sch.emit("v", f_tt)
            sch.wait("v", "V", ce3)
            ce4 = sch.tick("V")

            def f_rd(v, pl0=pl0, npl=npl):
                jv = junk[:].rearrange("p (pl k) -> p pl k", pl=4)[:, pl0:4]
                v.tensor_reduce(col32[:, pl0:4].unsqueeze(2), jv,
                                AX.X, OP.add).then_inc(sch.sems["V"], 1)
            sch.emit("v", f_rd)
            sch.wait("v", "V", ce2)
            sch.wait("v", "V", ce4)
            ce5 = sch.tick("V")

            def f_sc(v, c=c, pl0=pl0):
                v.tensor_scalar(wb3v[:, pl0:4, W + c:W + c + 1],
                                col32[:, pl0:4].unsqueeze(2),
                                zac[:, 2:3], None, OP.mult).then_inc(
                    sch.sems["V"], 1)
            sch.emit("v", f_sc)
            v_col[c] = ce5

        def emit_vtail(k):
            par = k % 2
            vcur = vtok[:, par * 2 * D:(par + 1) * 2 * D]
            vprev = vtok[:, (1 - par) * 2 * D:(2 - par) * 2 * D]
            vprev3 = vprev.rearrange("p (pl d) -> p pl d", pl=2)
            sch.wait("t", "V")
            pmap = {(0, 0): pA, (0, 1): pB, (1, 0): pC, (1, 1): pD}
            for b in range(BPC):
                for dh in range(2):
                    for plo in range(2):
                        pq = pmap[(dh, plo)]
                        for pli in range(2):
                            sch.emit("t", lambda t, b=b, dh=dh, plo=plo, pli=pli,
                                     pq=pq: t.matmul(
                                pq[b * 32:(b + 1) * 32, 0:512],
                                wb3v[b * 32:(b + 1) * 32, 2 + pli,
                                     W + plo * 32: W + plo * 32 + 32],
                                vprev3[b * 32:(b + 1) * 32, pli,
                                       dh * 512:(dh + 1) * 512],
                                start=(pli == 0), stop=(pli == 1),
                                skip_group_check=True,
                                tile_position=(b * 32, b * 32)))
            vmm_done = sch.edge("t", "P")
            sch.wait("v", "P", vmm_done)

            def f_vev(v, vcur=vcur):
                for dh in range(2):
                    for plo in range(2):
                        v.tensor_copy(
                            vcur[:, plo * D + dh * 512: plo * D + (dh + 1) * 512],
                            pmap[(dh, plo)][:])
                v.drain()
            sch.emit("v", f_vev)
            VRD[k] = v_ready = sch.edge("v", "V")
            if k + 1 < NCH:
                sch.wait("t", "V", v_ready)
                sch.wait("s", "V", v_ready)
                if k in P0D:
                    sch.wait("s", "P", P0D[k])
                pe_xbar_now(vcur, vT3)

        def post_gen(k):
            par = k % 2
            xt, h2t = slot(k)
            vcur = vtok[:, par * 2 * D:(par + 1) * 2 * D]
            vprev = vtok[:, (1 - par) * 2 * D:(2 - par) * 2 * D]
            sch.wait("s", "V", VRD[k])
            if (k + 1) in P0D:
                sch.wait("s", "P", P0D[k + 1])
            sch.emit("s", lambda s, vcur=vcur: s.activation(yt[:], vcur, AT.Tanh))
            tanh_done = sch.edge("s", "S")
            sch.wait("v", "S", tanh_done)

            def f_y1(v):
                v.tensor_scalar(yt[:], yt[:], 1.0, None, OP.add)
                v.drain()
            sch.emit("v", f_y1)
            yield

            def f_y2(v):
                v.tensor_tensor(yt[:], yt[:], h2t, OP.mult)
                v.drain()
            sch.emit("v", f_y2)
            yield
            yield from emit_ln(yt[:], h2t, 0, 2, 12, vT[:])
            yield
            yield from tok_affine(yt[:], h2t, lgt, lbt)
            yield

            def f_res(v):
                v.tensor_tensor(xt, xt, yt[:], OP.add)
                v.drain()
            sch.emit("v", f_res)
            yield from emit_ln(xt, h2t, 0, 2, 12, vT[:])
            yield
            yield from tok_affine(h2t, h2t, cbt, sbt)
            yield
            for q0 in range(0, 2 * D, QW):
                qw = min(QW, 2 * D - q0)
                yield from emit_quant(h2t[:, q0:q0 + qw], xq[:, q0:q0 + qw],
                                      qw, h3, hqt)
                yield
            c_v = sch.edge("v", "V")
            sch.wait("t", "V", c_v)
            if (k + 1) in QMP:
                sch.wait("s", "P", QMP[k + 1])
            yield from pe_xbar(xq, fT3)
            sch.wait("t", "S", PTS["last"])
            yield
            for tl in range(2):
                for half in range(2):
                    sch.wait("s", "V")
                    ICH3 = IHALF[half]

                    def gu_mms(j, dcs, edge=False, tl=tl, ICH3=ICH3):
                        i0, i1 = ICH3[j]
                        iw = i1 - i0
                        pg = pA if j % 2 == 0 else pB
                        pu = pC if j % 2 == 0 else pD
                        for dc in dcs:
                            sch.emit("t", lambda t, tl=tl, i0=i0, iw=iw,
                                     dc=dc, pg=pg: t.matmul(
                                pg[:, 0:iw],
                                fT3[:, dc, tl * 128:(tl + 1) * 128],
                                wgT[:, dc * I_FF + i0: dc * I_FF + i0 + iw],
                                start=(dc == 0), stop=(dc == DC - 1),
                                skip_group_check=True))
                            sch.emit("t", lambda t, tl=tl, i0=i0, iw=iw,
                                     dc=dc, pu=pu: t.matmul(
                                pu[:, 0:iw],
                                fT3[:, dc, tl * 128:(tl + 1) * 128],
                                wuT[:, dc * I_FF + i0: dc * I_FF + i0 + iw],
                                start=(dc == 0), stop=(dc == DC - 1),
                                skip_group_check=True))
                        if edge:
                            return sch.edge("t", "P")
                        return None

                    c_p = {}
                    c_vs = {}
                    gu_mms(0, range(0, 4))
                    yield
                    c_p[0] = gu_mms(0, range(4, DC), edge=True)
                    yield
                    for j in range(len(ICH3)):
                        if j + 1 < len(ICH3):
                            if j >= 1:
                                sch.wait("t", "V", c_vs[j - 1])
                            gu_mms(j + 1, range(0, 4))
                            yield
                            c_p[j + 1] = gu_mms(j + 1, range(4, DC), edge=True)
                        i0, i1 = ICH3[j]
                        iw = i1 - i0
                        hoff = half * 1408
                        pg = pA if j % 2 == 0 else pB
                        pu = pC if j % 2 == 0 else pD
                        so = (j % 2) * 512
                        sch.wait("s", "P", c_p[j])
                        sch.emit("s", lambda s, iw=iw, pg=pg, so=so:
                                 s.activation(h2t[:, so:so + iw],
                                              pg[:, 0:iw], AT.Tanh,
                                              scale=0.5))
                        sch.emit("s", lambda s: s.drain())
                        sch.emit("s", lambda s, iw=iw, so=so: s.activation(
                            h2t[:, so:so + iw], h2t[:, so:so + iw], AT.Copy,
                            bias=0.5, scale=0.5))
                        c_s = sch.edge("s", "S")
                        sch.wait("v", "S", c_s)

                        def f_su(v, i0=i0, iw=iw, hoff=hoff, pg=pg, pu=pu,
                                 so=so, h2t=h2t):
                            v.tensor_tensor(h2t[:, D + so:D + so + iw],
                                            pg[:, 0:iw],
                                            h2t[:, so:so + iw], OP.mult)
                            v.drain()
                            v.tensor_tensor(h3[:, i0 - hoff:i0 - hoff + iw],
                                            h2t[:, D + so:D + so + iw],
                                            pu[:, 0:iw], OP.mult)
                            v.drain()
                        sch.emit("v", f_su)
                        c_vs[j] = sch.edge("v", "V")
                        yield
                    sch.wait("t", "V", c_vs[len(ICH3) - 1])
                    c_h3 = sch.edge("v", "V")
                    sch.wait("s", "V", c_h3)
                    sch.wait("v", "S")
                    for q0 in range(0, 1408, QW):
                        yield from emit_quant(h3[:, q0:q0 + QW],
                                              hqt[:, half * 1408 + q0:
                                                  half * 1408 + q0 + QW], QW,
                                              h2t, vprev)
                    hq_done = sch.edge("v", "V")
                    yield
                sch.wait("y", "V", sch.counts["V"])
                pe_c = {}
                g_c = {}
                GS = ["G", "G2", "G3", "G4"]

                def wd_pref(i2, guard=None, pe_c=pe_c):
                    if guard is None and i2 < 4:
                        guard = DWN["last"]
                    if guard:
                        sch.wait("g", "P", guard)
                    gs = GS[i2 % 4]
                    sch.dma("g", gs, lambda e, h, i2=i2: e.dma_start(
                        out=wdTs[:, (i2 % 4) * D:(i2 % 4 + 1) * D],
                        in_=wdq_h[i2]).then_inc(h, 16))
                    return sch.counts[gs]
                for i2 in range(4):
                    g_c[i2] = wd_pref(i2)
                WV = [list(range(6 * w, min(6 * w + 6, IC))) for w in range(4)]

                def dn_tp(icb):
                    sch.dma("y", "X", lambda e, h, icb=icb: e.dma_start_transpose(
                        out=dn2[:, (icb % 12) * 128:(icb % 12 + 1) * 128],
                        in_=hqt[:, icb * 128:(icb + 1) * 128]).then_inc(h, 16))

                def issue_wave(w, guard):
                    if guard is not None:
                        sch.wait("y", "P", guard)
                    for icb in WV[w]:
                        dn_tp(icb)
                issue_wave(0, None)
                issue_wave(1, None)
                for w in range(4):
                    sch.wait("t", "X", sch.counts["X"])
                    for icb in WV[w]:
                        sch.wait("t", GS[icb % 4], g_c[icb])
                        for hh in range(2):
                            pq = pC if hh == 0 else pD
                            sch.emit("t", lambda t, icb=icb, hh=hh, pq=pq:
                                     t.matmul(
                                pq[:, 0:512],
                                dn2[:, (icb % 12) * 128:(icb % 12 + 1) * 128],
                                wdTs[:, (icb % 4) * D + hh * 512:
                                     (icb % 4) * D + (hh + 1) * 512],
                                start=(icb == 0), stop=(icb == IC - 1),
                                skip_group_check=True))
                        if icb % 2 == 1 or icb == IC - 1:
                            pe_c[icb] = sch.edge("t", "P")
                            if icb + 3 < IC:
                                g_c[icb + 3] = wd_pref(icb + 3, guard=pe_c[icb])
                            if icb + 4 < IC:
                                g_c[icb + 4] = wd_pref(icb + 4, guard=pe_c[icb])
                        if icb % 3 == 2:
                            yield
                    if w + 2 < 4:
                        issue_wave(w + 2, pe_c[WV[w][-1]])
                DWN["last"] = pe_c[IC - 1]
                c_p = pe_c[IC - 1]
                sch.wait("v", "P", c_p)

                def f_gb(v, tl=tl):
                    for hh in range(2):
                        pq = pC if hh == 0 else pD
                        v.tensor_tensor(
                            yt[:, tl * D + hh * 512: tl * D + (hh + 1) * 512],
                            pq[:, 0:512], gbt[:, hh * 512:(hh + 1) * 512],
                            OP.mult)
                    v.drain()
                sch.emit("v", f_gb)
                c_vg = sch.edge("v", "V")
                sch.wait("t", "V", c_vg)
                yield
            FFP[k] = sch.counts["P"]

            def f_out(v):
                v.tensor_tensor(xt, xt, yt[:], OP.add)
                v.drain()
            sch.emit("v", f_out)
            out_done = sch.edge("v", "V")
            sch.wait("y", "V", out_done)
            for pl in range(2):
                sch.dma("y", "O", lambda e, h, pl=pl, k=k: e.dma_start(
                    out=out_h[:, k * W + pl * 32: k * W + pl * 32 + 32, :],
                    in_=xt[:, pl * D:(pl + 1) * D]).then_inc(h, 16))

        for _ in pre_gen(0, fT[:]):
            pass
        xt0, h2t0 = slot(0)
        vcur0 = vtok[:, 0:2 * D]
        sch.emit("v", lambda v: v.tensor_copy(vcur0, h2t0))
        sch.emit("v", lambda v: v.drain())
        VRD[0] = v_ready0 = sch.edge("v", "V")
        sch.wait("t", "V", v_ready0)
        sch.wait("s", "V", v_ready0)
        pe_xbar_now(vcur0, vT3)
        for _ in pre_gen(1, fT[:]):
            pass
        sch.emit("t", lambda t: t.matmul(
            pC[:, 0:512], sel_sb[:, 0:128], sel_sb[:, 0:512],
            start=True, stop=True, skip_group_check=True))
        emit_qp0(1)

        def pump(gen, n):
            for _ in range(n):
                try:
                    next(gen)
                except StopIteration:
                    return False
            return True

        for k in range(1, NCH):
            gens = [post_gen(k - 1)]
            if k + 1 < NCH:
                gens.append(pre_gen(k + 1))
            gen = _ichain(*gens)
            st = {"s_evac": {}, "v_col": {}}
            alive = True
            for c in range(W):
                emit_scan_step(k, c, st)
                if alive:
                    alive = pump(gen, 2)
            while alive:
                alive = pump(gen, 8)
            emit_vtail(k)
            if k + 1 < NCH:
                emit_qp0(k + 1)
        for _ in post_gen(NCH - 1):
            pass

        sch.wait("y", "O")
        block = es.enter_context(nc.Block())
        sch.play(block)
    return nc


_T_FULL = 2048


def kernel(L0_attn_feature, L0_attn_wq_w, L0_attn_wq_b, L0_attn_w2_w,
           L0_attn_w2_b, L0_attn_ln_g, L0_attn_ln_b, L0_ffn_w_gate,
           L0_ffn_w_up, L0_ffn_w_down, L0_c, L0_wmod):
    x = np.ascontiguousarray(np.asarray(L0_attn_feature, dtype=np.float32))
    consts = make_constants()
    shared = {
        "wq": np.ascontiguousarray(np.asarray(L0_attn_wq_w, np.float32)),
        "wqb": np.ascontiguousarray(np.asarray(L0_attn_wq_b, np.float32)),
        "w2": np.ascontiguousarray(np.asarray(L0_attn_w2_w, np.float32)),
        "lng": np.ascontiguousarray(np.asarray(L0_attn_ln_g, np.float32)),
        "lnb": np.ascontiguousarray(np.asarray(L0_attn_ln_b, np.float32)),
        "wg": np.ascontiguousarray(np.asarray(L0_ffn_w_gate, np.float32)),
        "wu": np.ascontiguousarray(np.asarray(L0_ffn_w_up, np.float32)),
        "wd": np.ascontiguousarray(np.asarray(L0_ffn_w_down, np.float32)),
        "wmod": np.ascontiguousarray(np.asarray(L0_wmod, np.float32)),
        "sel": consts["sel"],
        "b3init": consts["b3init"],
        "ident": consts["ident"],
    }
    cv = np.ascontiguousarray(np.asarray(L0_c, np.float32))
    nc = build_program(_T_FULL)
    in_maps = []
    for core in range(N_CORES):
        m = dict(shared)
        m["x"] = np.ascontiguousarray(x[core * BPC:(core + 1) * BPC])
        m["cvec"] = np.ascontiguousarray(cv[core * BPC:(core + 1) * BPC])
        in_maps.append(m)
    res = run_bass_kernel_spmd(nc, in_maps, core_ids=list(range(N_CORES)))
    out = np.concatenate([np.asarray(r["out"]) for r in res.results], axis=0)
    return out.astype(np.float32)
